# version 03 (final)
# speedup vs baseline: 1.1256x; 1.0536x over previous
import math
import numpy as np
import concourse.bass as bass
import concourse.mybir as mybir
from concourse.bass_utils import run_bass_kernel_spmd

F32 = mybir.dt.float32
BF16 = mybir.dt.bfloat16
U16 = mybir.dt.uint16
U32 = mybir.dt.uint32
I32 = mybir.dt.int32
AF = mybir.ActivationFunctionType
ALU = mybir.AluOpType
AX = mybir.AxisListType

ENGINES = ("pe", "act", "dve", "pool", "sp")
_ENG_ATTR = {"pe": "tensor", "act": "scalar", "dve": "vector", "pool": "gpsimd", "sp": "sync"}


class Op:
    __slots__ = ("eng", "fn", "reads", "writes", "inc", "tag", "seq")

    def __init__(self, eng, fn, reads, writes, inc, tag):
        self.eng = eng
        self.fn = fn
        self.reads = reads
        self.writes = writes
        self.inc = inc
        self.tag = tag


class Prog:

    def __init__(self, nc):
        self.nc = nc
        self.ops = []

    def emit(self, eng, fn, reads=(), writes=(), inc=1, tag=""):
        self.ops.append(Op(eng, fn, tuple(reads), tuple(writes), inc, tag))

    def dma(self, fn, reads=(), writes=(), tag="", eng="sp"):
        self.emit(eng, fn, reads, writes, inc=16, tag=tag)

    NDMA = 24

    def finalize(self):
        nc = self.nc
        n = len(self.ops)
        for i, op in enumerate(self.ops):
            op.seq = i
        eng_count = {e: 0 for e in ENGINES}
        dma_slot_count = [0] * self.NDMA
        dma_idx = 0
        n_sdma = 0
        stamp = [None] * n
        self._dma_slot = [None] * n
        self._sdma_slot = [None] * n
        for op in self.ops:
            if op.inc == 16:
                if op.eng == "pool":
                    stamp[op.seq] = (f"sdma{n_sdma}", 16)
                    self._sdma_slot[op.seq] = n_sdma
                    n_sdma += 1
                else:
                    slot = dma_idx % self.NDMA
                    dma_idx += 1
                    dma_slot_count[slot] += 16
                    stamp[op.seq] = (f"dma{slot}", dma_slot_count[slot])
                    self._dma_slot[op.seq] = slot
            else:
                eng_count[op.eng] += op.inc
                stamp[op.seq] = (op.eng, eng_count[op.eng])
        self._dma_final = list(dma_slot_count)
        self._n_sdma = n_sdma

        last_writer = {}
        readers_since = {}
        deps = [[] for _ in range(n)]
        for op in self.ops:
            s = op.seq
            for k in op.reads:
                w = last_writer.get(k)
                if w is not None:
                    deps[s].append(w)
                readers_since.setdefault(k, []).append(s)
            for k in op.writes:
                w = last_writer.get(k)
                if w is not None:
                    deps[s].append(w)
                for r in readers_since.get(k, ()):
                    deps[s].append(r)
                readers_since[k] = []
                last_writer[k] = s

        all_sems = (list(ENGINES) + [f"dma{i}" for i in range(self.NDMA)]
                    + [f"sdma{i}" for i in range(self._n_sdma)])
        known = {e: {e2: 0 for e2 in all_sems} for e in ENGINES}
        waits = [[] for _ in range(n)]
        for op in self.ops:
            s = op.seq
            need = {}
            if self._dma_slot[s] is not None:
                slot = self._dma_slot[s]
                prev = stamp[s][1] - 16
                if prev > 0:
                    need[f"dma{slot}"] = prev
            for d in deps[s]:
                if d == s:
                    continue
                de, dc = stamp[d]
                if de == op.eng and op.eng == "pe":
                    continue
                need[de] = max(need.get(de, 0), dc)
            kn = known[op.eng]
            for de, dc in sorted(need.items()):
                if kn[de] >= dc:
                    continue
                waits[s].append((de, dc))
                kn[de] = dc
        self._waits = waits
        self._stamp = stamp

        sems = {}
        from contextlib import ExitStack
        with ExitStack() as es:
            for e in all_sems:
                sems[e] = es.enter_context(nc.semaphore(f"sem_{e}"))
            with nc.Block() as block:
                by_eng = {e: [op for op in self.ops if op.eng == e] for e in ENGINES}

                def emit_for(eng_obj, eng_name):
                    for op in by_eng[eng_name]:
                        for (de, dc) in self._waits[op.seq]:
                            eng_obj.wait_ge(sems[de], dc)
                        ins = op.fn(eng_obj)
                        if ins is None:
                            raise RuntimeError(f"op {op.tag} returned None")
                        if op.inc == 16:
                            if self._sdma_slot[op.seq] is not None:
                                ins.then_inc(sems[f"sdma{self._sdma_slot[op.seq]}"], 16)
                            else:
                                ins.then_inc(sems[f"dma{self._dma_slot[op.seq]}"], 16)
                        else:
                            ins.then_inc(sems[eng_name], op.inc)
                    if eng_name == "sp":
                        for i in range(self.NDMA):
                            if self._dma_final[i] > 0:
                                eng_obj.wait_ge(sems[f"dma{i}"], self._dma_final[i])
                        for i in range(self._n_sdma):
                            eng_obj.wait_ge(sems[f"sdma{i}"], 16)

                @block.tensor
                def _(t):
                    emit_for(t, "pe")

                @block.scalar
                def _(s):
                    emit_for(s, "act")

                @block.vector
                def _(v):
                    emit_for(v, "dve")

                @block.gpsimd
                def _(g):
                    emit_for(g, "pool")

                @block.sync
                def _(sp):
                    emit_for(sp, "sp")
        return sems


def ceil_div(a, b):
    return (a + b - 1) // b



class Ctx:

    def __init__(self, nc, prog, es):
        self.nc = nc
        self.P = prog
        self.es = es
        self.sb = {}
        self.ps = {}

    def sbuf(self, name, shape, dtype=F32):
        t = self.es.enter_context(self.nc.sbuf_tensor(name, shape, dtype))
        self.sb[name] = t
        return t

    def psum(self, name, shape, dtype=F32):
        t = self.es.enter_context(self.nc.psum_tensor(name, shape, dtype))
        self.ps[name] = t
        return t

    def identities(self):
        self.id_bf = self.sbuf("ident_bf", [128, 128], BF16)
        self.id_f32 = self.sbuf("ident_f32", [128, 128], F32)
        P = self.P
        for t, key in ((self.id_bf, "id_bf"), (self.id_f32, "id_f32")):
            ap = t[:]
            P.emit("pool", lambda g, ap=ap: g.memset(ap, 0.0), (), (key,), tag="ident0")
            P.emit("pool", lambda g, ap=ap: g.affine_select(
                out=ap, in_=ap, compare_op=ALU.not_equal, fill=1.0, base=0,
                pattern=[[-1, 128]], channel_multiplier=1), (key,), (key,), tag="ident1")

    def ident_for(self, dtype):
        return self.id_bf if dtype == BF16 else self.id_f32


def emit_transpose(ctx, out_psum_ap, in_sb_ap, rkeys, wkeys, dtype):
    ident = ctx.ident_for(dtype)
    ikey = "id_bf" if dtype == BF16 else "id_f32"
    np_ = in_sb_ap.shape[0]
    ctx.P.emit("pe", lambda t: t.transpose(out_psum_ap, in_sb_ap,
                                           ident[0:np_, 0:np_]),
               tuple(rkeys) + (ikey,), tuple(wkeys), tag="transpose")


def emit_quant_grid(ctx, x_ap, out_ap, scr, scr2, rkeys, wkeys, skey, s2key):
    P = ctx.P
    W = x_ap.shape[1]
    nb = W // 32
    sA = scr[:, :nb]
    sR = scr[:, nb:2*nb]
    P.emit("dve", lambda v: v.tensor_reduce(
        out=sA, in_=x_ap.rearrange("p (b w) -> p b w", w=32), axis=AX.X,
        op=ALU.max, apply_absolute_value=True), rkeys, (skey,), tag="q_amax")
    P.emit("dve", lambda v: v.tensor_scalar(
        out=sA, in0=sA, scalar1=1.0/7.5, scalar2=1e-30, op0=ALU.mult, op1=ALU.max),
        (skey,), (skey,), tag="q_s")
    P.emit("dve", lambda v: v.reciprocal(out=sR, in_=sA), (skey,), (skey,), tag="q_recip")
    rb = sR.rearrange("p (b o) -> p b o", o=1).to_broadcast([128, nb, 32])
    xv = x_ap.rearrange("p (b w) -> p b w", w=32)
    yv = scr2[:, :W].rearrange("p (b w) -> p b w", w=32)
    P.emit("dve", lambda v: v.tensor_tensor(out=yv, in0=xv, in1=rb, op=ALU.mult),
           rkeys + (skey,), (s2key,), tag="q_y")
    y = scr2[:, :W]
    ay = scr[:, 2*nb:2*nb+W] if False else None
    kf = scr[:, nb*2:nb*2+W]
    P.emit("dve", lambda g: g.scalar_tensor_tensor(
        out=kf, in0=y, scalar=-1.0, in1=y, op0=ALU.mult, op1=ALU.max),
        (s2key,), (skey + "_kf",), tag="q_abs")
    P.emit("dve", lambda g: g.tensor_scalar(
        out=scr[:, nb*2+W:nb*2+2*W], in0=kf, scalar1=4.0, scalar2=-2.0,
        op0=ALU.is_ge, op1=ALU.mult), (skey + "_kf",), (skey + "_k2",), tag="q_s2")
    P.emit("dve", lambda g: g.tensor_scalar(
        out=kf, in0=kf, scalar1=2.0, scalar2=-4.0, op0=ALU.is_ge, op1=ALU.mult),
        (skey + "_kf",), (skey + "_kf",), tag="q_s1")
    k2 = scr[:, nb*2+W:nb*2+2*W]
    P.emit("dve", lambda g: g.tensor_scalar(
        out=kf, in0=kf, scalar1=8.0, scalar2=None, op0=ALU.add),
        (skey + "_kf",), (skey + "_kf",), tag="q_k8")
    P.emit("pool", lambda g: g.tensor_tensor(out=kf, in0=kf, in1=k2, op=ALU.add),
           (skey + "_kf", skey + "_k2"), (skey + "_kf",), tag="q_k")
    P.emit("dve", lambda v: v.tensor_tensor(out=k2, in0=y, in1=kf, op=ALU.mult),
           (s2key, skey + "_kf"), (skey + "_k2",), tag="q_v")
    MAGIC = 12582912.0
    P.emit("dve", lambda v: v.tensor_scalar(
        out=y, in0=k2, scalar1=-0.5, scalar2=MAGIC, op0=ALU.add, op1=ALU.add),
        (skey + "_k2",), (s2key,), tag="q_f1")
    P.emit("dve", lambda v: v.tensor_scalar(
        out=y, in0=y, scalar1=MAGIC, scalar2=None, op0=ALU.subtract),
        (s2key,), (s2key,), tag="q_f2")
    P.emit("dve", lambda v: v.tensor_tensor(out=k2, in0=k2, in1=y, op=ALU.subtract),
           (skey + "_k2", s2key), (skey + "_k2",), tag="q_w1")
    P.emit("dve", lambda v: v.tensor_scalar(
        out=k2, in0=k2, scalar1=0.5, scalar2=None, op0=ALU.is_gt),
        (skey + "_k2",), (skey + "_k2",), tag="q_w2")
    P.emit("dve", lambda v: v.tensor_tensor(out=y, in0=y, in1=k2, op=ALU.add),
           (s2key, skey + "_k2"), (s2key,), tag="q_q")
    P.emit("dve", lambda v: v.reciprocal(out=kf, in_=kf), (skey + "_kf",), (skey + "_kf",), tag="q_invk")
    P.emit("dve", lambda v: v.tensor_tensor(out=y, in0=y, in1=kf, op=ALU.mult),
           (s2key, skey + "_kf"), (s2key,), tag="q_qs")
    sb2 = sA.rearrange("p (b o) -> p b o", o=1).to_broadcast([128, nb, 32])
    ov = out_ap.rearrange("p (b w) -> p b w", w=32)
    P.emit("dve", lambda v: v.tensor_tensor(
        out=ov, in0=y.rearrange("p (b w) -> p b w", w=32), in1=sb2, op=ALU.mult),
        (s2key, skey), tuple(wkeys), tag="q_deq")


def emit_ln_tile(ctx, x_tm_ap, ln_out_ap, sq_scr, stat_scr, rkeys, wkeys, tkey):
    P = ctx.P
    D = x_tm_ap.shape[1]
    mu = stat_scr[:, 0:1]
    ssq = stat_scr[:, 1:2]
    t1 = stat_scr[:, 2:3]
    rstd = stat_scr[:, 3:4]
    P.emit("dve", lambda v: v.reduce_sum(out=mu, in_=x_tm_ap, axis=AX.X),
           rkeys, (tkey + "_mu",), tag="ln_sum")
    P.emit("dve", lambda v: v.tensor_scalar(
        out=mu, in0=mu, scalar1=1.0/D, scalar2=None, op0=ALU.mult),
        (tkey + "_mu",), (tkey + "_mu",), tag="ln_mu")
    P.emit("act", lambda s: s.activation(out=sq_scr, in_=x_tm_ap, func=AF.Square,
                                         accum_out=ssq),
           rkeys, (tkey + "_sq", tkey + "_ssq"), tag="ln_sq")
    P.emit("dve", lambda v: v.tensor_tensor(out=t1, in0=mu, in1=mu, op=ALU.mult),
           (tkey + "_mu",), (tkey + "_t1",), tag="ln_mu2")
    P.emit("dve", lambda v: v.scalar_tensor_tensor(
        out=t1, in0=ssq, scalar=1.0/D, in1=t1, op0=ALU.mult, op1=ALU.subtract),
        (tkey + "_ssq", tkey + "_t1"), (tkey + "_t1",), tag="ln_var")
    P.emit("dve", lambda v: v.tensor_scalar(
        out=t1, in0=t1, scalar1=1e-5, scalar2=None, op0=ALU.add),
        (tkey + "_t1",), (tkey + "_t1",), tag="ln_vare")
    P.emit("act", lambda s: s.activation(out=rstd, in_=t1, func=AF.Sqrt),
           (tkey + "_t1",), (tkey + "_rstd",), tag="ln_sqrt")
    P.emit("dve", lambda v: v.reciprocal(out=rstd, in_=rstd),
           (tkey + "_rstd",), (tkey + "_rstd",), tag="ln_recip")
    P.emit("dve", lambda v: v.tensor_scalar(
        out=ln_out_ap, in0=x_tm_ap, scalar1=mu, scalar2=rstd,
        op0=ALU.subtract, op1=ALU.mult),
        rkeys + (tkey + "_mu", tkey + "_rstd"), tuple(wkeys), tag="ln_apply")



def build_l3_ffn():
    from contextlib import ExitStack
    nc = bass.Bass("TRN2", target_bir_lowering=False, debug=False)
    T = 2048
    NT = T // 128
    x_d = nc.dram_tensor("x", [1024, T], F32, kind="ExternalInput").ap()
    w_d = nc.dram_tensor("w", [1024, 1024], F32, kind="ExternalInput").ap()
    b_d = nc.dram_tensor("bias", [128, 8], F32, kind="ExternalInput").ap()
    y_d = nc.dram_tensor("y", [1024, T], F32, kind="ExternalOutput").ap()

    es = ExitStack()
    P = Prog(nc)
    ctx = Ctx(nc, P, es)
    with es:
        x_fm = ctx.sbuf("x_fm", [128, 8 * T], F32)
        w_sb = ctx.sbuf("w_sb", [128, 8 * 1024], F32)
        bias = ctx.sbuf("bias_sb", [128, 8], F32)
        wd = ctx.sbuf("wd", [128, 8 * 1024], BF16)
        wdT = ctx.sbuf("wdT", [128, 8 * 1024], BF16)
        xd_fm = ctx.sbuf("xd_fm", [128, 8 * T], BF16)
        xtm = [ctx.sbuf(f"xtm{i}", [128, 1024], F32) for i in range(2)]
        lntm = [ctx.sbuf(f"lntm{i}", [128, 1024], F32) for i in range(2)]
        xdtm = [ctx.sbuf(f"xdtm{i}", [128, 1024], BF16) for i in range(2)]
        _q = ctx.sbuf("qscr", [128, 2176], F32)
        qscr = [_q, _q]
        _q2 = ctx.sbuf("qscr2", [128, 1024], F32)
        qscr2 = [_q2, _q2]
        _sq = ctx.sbuf("sqs", [128, 1024], F32)
        sqs = [_sq, _sq]
        stat = [ctx.sbuf(f"stat{i}", [128, 8], F32) for i in range(2)]
        ctx.identities()
        pt_f = [ctx.psum(f"pt_f{i}", [128, 512], F32) for i in range(2)]
        pt_b = [ctx.psum(f"pt_b{i}", [128, 1024], BF16) for i in range(2)]
        pmm = [ctx.psum(f"pmm{i}", [128, 512], F32) for i in range(2)]

        P.dma(lambda sp: sp.dma_start(
            out=x_fm[:].rearrange("p (c t) -> p c t", c=8),
            in_=x_d.rearrange("(c p) t -> p c t", p=128)), (), ("x_fm",), tag="ld_x")
        P.dma(lambda sp: sp.dma_start(
            out=w_sb[:].rearrange("p (c d) -> p c d", c=8),
            in_=w_d.rearrange("(c p) d -> p c d", p=128)), (), ("w_sb",), tag="ld_w")
        P.dma(lambda sp: sp.dma_start(out=bias[:], in_=b_d), (), ("bias",), tag="ld_b")

        for oc in range(8):
            p = oc % 2
            emit_quant_grid(ctx, w_sb[:, oc*1024:(oc+1)*1024], wd[:, oc*1024:(oc+1)*1024],
                            qscr[p][:], qscr2[p][:], ("w_sb",), (f"wd_{oc}",),
                            "qw", "qw2")
        for dc in range(8):
            pp = dc % 2
            for oc in range(8):
                emit_transpose(ctx, pt_b[pp][:, oc*128:(oc+1)*128],
                               wd[:, oc*1024 + dc*128: oc*1024 + (dc+1)*128],
                               (f"wd_{oc}",), (f"ptb{pp}",), BF16)
            P.emit("dve", lambda v, pp=pp, dc=dc: v.tensor_copy(
                wdT[:, dc*1024:(dc+1)*1024], pt_b[pp][:]),
                (f"ptb{pp}",), (f"wdT_{dc}",), tag="evac_wdT")

        for t in range(NT):
            p = t % 2
            for half in range(2):
                pf = half
                for dc4 in range(4):
                    dc = half * 4 + dc4
                    emit_transpose(ctx, pt_f[pf][:, dc4*128:(dc4+1)*128],
                                   x_fm[:, dc*T + t*128: dc*T + (t+1)*128],
                                   ("x_fm",), (f"ptf{pf}",), F32)
                P.emit("act", lambda s, pf=pf, half=half, p=p: s.copy(
                    xtm[p][:, half*512:(half+1)*512], pt_f[pf][:]),
                    (f"ptf{pf}",), (f"xtm{p}",), tag="evac_xtm")
            emit_ln_tile(ctx, xtm[p][:], lntm[p][:], sqs[p][:], stat[p][:],
                         (f"xtm{p}",), (f"lntm{p}",), f"ln{p}")
            emit_quant_grid(ctx, lntm[p][:], xdtm[p][:], qscr[p][:], qscr2[p][:],
                            (f"lntm{p}",), (f"xdtm{p}",), "qw", "qw2")
            pb = t % 2
            for dc in range(8):
                emit_transpose(ctx, pt_b[pb][:, dc*128:(dc+1)*128],
                               xdtm[p][:, dc*128:(dc+1)*128],
                               (f"xdtm{p}",), (f"ptb{pb}",), BF16)
            P.emit("dve", lambda v, pb=pb, t=t: v.tensor_copy(
                xd_fm[:].rearrange("q (c tt) -> q c tt", c=8)[:, :, t*128:(t+1)*128],
                pt_b[pb][:].rearrange("q (c tt) -> q c tt", c=8)),
                (f"ptb{pb}",), ("xd_fm",), tag="evac_xdfm")

        for oc in range(8):
            for n in range(T // 512):
                pm = (oc * (T//512) + n) % 2
                for dc in range(8):
                    P.emit("pe", lambda t_, oc=oc, n=n, dc=dc, pm=pm: t_.matmul(
                        pmm[pm][:],
                        wdT[:, dc*1024 + oc*128:(dc)*1024 + (oc+1)*128],
                        xd_fm[:, dc*T + n*512: dc*T + (n+1)*512],
                        start=(dc == 0), stop=(dc == 7)),
                        (f"wdT_{dc}", "xd_fm"), (f"pmm{pm}",), tag="mm")
                P.emit("dve", lambda v, oc=oc, n=n, pm=pm: v.scalar_tensor_tensor(
                    out=x_fm[:, oc*T + n*512: oc*T + (n+1)*512],
                    in0=pmm[pm][:], scalar=bias[:, oc:oc+1],
                    in1=x_fm[:, oc*T + n*512: oc*T + (n+1)*512],
                    op0=ALU.add, op1=ALU.add),
                    (f"pmm{pm}", "bias", "x_fm"), ("x_fm", f"pmm{pm}_free"), tag="resid")

        P.dma(lambda sp: sp.dma_start(
            out=y_d.rearrange("(c p) t -> p c t", p=128),
            in_=x_fm[:].rearrange("p (c t) -> p c t", c=8)), ("x_fm",), ("y",), tag="st_y")

        P.finalize()
    return nc





def emit_ln_fm_stream(ctx, x_dram, ln_fm, N, statT, sq, xring, ps, lnkey, uid):
    P = ctx.P
    ones1 = ctx.sb["ones1"]
    NCH = N // 512
    K_mu = uid + "_mu"; K_rs = uid + "_rs"; K_sq = uid + "_sq"
    tmp = statT[0:1, 2*N:2*N+512]

    def ld(ch, r):
        P.dma(lambda sp, ch=ch, r=r: sp.dma_start(
            out=xring[r][:].rearrange("p (c t) -> p c t", c=8),
            in_=x_dram[:, ch*512:(ch+1)*512].rearrange("(c p) t -> p c t", p=128)),
            (uid + f"_xr{r}_f",), (uid + f"_xr{r}",), tag="lnf_ldx")

    for ch in range(NCH):
        r = ch % 2
        ld(ch, r)
        xr = xring[r]
        for dc in range(8):
            P.emit("pe", lambda t, dc=dc, xr=xr, r=r: t.matmul(
                ps["sum"][0:1, :], ones1[:], xr[:, dc*512:(dc+1)*512],
                start=(dc == 0), stop=(dc == 7)),
                (uid + f"_xr{r}", "ones1"), (uid + "_pssum",), tag="lnf_sum")
        for dc in range(8):
            P.emit("act", lambda s, dc=dc, xr=xr, r=r: s.activation(
                out=sq[:], in_=xr[:, dc*512:(dc+1)*512], func=AF.Square),
                (uid + f"_xr{r}",), (K_sq,), tag="lnf_sqx")
            P.emit("pe", lambda t, dc=dc: t.matmul(
                ps["sumsq"][0:1, :], ones1[:], sq[:],
                start=(dc == 0), stop=(dc == 7)),
                (K_sq, "ones1"), (uid + "_pssq",), tag="lnf_sumsq")
        P.emit("dve", lambda v: v.engine_nop(), (uid + f"_xr{r}",), (uid + f"_xr{r}_f",),
               tag="lnf_xfree")
        mu = statT[0:1, ch*512:(ch+1)*512]
        rs = statT[0:1, N + ch*512: N + (ch+1)*512]
        P.emit("dve", lambda v, mu=mu: v.tensor_scalar(
            out=mu, in0=ps["sum"][0:1, :], scalar1=1.0/1024, scalar2=None,
            op0=ALU.mult), (uid + "_pssum",), (K_mu,), tag="lnf_mu")
        P.emit("dve", lambda v, rs=rs: v.tensor_scalar(
            out=rs, in0=ps["sumsq"][0:1, :], scalar1=1.0/1024, scalar2=None,
            op0=ALU.mult), (uid + "_pssq",), (K_rs,), tag="lnf_ms")
        P.emit("dve", lambda v, mu=mu: v.tensor_tensor(
            out=tmp, in0=mu, in1=mu, op=ALU.mult),
            (K_mu,), (uid + "_tmp",), tag="lnf_mu2")
        P.emit("dve", lambda v, rs=rs: v.tensor_tensor(
            out=rs, in0=rs, in1=tmp, op=ALU.subtract),
            (K_rs, uid + "_tmp"), (K_rs,), tag="lnf_var")
        P.emit("dve", lambda v, rs=rs: v.tensor_scalar(
            out=rs, in0=rs, scalar1=1e-5, scalar2=None, op0=ALU.add),
            (K_rs,), (K_rs,), tag="lnf_eps")
        P.emit("act", lambda s, rs=rs: s.activation(out=rs, in_=rs, func=AF.Sqrt),
               (K_rs,), (K_rs,), tag="lnf_sqrt")
        P.emit("dve", lambda v, rs=rs: v.reciprocal(out=rs, in_=rs),
               (K_rs,), (K_rs,), tag="lnf_recip")
    for ch in range(NCH):
        r = ch % 2
        ld(ch, r)
        xr = xring[r]
        P.emit("pe", lambda t, ch=ch: t.matmul(
            ps["bc_mu"][:], ctx.sb["onesr"][0:1, :], statT[0:1, ch*512:(ch+1)*512],
            start=True, stop=True),
            (K_mu, "onesr"), (uid + "_bcmu",), tag="lnf_bcmu")
        P.emit("pe", lambda t, ch=ch: t.matmul(
            ps["bc_rs"][:], ctx.sb["onesr"][0:1, :], statT[0:1, N + ch*512: N + (ch+1)*512],
            start=True, stop=True),
            (K_rs, "onesr"), (uid + "_bcrs",), tag="lnf_bcrs")
        for dc in range(8):
            P.emit("dve", lambda v, dc=dc, ch=ch, xr=xr, r=r: v.tensor_tensor(
                out=sq[:], in0=xr[:, dc*512:(dc+1)*512],
                in1=ps["bc_mu"][:], op=ALU.subtract),
                (uid + f"_xr{r}", uid + "_bcmu"), (K_sq,), tag="lnf_sub")
            P.emit("dve", lambda v, dc=dc, ch=ch: v.tensor_tensor(
                out=ln_fm[:, dc*N + ch*512: dc*N + (ch+1)*512], in0=sq[:],
                in1=ps["bc_rs"][:], op=ALU.mult),
                (K_sq, uid + "_bcrs"), (lnkey,), tag="lnf_mul")
        P.emit("dve", lambda v: v.engine_nop(), (uid + f"_xr{r}",), (uid + f"_xr{r}_f",),
               tag="lnf_xfree2")


ALIBI_SLOPES16 = None
def alibi_slopes16():
    global ALIBI_SLOPES16
    if ALIBI_SLOPES16 is None:
        start = 2 ** (-2 ** (-(math.log2(16) - 3)))
        ALIBI_SLOPES16 = [start ** i for i in range(16)]
    return ALIBI_SLOPES16



def build_l2_proj():
    from contextlib import ExitStack
    nc = bass.Bass("TRN2", target_bir_lowering=False, debug=False)
    N = 4096
    x_d = nc.dram_tensor("x", [1024, N], F32, kind="ExternalInput").ap()
    w_d = nc.dram_tensor("wqkv", [1024, 3072], F32, kind="ExternalInput").ap()
    b_d = nc.dram_tensor("bqkv", [128, 24], F32, kind="ExternalInput").ap()
    q_d = nc.dram_tensor("q", [1024, N], BF16, kind="ExternalOutput").ap()
    k_d = nc.dram_tensor("k", [1024, N], BF16, kind="ExternalOutput").ap()
    v_d = nc.dram_tensor("v", [N, 1024], BF16, kind="ExternalOutput").ap()

    es = ExitStack()
    P = Prog(nc)
    ctx = Ctx(nc, P, es)
    with es:
        xring = [ctx.sbuf(f"xring{i}", [128, 8 * 512], F32) for i in range(2)]
        ln_fm = ctx.sbuf("ln_fm", [128, 8 * N], BF16)
        statT = ctx.sbuf("statT", [1, 2 * N + 512], F32)
        sq = ctx.sbuf("sq", [128, 512], F32)
        ones1 = ctx.sbuf("ones1", [128, 1], F32)
        onesr = ctx.sbuf("onesr", [1, 128], F32)
        wc = [ctx.sbuf(f"wc{i}", [128, 8 * 512], F32) for i in range(1)]
        wcb = [ctx.sbuf(f"wcb{i}", [128, 8 * 512], BF16) for i in range(2)]
        bq = ctx.sbuf("bq", [128, 24], F32)
        vb_row = ctx.sbuf("vb_row", [1, 1024], F32)
        stg = [ctx.sbuf(f"stg{i}", [128, 512], BF16) for i in range(2)]
        ctx.identities()
        ps = {k: ctx.psum(f"ps_{k}", [128, 512], F32)
              for k in ("sum", "sumsq", "bc_mu", "bc_rs")}
        pmm = [ctx.psum(f"pmm{i}", [128, 512], F32) for i in range(2)]
        ps_vb = ctx.psum("ps_vb", [128, 512], F32)

        P.emit("pool", lambda g: g.memset(ones1[:], 1.0), (), ("ones1",), tag="ones")
        P.emit("pool", lambda g: g.memset(onesr[:], 1.0), (), ("onesr",), tag="ones")
        P.dma(lambda sp: sp.dma_start(out=bq[:], in_=b_d), (), ("bq",), tag="ld_b")

        emit_ln_fm_stream(ctx, x_d, ln_fm, N, statT, sq[:], xring, ps, "ln_fm", "ln")

        vbrow_d = nc.dram_tensor("vbrow", [1, 1024], F32, kind="ExternalInput").ap()
        P.dma(lambda sp: sp.dma_start(out=vb_row[:], in_=vbrow_d), (), ("vbrow",), tag="ld_vb")
        P.emit("pe", lambda t: t.matmul(ps_vb[:], onesr[0:1, :], vb_row[0:1, 0:512],
                                        start=True, stop=True),
               ("vbrow", "onesr"), ("ps_vb",), tag="vb_bc")
        vb1 = ctx.sbuf("vb1", [128, 1024], F32)
        P.emit("act", lambda s: s.copy(vb1[:, 0:512], ps_vb[:]), ("ps_vb",), ("vb1",), tag="vb_cp")
        P.emit("pe", lambda t: t.matmul(ps_vb[:], onesr[0:1, :], vb_row[0:1, 512:1024],
                                        start=True, stop=True),
               ("vbrow", "onesr", "vb1"), ("ps_vb",), tag="vb_bc2")
        P.emit("act", lambda s: s.copy(vb1[:, 512:1024], ps_vb[:]), ("ps_vb",), ("vb1",), tag="vb_cp2")

        NCH = N // 512
        for cg in range(6):
            wp = 0
            cb = cg % 2
            P.dma(lambda sp, cg=cg: sp.dma_start(
                out=wc[0][:].rearrange("p (c d) -> p c d", c=8),
                in_=w_d[:, cg*512:(cg+1)*512].rearrange("(c p) d -> p c d", p=128)),
                ("wc_f",), ("wc",), tag="ld_w")
            P.emit("dve", lambda v, cb=cb: v.tensor_copy(wcb[cb][:], wc[0][:]),
                   ("wc",), (f"wcb{cb}", "wc_f"), tag="cast_w")
            if cg < 4:
                scale = 0.125 if cg < 2 else 1.0
                dst = q_d if cg < 2 else k_d
                dkey = "q_out" if cg < 2 else "k_out"
                for oc in range(4):
                    gchunk = cg * 4 + oc
                    for tc in range(NCH):
                        pm = (oc * NCH + tc) % 2
                        sg = (oc * NCH + tc) % 2
                        for dc in range(8):
                            P.emit("pe", lambda t, oc=oc, tc=tc, dc=dc, pm=pm, cb=cb: t.matmul(
                                pmm[pm][:], wcb[cb][:, dc*512 + oc*128: dc*512 + (oc+1)*128],
                                ln_fm[:, dc*N + tc*512: dc*N + (tc+1)*512],
                                start=(dc == 0), stop=(dc == 7)),
                                (f"wcb{cb}", "ln_fm"), (f"pmm{pm}",), tag="qk_mm")
                        P.emit("dve", lambda v, pm=pm, sg=sg, gchunk=gchunk, scale=scale: v.tensor_scalar(
                            out=stg[sg][:], in0=pmm[pm][:], scalar1=bq[:, gchunk:gchunk+1],
                            scalar2=scale, op0=ALU.add, op1=ALU.mult),
                            (f"pmm{pm}", "bq"), (f"stg{sg}",), tag="qk_evac")
                        P.dma(lambda sp, sg=sg, gchunk=gchunk, tc=tc, dst=dst: sp.dma_start(
                            out=dst[gchunk % 8 * 128: (gchunk % 8 + 1) * 128, tc*512:(tc+1)*512],
                            in_=stg[sg][:]), (f"stg{sg}",), (dkey,), tag="qk_st")
            else:
                for jt in range(32):
                    pm = jt % 2
                    sg = jt % 2
                    for dc in range(8):
                        P.emit("pe", lambda t, jt=jt, dc=dc, pm=pm, cb=cb: t.matmul(
                            pmm[pm][:], ln_fm[:, dc*N + jt*128: dc*N + (jt+1)*128],
                            wcb[cb][:, dc*512:(dc+1)*512],
                            start=(dc == 0), stop=(dc == 7)),
                            (f"wcb{cb}", "ln_fm"), (f"pmm{pm}",), tag="v_mm")
                    P.emit("dve", lambda v, pm=pm, sg=sg, cg=cg: v.tensor_tensor(
                        out=stg[sg][:], in0=pmm[pm][:],
                        in1=vb1[:, (cg-4)*512:(cg-3)*512], op=ALU.add),
                        (f"pmm{pm}", "vb1"), (f"stg{sg}",), tag="v_evac")
                    P.dma(lambda sp, sg=sg, jt=jt, cg=cg: sp.dma_start(
                        out=v_d[jt*128:(jt+1)*128, (cg-4)*512:(cg-3)*512],
                        in_=stg[sg][:]), (f"stg{sg}",), ("v_out",), tag="v_st")
        P.finalize()
    return nc



L2_C = 8.0

def build_l2_attn():
    from contextlib import ExitStack
    nc = bass.Bass("TRN2", target_bir_lowering=False, debug=False)
    N = 4096
    NJT = 32
    VW = 65 * 16
    q_d = nc.dram_tensor("q", [1024, 2048], BF16, kind="ExternalInput").ap()
    k_d = nc.dram_tensor("k", [1024, N], BF16, kind="ExternalInput").ap()
    v_d = nc.dram_tensor("v", [N, 1024], BF16, kind="ExternalInput").ap()
    wo_d = nc.dram_tensor("wout", [1024, 1024], F32, kind="ExternalInput").ap()
    bo_d = nc.dram_tensor("bout", [128, 8], F32, kind="ExternalInput").ap()
    xo_d = nc.dram_tensor("xo", [1024, 2048], F32, kind="ExternalInput").ap()
    as_d = nc.dram_tensor("astrip", [128, 1], F32, kind="ExternalInput").ap()
    y_d = nc.dram_tensor("y", [1024, 2048], F32, kind="ExternalOutput").ap()

    slopes = alibi_slopes16()
    es = ExitStack()
    P = Prog(nc)
    ctx = Ctx(nc, P, es)
    with es:
        k_fm = ctx.sbuf("k_fm", [128, 8 * N], BF16)
        v_tm = ctx.sbuf("v_tm", [128, NJT * VW], BF16)
        strip = ctx.sbuf("strip", [128, 7552], BF16)
        qring = ctx.sbuf("qring", [128, 8 * 512], BF16)
        a_sb = ctx.sbuf("a_sb", [128, 1], F32)
        sprim = [ctx.sbuf(f"sprim{i}", [128, 512], F32) for i in range(2)]
        ub = [ctx.sbuf(f"ub{i}", [128, 512], BF16) for i in range(2)]
        o_fm = ctx.sbuf("o_fm", [128, 8 * 512], BF16)
        rden = ctx.sbuf("rden", [1, 512], F32)
        ones64 = ctx.sbuf("ones64", [1, 64], F32)
        bcden = ctx.sbuf("bcden", [128, 512], F32)
        wob = ctx.sbuf("wob", [128, 8 * 1024], BF16)
        wof = ctx.sbuf("wof", [128, 1024], F32)
        bo_sb = ctx.sbuf("bo_sb", [128, 8], F32)
        xr = [ctx.sbuf(f"xr{i}", [128, 512], F32) for i in range(2)]
        ystg = [ctx.sbuf(f"ystg{i}", [128, 512], F32) for i in range(2)]
        ctx.identities()
        ps_sc = [ctx.psum(f"ps_sc{i}", [128, 512], F32) for i in range(2)]
        ps_av = [ctx.psum(f"ps_av{i}", [128, 512], F32) for i in range(2)]
        ps_bc = ctx.psum("ps_bc", [128, 512], F32)
        ps_o = [ctx.psum(f"ps_o{i}", [128, 512], F32) for i in range(2)]

        cb8 = ctx.sbuf("cb8", [128, 1], F32)
        P.emit("pool", lambda g: g.memset(cb8[:], -L2_C), (), ("cb8",), tag="cb8")
        P.emit("pool", lambda g: g.memset(ones64[:], 1.0), (), ("ones64",), tag="ones")
        P.dma(lambda sp: sp.dma_start(out=a_sb[:], in_=as_d), (), ("a_sb",), tag="ld_a")
        scr = k_fm[:].bitcast(F32)[:, 0:7552]
        P.emit("pool", lambda g: g.iota(scr, pattern=[[-1, 7552]], base=0,
                                        channel_multiplier=1,
                                        allow_small_or_imprecise_dtypes=True),
               (), ("scrK",), tag="strip_iota")
        P.emit("dve", lambda v: v.tensor_scalar(
            out=scr, in0=scr, scalar1=a_sb[:], scalar2=None, op0=ALU.add),
            ("scrK", "a_sb"), ("scrK",), tag="strip_add")
        P.emit("act", lambda s: s.activation(out=strip[:], in_=scr, func=AF.Abs),
               ("scrK",), ("strip",), tag="strip_abs")
        P.dma(lambda sp: sp.dma_start(
            out=k_fm[:].rearrange("p (c t) -> p c t", c=8),
            in_=k_d.rearrange("(c p) t -> p c t", p=128)), ("strip",), ("k_fm",), tag="ld_k")
        for hh in range(16):
            P.dma(lambda sp, hh=hh: sp.dma_start(
                out=v_tm[:].rearrange("p (j h e) -> p j h e", h=16, e=65)[:, :, hh, 0:64],
                in_=v_d.rearrange("(j p) (h e) -> p j h e", p=128, e=64)[:, :, hh]),
                (), ("v_tm",), tag="ld_v")
        P.emit("dve", lambda v: v.memset(
            v_tm[:].rearrange("p (j h e) -> p j h e", h=16, e=65)[:, :, :, 64:65], 1.0),
            (), ("v_ones",), tag="v_ones")
        P.dma(lambda sp: sp.dma_start(
            out=wof[:].rearrange("p (c d) -> p c d", c=1),
            in_=wo_d[0:128, :].rearrange("(c p) d -> p c d", p=128)), (), ("wof",), tag="ld_wo0")
        P.emit("dve", lambda v: v.tensor_copy(wob[:, 0:1024], wof[:]),
               ("wof",), ("wob",), tag="cast_wo0")
        for kc in range(1, 8):
            P.dma(lambda sp, kc=kc: sp.dma_start(
                out=wof[:].rearrange("p (c d) -> p c d", c=1),
                in_=wo_d[kc*128:(kc+1)*128, :].rearrange("(c p) d -> p c d", p=128)),
                ("wob",), ("wof",), tag="ld_wo")
            P.emit("dve", lambda v, kc=kc: v.tensor_copy(wob[:, kc*1024:(kc+1)*1024], wof[:]),
                   ("wof",), ("wob",), tag="cast_wo")
        P.dma(lambda sp: sp.dma_start(out=bo_sb[:], in_=bo_d), (), ("bo_sb",), tag="ld_bo")

        def near(m, jt, h):
            ilo, ihi = 1024*m, 1024*m + 1023
            jlo, jhi = 128*jt, 128*jt + 127
            dist = max(0, jlo - ihi, ilo - jhi)
            return slopes[h] * dist < 160.0

        for m in range(4):
            P.dma(lambda sp, m=m: sp.dma_start(
                out=qring[:].rearrange("p (c t) -> p c t", c=8),
                in_=q_d[:, m*512:(m+1)*512].rearrange("(c p) t -> p c t", p=128)),
                ("qk_done",), ("qring",), tag="ld_q")
            for h in range(16):
                pav = h % 2
                c = h // 2
                ro = (h % 2) * 64
                njts = [jt for jt in range(NJT) if near(m, jt, h)]
                for idx, jt in enumerate(njts):
                    psc = idx % 2
                    P.emit("pe", lambda t, jt=jt, c=c, ro=ro, psc=psc: t.matmul(
                        ps_sc[psc][:],
                        k_fm[ro:ro+64, c*N + jt*128: c*N + (jt+1)*128],
                        qring[ro:ro+64, c*512:(c+1)*512],
                        start=True, stop=True),
                        ("k_fm", "qring"), (f"ps_sc{psc}",), tag="sc_mm")
                    u0 = 1024*m - 128*jt + 3968
                    P.emit("dve", lambda v, psc=psc, u0=u0, h=h: v.scalar_tensor_tensor(
                        out=sprim[psc][:], in0=strip[:, u0:u0+512], scalar=-slopes[h],
                        in1=ps_sc[psc][:], op0=ALU.mult, op1=ALU.add),
                        ("strip", f"ps_sc{psc}"), (f"sprim{psc}",), tag="sc_bias")
                    P.emit("act", lambda s, psc=psc: s.activation(
                        out=ub[psc][:], in_=sprim[psc][:], func=AF.Exp, bias=cb8[:]),
                        (f"sprim{psc}", "cb8"), (f"ub{psc}",), tag="sc_exp")
                    P.emit("pe", lambda t, jt=jt, h=h, psc=psc, pav=pav, first=(idx == 0),
                           last=(idx == len(njts)-1): t.matmul(
                        ps_av[pav][0:65, :], v_tm[:, jt*VW + h*65: jt*VW + (h+1)*65],
                        ub[psc][:], start=first, stop=last),
                        ("v_tm", "v_ones", f"ub{psc}"), (f"ps_av{pav}",), tag="av_mm")
                P.emit("dve", lambda v, pav=pav: v.reciprocal(
                    out=rden[:], in_=ps_av[pav][64:65, :]),
                    (f"ps_av{pav}",), ("rden",), tag="rden")
                P.emit("pe", lambda t: t.matmul(
                    ps_bc[0:64, :], ones64[:], rden[:], start=True, stop=True),
                    ("rden", "ones64"), ("ps_bc",), tag="den_bc")
                P.emit("act", lambda s: s.copy(bcden[0:64, :], ps_bc[0:64, :]),
                       ("ps_bc",), ("bcden",), tag="den_cp")
                P.emit("dve", lambda v, pav=pav, c=c, ro=ro: v.tensor_tensor(
                    out=o_fm[ro:ro+64, c*512:(c+1)*512], in0=ps_av[pav][0:64, :],
                    in1=bcden[0:64, :], op=ALU.mult),
                    (f"ps_av{pav}", "bcden"), ("o_fm",), tag="o_norm")
            for dchunk in range(8):
                po = dchunk % 2
                pr = dchunk % 2
                P.dma(lambda sp, m=m, dchunk=dchunk, pr=pr: sp.dma_start(
                    out=xr[pr][:], in_=xo_d[dchunk*128:(dchunk+1)*128, m*512:(m+1)*512]),
                    (f"ystg{pr}",), (f"xr{pr}",), tag="ld_xr")
                for kc in range(8):
                    P.emit("pe", lambda t, kc=kc, dchunk=dchunk, po=po: t.matmul(
                        ps_o[po][:], wob[:, kc*1024 + dchunk*128: kc*1024 + (dchunk+1)*128],
                        o_fm[:, kc*512:(kc+1)*512], start=(kc == 0), stop=(kc == 7)),
                        ("wob", "o_fm"), (f"ps_o{po}",), tag="op_mm")
                P.emit("dve", lambda v, po=po, pr=pr, dchunk=dchunk: v.scalar_tensor_tensor(
                    out=ystg[pr][:], in0=ps_o[po][:], scalar=bo_sb[:, dchunk:dchunk+1],
                    in1=xr[pr][:], op0=ALU.add, op1=ALU.add),
                    (f"ps_o{po}", "bo_sb", f"xr{pr}"), (f"ystg{pr}",), tag="resid")
                P.dma(lambda sp, m=m, dchunk=dchunk, pr=pr: sp.dma_start(
                    out=y_d[dchunk*128:(dchunk+1)*128, m*512:(m+1)*512], in_=ystg[pr][:]),
                    (f"ystg{pr}",), ("y_out",), tag="st_y")
            P.emit("dve", lambda v: v.engine_nop(), ("qring",), ("qk_done",), tag="qdone")
        P.finalize()
    return nc



L3_C = 8.0

def build_l3_attn():
    from contextlib import ExitStack
    nc = bass.Bass("TRN2", target_bir_lowering=False, debug=False)
    N = 4096
    NJT = 32
    VW = 65 * 16
    q_d = nc.dram_tensor("q", [1024, 2048], BF16, kind="ExternalInput").ap()
    k_d = nc.dram_tensor("k", [1024, N], BF16, kind="ExternalInput").ap()
    v_d = nc.dram_tensor("v", [N, 1024], BF16, kind="ExternalInput").ap()
    wo_d = nc.dram_tensor("wout", [1024, 1024], F32, kind="ExternalInput").ap()
    xo_d = nc.dram_tensor("xo", [1024, 2048], F32, kind="ExternalInput").ap()
    as_d = nc.dram_tensor("astrip", [128, 1], F32, kind="ExternalInput").ap()
    y_d = nc.dram_tensor("y", [1024, 2048], F32, kind="ExternalOutput").ap()

    es = ExitStack()
    P = Prog(nc)
    ctx = Ctx(nc, P, es)
    with es:
        k_fm = ctx.sbuf("k_fm", [128, 8 * N], BF16)
        v_tm = ctx.sbuf("v_tm", [128, NJT * VW], BF16)
        strip = ctx.sbuf("strip", [128, 7552], BF16)
        qring = ctx.sbuf("qring", [128, 8 * 512], BF16)
        a_sb = ctx.sbuf("a_sb", [128, 1], F32)
        msk = ctx.sbuf("msk", [128, 512], BF16)
        ub = [ctx.sbuf(f"ub{i}", [128, 512], BF16) for i in range(2)]
        o_fm = ctx.sbuf("o_fm", [128, 8 * 512], BF16)
        rden = ctx.sbuf("rden", [1, 512], F32)
        ones64 = ctx.sbuf("ones64", [1, 64], F32)
        bcden = ctx.sbuf("bcden", [128, 512], F32)
        wob = ctx.sbuf("wob", [128, 8 * 1024], BF16)
        wof = ctx.sbuf("wof", [128, 1024], F32)
        xr = [ctx.sbuf(f"xr{i}", [128, 512], F32) for i in range(2)]
        ystg = [ctx.sbuf(f"ystg{i}", [128, 512], F32) for i in range(2)]
        cb = ctx.sbuf("cb", [128, 1], F32)
        ctx.identities()
        ps_sc = [ctx.psum(f"ps_sc{i}", [128, 512], F32) for i in range(2)]
        ps_av = [ctx.psum(f"ps_av{i}", [128, 512], F32) for i in range(2)]
        ps_bc = ctx.psum("ps_bc", [128, 512], F32)
        ps_o = [ctx.psum(f"ps_o{i}", [128, 512], F32) for i in range(2)]

        P.emit("pool", lambda g: g.memset(cb[:], -L3_C), (), ("cb",), tag="cb")
        P.emit("pool", lambda g: g.memset(ones64[:], 1.0), (), ("ones64",), tag="ones")
        P.dma(lambda sp: sp.dma_start(out=a_sb[:], in_=as_d), (), ("a_sb",), tag="ld_a")
        scr = k_fm[:].bitcast(F32)[:, 0:7552]
        P.emit("pool", lambda g: g.iota(scr, pattern=[[-1, 7552]], base=0,
                                        channel_multiplier=1,
                                        allow_small_or_imprecise_dtypes=True),
               (), ("scrK",), tag="strip_iota")
        P.emit("dve", lambda v: v.tensor_scalar(
            out=scr, in0=scr, scalar1=a_sb[:], scalar2=None, op0=ALU.add),
            ("scrK", "a_sb"), ("scrK",), tag="strip_add")
        P.emit("dve", lambda v: v.tensor_copy(strip[:], scr), ("scrK",), ("strip",), tag="strip_cp")
        P.dma(lambda sp: sp.dma_start(
            out=k_fm[:].rearrange("p (c t) -> p c t", c=8),
            in_=k_d.rearrange("(c p) t -> p c t", p=128)), ("strip",), ("k_fm",), tag="ld_k")
        for hh in range(16):
            P.dma(lambda sp, hh=hh: sp.dma_start(
                out=v_tm[:].rearrange("p (j h e) -> p j h e", h=16, e=65)[:, :, hh, 0:64],
                in_=v_d.rearrange("(j p) (h e) -> p j h e", p=128, e=64)[:, :, hh]),
                (), ("v_tm",), tag="ld_v")
        P.emit("dve", lambda v: v.memset(
            v_tm[:].rearrange("p (j h e) -> p j h e", h=16, e=65)[:, :, :, 64:65], 1.0),
            (), ("v_ones",), tag="v_ones")
        for kc in range(8):
            P.dma(lambda sp, kc=kc: sp.dma_start(
                out=wof[:].rearrange("p (c d) -> p c d", c=1),
                in_=wo_d[kc*128:(kc+1)*128, :].rearrange("(c p) d -> p c d", p=128)),
                ("wob",) if kc else (), ("wof",), tag="ld_wo")
            P.emit("dve", lambda v, kc=kc: v.tensor_copy(wob[:, kc*1024:(kc+1)*1024], wof[:]),
                   ("wof",), ("wob",), tag="cast_wo")

        for m in range(4):
            P.dma(lambda sp, m=m: sp.dma_start(
                out=qring[:].rearrange("p (c t) -> p c t", c=8),
                in_=q_d[:, m*512:(m+1)*512].rearrange("(c p) t -> p c t", p=128)),
                ("q_done",), ("qring",), tag="ld_q")
            njts = list(range(8*m + 8))
            for h in range(16):
                pav = h % 2
                c = h // 2
                ro = (h % 2) * 64
                for idx, jt in enumerate(njts):
                    psc = idx % 2
                    P.emit("pe", lambda t, jt=jt, c=c, ro=ro, psc=psc: t.matmul(
                        ps_sc[psc][:],
                        k_fm[ro:ro+64, c*N + jt*128: c*N + (jt+1)*128],
                        qring[ro:ro+64, c*512:(c+1)*512],
                        start=True, stop=True),
                        ("k_fm", "qring"), (f"ps_sc{psc}",), tag="sc_mm")
                    P.emit("act", lambda s, psc=psc: s.activation(
                        out=ub[psc][:], in_=ps_sc[psc][:], func=AF.Exp, bias=cb[:]),
                        (f"ps_sc{psc}", "cb"), (f"ub{psc}",), tag="sc_exp")
                    if jt >= 8*m:
                        u0 = 1024*m - 128*jt + 3968
                        P.emit("dve", lambda v, u0=u0: v.tensor_scalar(
                            out=msk[:], in0=strip[:, u0:u0+512], scalar1=0.0,
                            scalar2=None, op0=ALU.is_le),
                            ("strip",), ("msk",), tag="sc_msk")
                        P.emit("dve", lambda v, psc=psc: v.tensor_tensor(
                            out=ub[psc][:], in0=ub[psc][:], in1=msk[:], op=ALU.mult),
                            (f"ub{psc}", "msk"), (f"ub{psc}",), tag="sc_apmsk")
                    P.emit("pe", lambda t, jt=jt, h=h, psc=psc, pav=pav, first=(idx == 0),
                           last=(idx == len(njts)-1): t.matmul(
                        ps_av[pav][0:65, :], v_tm[:, jt*VW + h*65: jt*VW + (h+1)*65],
                        ub[psc][:], start=first, stop=last),
                        ("v_tm", "v_ones", f"ub{psc}"), (f"ps_av{pav}",), tag="av_mm")
                P.emit("dve", lambda v, pav=pav: v.reciprocal(
                    out=rden[:], in_=ps_av[pav][64:65, :]),
                    (f"ps_av{pav}",), ("rden",), tag="rden")
                P.emit("pe", lambda t: t.matmul(
                    ps_bc[0:64, :], ones64[:], rden[:], start=True, stop=True),
                    ("rden", "ones64"), ("ps_bc",), tag="den_bc")
                P.emit("act", lambda s: s.copy(bcden[0:64, :], ps_bc[0:64, :]),
                       ("ps_bc",), ("bcden",), tag="den_cp")
                P.emit("dve", lambda v, pav=pav, c=c, ro=ro: v.tensor_tensor(
                    out=o_fm[ro:ro+64, c*512:(c+1)*512], in0=ps_av[pav][0:64, :],
                    in1=bcden[0:64, :], op=ALU.mult),
                    (f"ps_av{pav}", "bcden"), ("o_fm",), tag="o_norm")
            for dchunk in range(8):
                po = dchunk % 2
                pr = dchunk % 2
                P.dma(lambda sp, m=m, dchunk=dchunk, pr=pr: sp.dma_start(
                    out=xr[pr][:], in_=xo_d[dchunk*128:(dchunk+1)*128, m*512:(m+1)*512]),
                    (f"ystg{pr}",), (f"xr{pr}",), tag="ld_xr")
                for kc in range(8):
                    P.emit("pe", lambda t, kc=kc, dchunk=dchunk, po=po: t.matmul(
                        ps_o[po][:], wob[:, kc*1024 + dchunk*128: kc*1024 + (dchunk+1)*128],
                        o_fm[:, kc*512:(kc+1)*512], start=(kc == 0), stop=(kc == 7)),
                        ("wob", "o_fm"), (f"ps_o{po}",), tag="op_mm")
                P.emit("dve", lambda v, po=po, pr=pr: v.tensor_tensor(
                    out=ystg[pr][:], in0=ps_o[po][:], in1=xr[pr][:], op=ALU.add),
                    (f"ps_o{po}", f"xr{pr}"), (f"ystg{pr}",), tag="resid")
                P.dma(lambda sp, m=m, dchunk=dchunk, pr=pr: sp.dma_start(
                    out=y_d[dchunk*128:(dchunk+1)*128, m*512:(m+1)*512], in_=ystg[pr][:]),
                    (f"ystg{pr}",), ("y_out",), tag="st_y")
            P.emit("dve", lambda v: v.engine_nop(), ("qring",), ("q_done",), tag="qdone")
        P.finalize()
    return nc



def _emit_top32_weights(ctx, L_sb, wexp, ps_we, scratch, uid):
    P = ctx.P
    Lw = scratch[0:1, 0:256]
    lad = scratch[0:1, 256:288]
    t1 = scratch[0:1, 288:289]
    erow = scratch[0:1, 300:556]
    K = uid
    P.emit("dve", lambda v: v.tensor_copy(Lw, L_sb), (uid + "_L",), (K + "_Lw",), tag="t32_cp")
    for r in range(4):
        P.emit("dve", lambda v, r=r: v.max(out=lad[0:1, r*8:(r+1)*8], in_=Lw),
               (K + "_Lw",), (K + "_lad",), tag="t32_max8")
        if r < 4:
            P.emit("dve", lambda v, r=r: v.match_replace(
                out=Lw, in_to_replace=lad[0:1, r*8:(r+1)*8], in_values=Lw,
                imm_value=-1e30), (K + "_Lw", K + "_lad"), (K + "_Lw",), tag="t32_mr")
    thr = lad[0:1, 31:32]
    rmax = lad[0:1, 0:1]
    P.emit("dve", lambda v: v.tensor_scalar(
        out=t1, in0=rmax, scalar1=-1.0, scalar2=None, op0=ALU.mult),
        (K + "_lad",), (K + "_nm",), tag="t32_nm")
    P.emit("act", lambda s: s.activation(out=erow, in_=L_sb, func=AF.Exp, bias=t1),
           (uid + "_L", K + "_nm"), (K + "_e",), tag="t32_exp")
    ind = scratch[0:1, 556:600 + 212] if False else None
    P.emit("dve", lambda v: v.tensor_scalar(
        out=Lw, in0=L_sb, scalar1=thr, scalar2=None, op0=ALU.is_ge),
        (uid + "_L", K + "_lad"), (K + "_Lw",), tag="t32_ind")
    P.emit("dve", lambda v: v.scalar_tensor_tensor(
        out=erow, in0=erow, scalar=1.0, in1=Lw, op0=ALU.mult, op1=ALU.mult,
        accum_out=None), (K + "_e", K + "_Lw"), (K + "_e",), tag="t32_es")
    P.emit("dve", lambda v: v.reduce_sum(out=t1, in_=erow, axis=AX.X),
           (K + "_e",), (K + "_s",), tag="t32_sum")
    P.emit("dve", lambda v: v.reciprocal(out=t1, in_=t1), (K + "_s",), (K + "_s",), tag="t32_r")
    P.emit("dve", lambda v: v.tensor_scalar(
        out=erow, in0=erow, scalar1=t1, scalar2=None, op0=ALU.mult),
        (K + "_e", K + "_s"), (K + "_e",), tag="t32_w")
    ones64 = ctx.sb["ones64"]
    ev = erow.rearrange("o (k two) -> o k two", two=2)
    P.emit("pe", lambda t: t.matmul(ps_we[0:64, 0:128], ones64[:], ev[:, :, 0],
                                    start=True, stop=True),
           (K + "_e", "ones64"), ("psW0",), tag="t32_bc0")
    P.emit("pe", lambda t: t.matmul(ps_we[64:128, 0:128], ones64[:], ev[:, :, 1],
                                    start=True, stop=True),
           (K + "_e", "ones64"), ("psW0",), tag="t32_bc1")
    P.emit("act", lambda s: s.copy(wexp[:], ps_we[:, 0:128]),
           ("psW0",), (uid + "_wexp",), tag="t32_evac")


def build_l0ffn_w():
    from contextlib import ExitStack
    nc = bass.Bass("TRN2", target_bir_lowering=False, debug=False)
    KC = 128
    a1_d = nc.dram_tensor("a1", [16384, 1024], F32, kind="ExternalInput").ap()
    b1_d = nc.dram_tensor("b1", [16384, 512], F32, kind="ExternalInput").ap()
    a2_d = nc.dram_tensor("a2", [16384, 512], F32, kind="ExternalInput").ap()
    b2_d = nc.dram_tensor("b2", [16384, 1024], F32, kind="ExternalInput").ap()
    l1_d = nc.dram_tensor("lg1", [1, 256], F32, kind="ExternalInput").ap()
    l2_d = nc.dram_tensor("lg2", [1, 256], F32, kind="ExternalInput").ap()
    w1_d = nc.dram_tensor("w1s", [1024, 512], BF16, kind="ExternalOutput").ap()
    w2_d = nc.dram_tensor("w2s", [512, 1024], BF16, kind="ExternalOutput").ap()

    es = ExitStack()
    P = Prog(nc)
    ctx = Ctx(nc, P, es)
    with es:
        l1 = ctx.sbuf("l1", [1, 256], F32)
        l2 = ctx.sbuf("l2", [1, 256], F32)
        scr = ctx.sbuf("scr", [1, 812], F32)
        we1 = ctx.sbuf("we1", [128, 128], F32)
        we2 = ctx.sbuf("we2", [128, 128], F32)
        ones64 = ctx.sbuf("ones64", [1, 64], F32)
        af = [ctx.sbuf(f"af{i}", [128, 1024], F32) for i in range(2)]
        bf = [ctx.sbuf(f"bf{i}", [128, 1024], F32) for i in range(2)]
        ab = [ctx.sbuf(f"ab{i}", [128, 1024], BF16) for i in range(2)]
        bb = [ctx.sbuf(f"bb{i}", [128, 1024], BF16) for i in range(2)]
        stg = [ctx.sbuf(f"stg{i}", [128, 512], BF16) for i in range(2)]
        ctx.identities()
        psW = [ctx.psum(f"psW{i}", [128, 512], F32) for i in range(8)]

        P.emit("pool", lambda g: g.memset(ones64[:], 1.0), (), ("ones64",), tag="ones")
        P.dma(lambda sp: sp.dma_start(out=l1[:], in_=l1_d), (), ("t1_L",), tag="ld_l1")
        P.dma(lambda sp: sp.dma_start(out=l2[:], in_=l2_d), (), ("t2_L",), tag="ld_l2")
        _emit_top32_weights(ctx, l1[:], we1[:], psW[0][:], scr[:], "t1")
        _emit_top32_weights(ctx, l2[:], we2[:], psW[0][:], scr[:], "t2")

        def phase(a_d, b_d, we, wekey, aw, bw, out_d, M_chunks, N, uid):
            nb = N // 512
            for kc in range(KC):
                r = kc % 2
                P.dma(lambda sp, kc=kc, r=r, a_d=a_d, aw=aw: sp.dma_start(
                    out=af[r][:, 0:aw], in_=a_d[kc*128:(kc+1)*128, :]),
                    (uid + f"_afz{r}",), (uid + f"_af{r}",), tag="ld_a")
                P.dma(lambda sp, kc=kc, r=r, b_d=b_d, bw=bw: sp.dma_start(
                    out=bf[r][:, 0:bw], in_=b_d[kc*128:(kc+1)*128, :]),
                    (uid + f"_bfz{r}",), (uid + f"_bf{r}",), tag="ld_b")
                P.emit("dve", lambda v, kc=kc, r=r, aw=aw, we=we: v.tensor_scalar(
                    out=ab[r][:, 0:aw], in0=af[r][:, 0:aw], scalar1=we[:, kc:kc+1],
                    scalar2=None, op0=ALU.mult),
                    (uid + f"_af{r}", wekey), (uid + f"_ab{r}", uid + f"_afz{r}"), tag="sc_a")
                P.emit("dve", lambda v, r=r, bw=bw: v.tensor_copy(
                    bb[r][:, 0:bw], bf[r][:, 0:bw]),
                    (uid + f"_bf{r}",), (uid + f"_bb{r}", uid + f"_bfz{r}"), tag="cast_b")
                for mi in range(M_chunks):
                    for ni in range(nb):
                        bank = mi * nb + ni
                        P.emit("pe", lambda t, kc=kc, r=r, mi=mi, ni=ni, bank=bank: t.matmul(
                            psW[bank][:], ab[r][:, mi*128:(mi+1)*128],
                            bb[r][:, ni*512:(ni+1)*512],
                            start=(kc == 0), stop=(kc == KC-1)),
                            (uid + f"_ab{r}", uid + f"_bb{r}"), (f"psW{bank}",),
                            tag="w_mm")
            for mi in range(M_chunks):
                for ni in range(nb):
                    bank = mi * nb + ni
                    sg = bank % 2
                    P.emit("dve", lambda v, bank=bank, sg=sg: v.tensor_copy(
                        stg[sg][:], psW[bank][:]),
                        (f"psW{bank}",), (uid + f"_stg{sg}",), tag="w_evac")
                    P.dma(lambda sp, mi=mi, ni=ni, sg=sg, out_d=out_d: sp.dma_start(
                        out=out_d[mi*128:(mi+1)*128, ni*512:(ni+1)*512], in_=stg[sg][:]),
                        (uid + f"_stg{sg}",), (uid + "_out",), tag="w_st")

        phase(a1_d, b1_d, we1[:], "t1_wexp", 1024, 512, w1_d, 8, 512, "w1")
        phase(a2_d, b2_d, we2[:], "t2_wexp", 512, 1024, w2_d, 4, 1024, "w2")
        P.finalize()
    return nc



def build_l0ffn_main():
    from contextlib import ExitStack
    nc = bass.Bass("TRN2", target_bir_lowering=False, debug=False)
    T = 2048
    x_d = nc.dram_tensor("x", [1024, T], F32, kind="ExternalInput").ap()
    w1_d = nc.dram_tensor("w1", [1024, 4096], BF16, kind="ExternalInput").ap()
    w2_d = nc.dram_tensor("w2", [4096, 1024], BF16, kind="ExternalInput").ap()
    y_d = nc.dram_tensor("y", [1024, T], F32, kind="ExternalOutput").ap()

    es = ExitStack()
    P = Prog(nc)
    ctx = Ctx(nc, P, es)
    C1 = 0.7978845608028654
    C2 = C1 * 0.044715
    with es:
        xring = [ctx.sbuf(f"xring{i}", [128, 8 * 512], F32) for i in range(2)]
        ln_fm = ctx.sbuf("ln_fm", [128, 8 * T], BF16)
        statT = ctx.sbuf("statT", [1, 2 * T + 512], F32)
        sq = ctx.sbuf("sq", [128, 512], F32)
        ones1 = ctx.sbuf("ones1", [128, 1], F32)
        onesr = ctx.sbuf("onesr", [1, 128], F32)
        w1b = [ctx.sbuf(f"w1b{i}", [128, 8 * 512], BF16) for i in range(2)]
        w2b = [ctx.sbuf(f"w2b{i}", [128, 1024], BF16) for i in range(4)]
        out_acc = ctx.sbuf("out_acc", [128, 8 * 512], F32)
        h_t = ctx.sbuf("h_t", [128, 32 * 512], BF16)
        g1 = ctx.sbuf("g1", [128, 512], F32)
        g2 = ctx.sbuf("g2", [128, 512], F32)
        xr = [ctx.sbuf(f"xr{i}", [128, 512], F32) for i in range(2)]
        ystg = [ctx.sbuf(f"ystg{i}", [128, 512], F32) for i in range(2)]
        ctx.identities()
        ps = {k: ctx.psum(f"ps_{k}", [128, 512], F32)
              for k in ("sum", "sumsq", "bc_mu", "bc_rs")}
        pmm = [ctx.psum(f"pmm{i}", [128, 512], F32) for i in range(2)]

        P.emit("pool", lambda g: g.memset(ones1[:], 1.0), (), ("ones1",), tag="ones")
        P.emit("pool", lambda g: g.memset(onesr[:], 1.0), (), ("onesr",), tag="ones")
        emit_ln_fm_stream(ctx, x_d, ln_fm, T, statT, sq[:], xring, ps, "ln_fm", "ln")

        for tb in range(4):
            tcols = slice(tb * 512, (tb + 1) * 512)
            for cg in range(8):
                r = cg % 2
                P.dma(lambda sp, cg=cg, r=r: sp.dma_start(
                    out=w1b[r][:].rearrange("p (c d) -> p c d", c=8),
                    in_=w1_d[:, cg*512:(cg+1)*512].rearrange("(c p) d -> p c d", p=128)),
                    (f"w1bz{r}",), (f"w1b{r}",), tag="ld_w1")
                for oc in range(4):
                    pm = (cg * 4 + oc) % 2
                    for dc in range(8):
                        P.emit("pe", lambda t, cg=cg, oc=oc, dc=dc, pm=pm, r=r, tb=tb: t.matmul(
                            pmm[pm][:], w1b[r][:, dc*512 + oc*128: dc*512 + (oc+1)*128],
                            ln_fm[:, dc*T + tb*512: dc*T + (tb+1)*512],
                            start=(dc == 0), stop=(dc == 7)),
                            (f"w1b{r}", "ln_fm"), (f"pmm{pm}",), tag="h_mm")
                    fi = cg * 4 + oc
                    P.emit("act", lambda s, pm=pm: s.activation(
                        out=g1[:], in_=pmm[pm][:], func=AF.Square),
                        (f"pmm{pm}",), ("g1",), tag="ge_sq")
                    P.emit("dve", lambda v: v.tensor_scalar(
                        out=g1[:], in0=g1[:], scalar1=C2, scalar2=C1,
                        op0=ALU.mult, op1=ALU.add), ("g1",), ("g1",), tag="ge_u")
                    P.emit("dve", lambda v, pm=pm: v.tensor_tensor(
                        out=g1[:], in0=g1[:], in1=pmm[pm][:], op=ALU.mult),
                        ("g1", f"pmm{pm}"), ("g1",), tag="ge_u2")
                    P.emit("act", lambda s: s.activation(out=g2[:], in_=g1[:], func=AF.Tanh),
                           ("g1",), ("g2",), tag="ge_tanh")
                    P.emit("dve", lambda v: v.tensor_scalar(
                        out=g2[:], in0=g2[:], scalar1=1.0, scalar2=0.5,
                        op0=ALU.add, op1=ALU.mult), ("g2",), ("g2",), tag="ge_r1")
                    P.emit("dve", lambda v, pm=pm, fi=fi: v.tensor_tensor(
                        out=h_t[:, fi*512:(fi+1)*512], in0=g2[:], in1=pmm[pm][:],
                        op=ALU.mult), ("g2", f"pmm{pm}"), ("h_t", f"pmm{pm}_f"), tag="ge_h")
            for fg in range(8):
                for ff in range(4):
                    fc = fg * 4 + ff
                    r = fc % 4
                    P.dma(lambda sp, fc=fc, r=r: sp.dma_start(
                        out=w2b[r][:], in_=w2_d[fc*128:(fc+1)*128, :]),
                        (f"w2bz{r}",), (f"w2b{r}",), tag="ld_w2")
                for oc in range(8):
                    po = oc % 2
                    for ff in range(4):
                        fc = fg * 4 + ff
                        r = fc % 4
                        P.emit("pe", lambda t, fc=fc, oc=oc, r=r, po=po, ff=ff: t.matmul(
                            pmm[po][:], w2b[r][:, oc*128:(oc+1)*128],
                            h_t[:, fc*512:(fc+1)*512],
                            start=(ff == 0), stop=(ff == 3)),
                            (f"w2b{r}", "h_t"), (f"pmm{po}",), tag="o_mm")
                    if fg == 0:
                        P.emit("dve", lambda v, po=po, oc=oc: v.tensor_copy(
                            out_acc[:, oc*512:(oc+1)*512], pmm[po][:]),
                            (f"pmm{po}",), ("out_acc", f"pmm{po}_f"), tag="o_cp")
                    else:
                        P.emit("dve", lambda v, po=po, oc=oc: v.tensor_tensor(
                            out=out_acc[:, oc*512:(oc+1)*512],
                            in0=out_acc[:, oc*512:(oc+1)*512], in1=pmm[po][:],
                            op=ALU.add), ("out_acc", f"pmm{po}"),
                            ("out_acc", f"pmm{po}_f"), tag="o_add")
                for ff in range(4):
                    r = (fg * 4 + ff) % 4
                    P.emit("dve", lambda v, r=r: v.engine_nop(), (f"w2b{r}",),
                           (f"w2bz{r}",), tag="w2free")
            for oc in range(8):
                pr = oc % 2
                P.dma(lambda sp, oc=oc, tb=tb, pr=pr: sp.dma_start(
                    out=xr[pr][:], in_=x_d[oc*128:(oc+1)*128, tb*512:(tb+1)*512]),
                    (f"ystg{pr}",), (f"xr{pr}",), tag="ld_xr")
                P.emit("dve", lambda v, pr=pr, oc=oc: v.tensor_tensor(
                    out=ystg[pr][:], in0=out_acc[:, oc*512:(oc+1)*512], in1=xr[pr][:],
                    op=ALU.add), ("out_acc", f"xr{pr}"), (f"ystg{pr}",), tag="resid")
                P.dma(lambda sp, oc=oc, tb=tb, pr=pr: sp.dma_start(
                    out=y_d[oc*128:(oc+1)*128, tb*512:(tb+1)*512], in_=ystg[pr][:]),
                    (f"ystg{pr}",), ("y_out",), tag="st_y")
        P.finalize()
    return nc



def build_l2_ffn():
    from contextlib import ExitStack
    nc = bass.Bass("TRN2", target_bir_lowering=False, debug=False)
    T = 2048
    NE = 20
    x_d = nc.dram_tensor("x", [1024, T], F32, kind="ExternalInput").ap()
    gw_d = nc.dram_tensor("gwt", [1024, 16], F32, kind="ExternalInput").ap()
    be_d = nc.dram_tensor("bias_e", [1, 16], F32, kind="ExternalInput").ap()
    up_d = nc.dram_tensor("upt", [1024, NE * 512], F32, kind="ExternalInput").ap()
    dn_d = nc.dram_tensor("dnt", [NE * 512, 1024], F32, kind="ExternalInput").ap()
    y_d = nc.dram_tensor("y", [1024, T], F32, kind="ExternalOutput").ap()

    es = ExitStack()
    P = Prog(nc)
    ctx = Ctx(nc, P, es)
    with es:
        out_acc = ctx.sbuf("out_acc", [128, 8 * T], F32)
        xring = [out_acc[:][:, 0:4096], out_acc[:][:, 4096:8192]]
        ln_fm = ctx.sbuf("ln_fm", [128, 8 * T], BF16)
        statT = ctx.sbuf("statT", [1, 2 * T + 512], F32)
        sq = ctx.sbuf("sq", [128, 512], F32)
        ones1 = ctx.sbuf("ones1", [128, 1], F32)
        onesr = ctx.sbuf("onesr", [1, 128], F32)
        gwb = ctx.sbuf("gwb", [128, 8 * 16], BF16)
        gwf = ctx.sbuf("gwf", [128, 8 * 16], F32)
        be_row = ctx.sbuf("be_row", [1, 16], F32)
        eps_row = ctx.sbuf("eps_row", [1, 16], F32)
        be_b = ctx.sbuf("be_b", [128, 16], F32)
        eps_b = ctx.sbuf("eps_b", [128, 16], F32)
        rt = ctx.sbuf("rt", [128, 16 * 8], F32)
        comb = ctx.sbuf("comb", [128, 16 * 16], F32)
        combT = ctx.sbuf("combT", [32, T], F32)
        upb = [ctx.sbuf(f"upb{i}", [128, 8 * 512], BF16) for i in range(2)]
        dnb = [ctx.sbuf(f"dnb{i}", [128, 4 * 1024], BF16) for i in range(2)]
        h_e = ctx.sbuf("h_e", [128, 4 * T], BF16)
        g1 = ctx.sbuf("g1", [128, 512], F32)
        xr = [ctx.sbuf(f"xr{i}", [128, 512], F32) for i in range(2)]
        ystg = [ctx.sbuf(f"ystg{i}", [128, 512], F32) for i in range(2)]
        cbrow = [ctx.sbuf(f"cbrow{i}", [1, 2048], F32) for i in range(2)]
        ctx.identities()
        ps = {k: ctx.psum(f"ps_{k}", [128, 512], F32)
              for k in ("sum", "sumsq", "bc_mu", "bc_rs")}
        pmm = [ctx.psum(f"pmm{i}", [128, 512], F32) for i in range(2)]
        ps_cb = ctx.psum("ps_cb", [128, 512], F32)
        ps_r = ctx.psum("ps_r", [128, 512], F32)

        P.emit("pool", lambda g: g.memset(ones1[:], 1.0), (), ("ones1",), tag="ones")
        P.emit("pool", lambda g: g.memset(onesr[:], 1.0), (), ("onesr",), tag="ones")
        emit_ln_fm_stream(ctx, x_d, ln_fm, T, statT, sq[:], xring, ps, "ln_fm", "ln")
        P.emit("pool", lambda g: g.engine_nop(), ("ln_xr0", "ln_xr1"), ("out_acc",),
               tag="alias_bridge")

        P.dma(lambda sp: sp.dma_start(
            out=gwf[:].rearrange("p (c e) -> p c e", c=8),
            in_=gw_d.rearrange("(c p) e -> p c e", p=128)), (), ("gwf",), tag="ld_gw")
        P.emit("dve", lambda v: v.tensor_copy(gwb[:], gwf[:]), ("gwf",), ("gwb",), tag="cast_gw")
        P.dma(lambda sp: sp.dma_start(out=be_row[:], in_=be_d), (), ("be_row",), tag="ld_be")
        P.emit("pool", lambda g: g.iota(eps_row[:], pattern=[[-1, 16]], base=16,
                                        channel_multiplier=0,
                                        allow_small_or_imprecise_dtypes=True),
               (), ("eps_row",), tag="eps_iota")
        P.emit("dve", lambda v: v.tensor_scalar(
            out=eps_row[:], in0=eps_row[:], scalar1=1e-20, scalar2=None, op0=ALU.mult),
            ("eps_row",), ("eps_row",), tag="eps_sc")
        P.emit("pe", lambda t: t.matmul(ps_cb[:, 0:16], onesr[0:1, :], be_row[:],
                                        start=True, stop=True),
               ("be_row", "onesr"), ("ps_cb",), tag="bc_be")
        P.emit("act", lambda s: s.copy(be_b[:], ps_cb[:, 0:16]), ("ps_cb",), ("be_b",), tag="cp_be")
        P.emit("pe", lambda t: t.matmul(ps_cb[:, 0:16], onesr[0:1, :], eps_row[:],
                                        start=True, stop=True),
               ("eps_row", "onesr", "be_b"), ("ps_cb",), tag="bc_eps")
        P.emit("act", lambda s: s.copy(eps_b[:], ps_cb[:, 0:16]), ("ps_cb",), ("eps_b",), tag="cp_eps")

        for tt in range(16):
            for dc in range(8):
                P.emit("pe", lambda t, tt=tt, dc=dc: t.matmul(
                    ps_r[:, 0:16], ln_fm[:, dc*T + tt*128: dc*T + (tt+1)*128],
                    gwb[:, dc*16:(dc+1)*16], start=(dc == 0), stop=(dc == 7)),
                    ("ln_fm", "gwb"), ("ps_r",), tag="r_mm")
            sc = rt[:, 0:16]; scb = rt[:, 16:32]; g1r = rt[:, 32:36]; g2r = rt[:, 36:40]
            gm = rt[:, 40:44]; t8 = rt[:, 48:56]; selw = rt[:, 64:80]
            ssum = rt[:, 80:81]; m1b4 = rt[:, 96:112]
            P.emit("act", lambda s: s.activation(out=sc, in_=ps_r[:, 0:16],
                                                 func=AF.Sigmoid),
                   ("ps_r",), ("rt_sc",), tag="r_sig")
            P.emit("dve", lambda v: v.tensor_tensor(out=scb, in0=sc, in1=be_b[:],
                                                    op=ALU.add),
                   ("rt_sc", "be_b"), ("rt_scb",), tag="r_scb")
            scb4 = scb.rearrange("p (g f) -> p g f", g=4)
            P.emit("dve", lambda v: v.tensor_reduce(out=g1r, in_=scb4, axis=AX.X,
                                                    op=ALU.max),
                   ("rt_scb",), ("rt_g1",), tag="r_g1")
            g1b = g1r.rearrange("p (g o) -> p g o", o=1).to_broadcast([128, 4, 4])
            P.emit("dve", lambda v: v.tensor_tensor(out=m1b4.rearrange("p (g f) -> p g f", g=4),
                                                    in0=scb4, in1=g1b, op=ALU.is_ge),
                   ("rt_scb", "rt_g1"), ("rt_m1",), tag="r_eq")
            P.emit("dve", lambda v: v.scalar_tensor_tensor(
                out=m1b4, in0=m1b4, scalar=-1e30, in1=scb, op0=ALU.mult, op1=ALU.add),
                ("rt_m1", "rt_scb"), ("rt_m1",), tag="r_rm")
            P.emit("dve", lambda v: v.tensor_reduce(
                out=g2r, in_=m1b4.rearrange("p (g f) -> p g f", g=4), axis=AX.X,
                op=ALU.max), ("rt_m1",), ("rt_g2",), tag="r_g2")
            P.emit("dve", lambda v: v.tensor_tensor(out=g1r, in0=g1r, in1=g2r,
                                                    op=ALU.add),
                   ("rt_g1", "rt_g2"), ("rt_g1",), tag="r_gsum")
            P.emit("dve", lambda v: v.tensor_reduce(
                out=gm[:, 0:1], in_=g1r.rearrange("p (o f) -> p o f", o=1), axis=AX.X,
                op=ALU.max), ("rt_g1",), ("rt_gm",), tag="r_gm1")
            gm1b = gm[:, 0:1].to_broadcast([128, 4])
            P.emit("dve", lambda v: v.tensor_tensor(out=g2r, in0=g1r, in1=gm1b,
                                                    op=ALU.is_ge),
                   ("rt_g1", "rt_gm"), ("rt_g2",), tag="r_geq")
            P.emit("dve", lambda v: v.scalar_tensor_tensor(
                out=g2r, in0=g2r, scalar=-1e30, in1=g1r, op0=ALU.mult, op1=ALU.add),
                ("rt_g2", "rt_g1"), ("rt_g2",), tag="r_grm")
            P.emit("dve", lambda v: v.tensor_reduce(
                out=gm[:, 1:2], in_=g2r.rearrange("p (o f) -> p o f", o=1), axis=AX.X,
                op=ALU.max), ("rt_g2",), ("rt_gm",), tag="r_gm2")
            P.emit("dve", lambda v: v.tensor_scalar(
                out=g2r, in0=g1r, scalar1=gm[:, 1:2], scalar2=None, op0=ALU.is_ge),
                ("rt_g1", "rt_gm"), ("rt_g2",), tag="r_gmask")
            g2b = g2r.rearrange("p (g o) -> p g o", o=1).to_broadcast([128, 4, 4])
            P.emit("dve", lambda v: v.tensor_tensor(
                out=m1b4.rearrange("p (g f) -> p g f", g=4), in0=scb4, in1=g2b,
                op=ALU.mult), ("rt_scb", "rt_g2"), ("rt_m1",), tag="r_masked")
            P.emit("dve", lambda v: v.tensor_tensor(out=m1b4, in0=m1b4, in1=eps_b[:],
                                                    op=ALU.add),
                   ("rt_m1", "eps_b"), ("rt_m1",), tag="r_eps")
            P.emit("dve", lambda v: v.max(out=t8, in_=m1b4), ("rt_m1",), ("rt_t8",), tag="r_max8")
            P.emit("dve", lambda v: v.tensor_scalar(
                out=selw, in0=m1b4, scalar1=t8[:, 3:4], scalar2=None, op0=ALU.is_ge),
                ("rt_m1", "rt_t8"), ("rt_selw",), tag="r_sel")
            P.emit("dve", lambda v: v.tensor_tensor(out=selw, in0=selw, in1=sc,
                                                    op=ALU.mult),
                   ("rt_selw", "rt_sc"), ("rt_selw",), tag="r_selw")
            P.emit("dve", lambda v: v.tensor_reduce(
                out=ssum, in_=selw.rearrange("p (o f) -> p o f", o=1), axis=AX.X,
                op=ALU.add), ("rt_selw",), ("rt_ss",), tag="r_ssum")
            P.emit("dve", lambda v: v.tensor_scalar(
                out=ssum, in0=ssum, scalar1=1e-20, scalar2=None, op0=ALU.add),
                ("rt_ss",), ("rt_ss",), tag="r_sse")
            P.emit("dve", lambda v: v.reciprocal(out=ssum, in_=ssum),
                   ("rt_ss",), ("rt_ss",), tag="r_rec")
            P.emit("dve", lambda v, tt=tt: v.tensor_scalar(
                out=comb[:, tt*16:(tt+1)*16], in0=selw, scalar1=ssum, scalar2=2.5,
                op0=ALU.mult, op1=ALU.mult),
                ("rt_selw", "rt_ss"), ("comb",), tag="r_comb")
        P.emit("dve", lambda v: v.memset(combT[:], 1.0), (), ("combT_sh",), tag="combT_init")
        for tt in range(16):
            P.emit("pe", lambda t, tt=tt: t.transpose(
                ps_r[0:16, 0:128].bitcast(F32) if False else ps_r[0:16, 0:128],
                comb[:, tt*16:(tt+1)*16], ctx.id_f32[:]),
                ("comb", "id_f32"), ("ps_r",), tag="combT_t")
            P.emit("act", lambda s, tt=tt: s.copy(combT[0:16, tt*128:(tt+1)*128],
                                                  ps_r[0:16, 0:128]),
                   ("ps_r", "combT_sh"), ("combT",), tag="combT_cp")

        for e in range(NE):
            r = e % 2
            P.dma(lambda sp, e=e, r=r: sp.dma_start(
                out=cbrow[r][:], in_=combT[e:e+1, :]),
                ("combT", "combT_sh", f"cbrowz{r}"), (f"cbrow{r}",), tag="ld_cbrow")
            P.emit("pool", lambda g, e=e, r=r: g.dma_start(
                out=upb[r][:].rearrange("p (c d) -> p c d", c=8),
                in_=up_d[:, e*512:(e+1)*512].rearrange("(c p) d -> p c d", p=128)),
                (f"upbz{r}",), (f"upb{r}",), inc=16, tag="ld_up")
            P.emit("pool", lambda g, e=e, r=r: g.dma_start(
                out=dnb[r][:].rearrange("p (c d) -> p c d", c=4),
                in_=dn_d[e*512:(e+1)*512, :].rearrange("(c p) d -> p c d", p=128)),
                (f"dnbz{r}",), (f"dnb{r}",), inc=16, tag="ld_dn")
            for oc in range(4):
                for tc in range(4):
                    pm = (oc * 4 + tc) % 2
                    for dc in range(8):
                        P.emit("pe", lambda t, oc=oc, tc=tc, dc=dc, pm=pm, r=r: t.matmul(
                            pmm[pm][:], upb[r][:, dc*512 + oc*128: dc*512 + (oc+1)*128],
                            ln_fm[:, dc*T + tc*512: dc*T + (tc+1)*512],
                            start=(dc == 0), stop=(dc == 7)),
                            (f"upb{r}", "ln_fm"), (f"pmm{pm}",), tag="up_mm")
                    P.emit("pe", lambda t, tc=tc, r=r: t.matmul(
                        ps_cb[:], onesr[0:1, :], cbrow[r][0:1, tc*512:(tc+1)*512],
                        start=True, stop=True),
                        (f"cbrow{r}", "onesr"), ("ps_cb",), tag="cb_bc")
                    P.emit("dve", lambda v, pm=pm: v.tensor_scalar(
                        out=g1[:], in0=pmm[pm][:], scalar1=0.0, scalar2=None,
                        op0=ALU.max), (f"pmm{pm}",), ("g1",), tag="relu")
                    P.emit("dve", lambda v, pm=pm: v.tensor_tensor(
                        out=g1[:], in0=g1[:], in1=pmm[pm][:], op=ALU.mult),
                        ("g1", f"pmm{pm}"), ("g1", f"pmm{pm}_f"), tag="relu2b")
                    P.emit("dve", lambda v, oc=oc, tc=tc: v.tensor_tensor(
                        out=h_e[:, oc*T + tc*512: oc*T + (tc+1)*512], in0=g1[:],
                        in1=ps_cb[:], op=ALU.mult),
                        ("g1", "ps_cb"), ("h_e",), tag="h_scale")
            P.emit("dve", lambda v, r=r: v.engine_nop(), (f"upb{r}", f"cbrow{r}"),
                   (f"upbz{r}", f"cbrowz{r}"), tag="upfree")
            for oc in range(8):
                for tc in range(4):
                    pm = (oc * 4 + tc) % 2
                    for jc in range(4):
                        P.emit("pe", lambda t, oc=oc, tc=tc, jc=jc, pm=pm, r=r: t.matmul(
                            pmm[pm][:], dnb[r][:, jc*1024 + oc*128: jc*1024 + (oc+1)*128],
                            h_e[:, jc*T + tc*512: jc*T + (tc+1)*512],
                            start=(jc == 0), stop=(jc == 3)),
                            (f"dnb{r}", "h_e"), (f"pmm{pm}",), tag="dn_mm")
                    if e == 0:
                        P.emit("dve", lambda v, pm=pm, oc=oc, tc=tc: v.tensor_copy(
                            out_acc[:, oc*T + tc*512: oc*T + (tc+1)*512], pmm[pm][:]),
                            (f"pmm{pm}",), ("out_acc", f"pmm{pm}_f"), tag="o_cp")
                    else:
                        P.emit("dve", lambda v, pm=pm, oc=oc, tc=tc: v.tensor_tensor(
                            out=out_acc[:, oc*T + tc*512: oc*T + (tc+1)*512],
                            in0=out_acc[:, oc*T + tc*512: oc*T + (tc+1)*512],
                            in1=pmm[pm][:], op=ALU.add),
                            ("out_acc", f"pmm{pm}"), ("out_acc", f"pmm{pm}_f"), tag="o_add")
            P.emit("dve", lambda v, r=r: v.engine_nop(), (f"dnb{r}",), (f"dnbz{r}",), tag="dnfree")
        for oc in range(8):
            for tc in range(4):
                pr = tc % 2
                P.dma(lambda sp, oc=oc, tc=tc, pr=pr: sp.dma_start(
                    out=xr[pr][:], in_=x_d[oc*128:(oc+1)*128, tc*512:(tc+1)*512]),
                    (f"ystg{pr}",), (f"xr{pr}",), tag="ld_xr")
                P.emit("dve", lambda v, pr=pr, oc=oc, tc=tc: v.tensor_tensor(
                    out=ystg[pr][:], in0=out_acc[:, oc*T + tc*512: oc*T + (tc+1)*512],
                    in1=xr[pr][:], op=ALU.add),
                    ("out_acc", f"xr{pr}"), (f"ystg{pr}",), tag="resid")
                P.dma(lambda sp, oc=oc, tc=tc, pr=pr: sp.dma_start(
                    out=y_d[oc*128:(oc+1)*128, tc*512:(tc+1)*512], in_=ystg[pr][:]),
                    (f"ystg{pr}",), ("y_out",), tag="st_y")
        P.finalize()
    return nc




def build_l1ffn_layer(first, last):
    from contextlib import ExitStack
    nc = bass.Bass("TRN2", target_bir_lowering=False, debug=False)
    PW = 1152
    xp_d = nc.dram_tensor("xp", [1024, 4608], F32, kind="ExternalInput").ap()
    xm_d = nc.dram_tensor("xm1", [1024, 4608], F32, kind="ExternalInput").ap()
    pos_d = nc.dram_tensor("pos", [128, 36], F32, kind="ExternalInput").ap()
    vm_d = nc.dram_tensor("vmask", [128, 36], F32, kind="ExternalInput").ap()
    wq_d = nc.dram_tensor("wq", [1024, 512], F32, kind="ExternalInput").ap()
    wk_d = nc.dram_tensor("wk", [1024, 512], F32, kind="ExternalInput").ap()
    wv_d = nc.dram_tensor("wv", [1024, 512], F32, kind="ExternalInput").ap()
    wo_d = nc.dram_tensor("wo", [512, 1024], F32, kind="ExternalInput").ap()
    qs_d = nc.dram_tensor("qsrow", [1, 64], F32, kind="ExternalInput").ap()
    ks_d = nc.dram_tensor("ksrow", [1, 64], F32, kind="ExternalInput").ap()
    if_d = nc.dram_tensor("invf", [1, 32], F32, kind="ExternalInput").ap()
    lb_d = nc.dram_tensor("lnb", [1, 32], F32, kind="ExternalInput").ap()
    w1_d = nc.dram_tensor("w1p", [1024, 5632], F32, kind="ExternalInput").ap()
    b1_d = nc.dram_tensor("b1p", [128, 44], F32, kind="ExternalInput").ap()
    w2_d = nc.dram_tensor("w2p", [2816, 1024], F32, kind="ExternalInput").ap()
    b2_d = nc.dram_tensor("b2", [128, 8], F32, kind="ExternalInput").ap()
    lg_d = nc.dram_tensor("lg", [128, 8], F32, kind="ExternalInput").ap()
    lbias_d = nc.dram_tensor("lbeta", [128, 8], F32, kind="ExternalInput").ap()
    xo_d = nc.dram_tensor("xo", [1024, 2048], F32, kind="ExternalInput").ap()
    y_d = nc.dram_tensor("y", [1024, 2048], F32, kind="ExternalOutput").ap()

    w1bd = nc.dram_tensor("w1bd", [1024, 5632], BF16, kind="Internal").ap()
    w2bd = nc.dram_tensor("w2bd", [2816, 1024], BF16, kind="Internal").ap()
    ydbg_d = nc.dram_tensor("ydbg", [1024, 4 * 640], F32, kind="ExternalOutput").ap()
    kdbg_d = nc.dram_tensor("kdbg", [128, 9 * 512], BF16, kind="ExternalOutput").ap()
    qdbg_d = nc.dram_tensor("qdbg", [128, 9 * 512], BF16, kind="ExternalOutput").ap()
    odbg_d = nc.dram_tensor("odbg", [128, 4 * 640], BF16, kind="ExternalOutput").ap()
    vdbg_d = nc.dram_tensor("vdbg", [128, 9 * 520], BF16, kind="ExternalOutput").ap()
    udbg_d = nc.dram_tensor("udbg", [128, 512], BF16, kind="ExternalOutput").ap()
    ldbg_d = nc.dram_tensor("ldbg", [128, 8 * 512], BF16, kind="ExternalOutput").ap()
    hdbg_d = nc.dram_tensor("hdbg", [128, 22 * 512], BF16, kind="ExternalOutput").ap()
    ddbg_d = nc.dram_tensor("ddbg", [1, 640], F32, kind="ExternalOutput").ap()
    mdbg_d = nc.dram_tensor("mdbg", [128, 640], BF16, kind="ExternalOutput").ap()
    es = ExitStack()
    P = Prog(nc)
    ctx = Ctx(nc, P, es)
    with es:
        wqb = ctx.sbuf("wqb", [128, 8 * 512], BF16)
        wkb = ctx.sbuf("wkb", [128, 8 * 512], BF16)
        wvb = ctx.sbuf("wvb", [128, 8 * 512], BF16)
        wob = ctx.sbuf("wob", [128, 4 * 1024], BF16)
        big = ctx.sbuf("big", [128, 8192], F32)
        wtmp = big[:][:, 0:4096]
        qs_b = ctx.sbuf("qs_b", [128, 64], F32)
        ks_b = ctx.sbuf("ks_b", [128, 64], F32)
        if_b = ctx.sbuf("if_b", [128, 32], F32)
        lnb_b = ctx.sbuf("lnb_b", [128, 32], F32)
        posv = ctx.sbuf("posv", [128, 36], F32)
        vmv = ctx.sbuf("vmv", [128, 36], F32)
        b1v = ctx.sbuf("b1v", [128, 44], F32)
        b2v = ctx.sbuf("b2v", [128, 8], F32)
        lgv = ctx.sbuf("lgv", [128, 8], F32)
        lbv = ctx.sbuf("lbv", [128, 8], F32)
        ones1 = ctx.sbuf("ones1", [128, 1], F32)
        onesr = ctx.sbuf("onesr", [1, 128], F32)
        ones64x = ctx.sbuf("ones64x", [1, 64], F32)
        cexp = ctx.sbuf("cexp", [128, 1], F32)
        xr = [big[:][:, 0:4096], big[:][:, 4096:8192]]
        srow = ctx.sbuf("srow", [1, 1536], F32)
        sq = ctx.sbuf("sq", [128, 640], F32)
        xmln = ctx.sbuf("xmln", [128, 4 * 512], BF16)
        xc0 = ctx.sbuf("xc0", [128, 8 * PW], BF16)
        qtm = ctx.sbuf("qtm", [128, 9 * 512], BF16)
        ktm = ctx.sbuf("ktm", [128, 9 * 512], BF16)
        vtm = ctx.sbuf("vtm", [128, 9 * 520], BF16)
        qfm = ctx.sbuf("qfm", [128, 4 * 640], BF16)
        kfm = ctx.sbuf("kfm", [128, 4 * PW], BF16)
        cs = ctx.sbuf("cs", [128, 8 * 32], F32)
        csq = ctx.sbuf("csq", [128, 4 * 64], F32)
        rt1 = ctx.sbuf("rt1", [128, 640], F32)
        rt2 = ctx.sbuf("rt2", [128, 640], F32)
        mstr = ctx.sbuf("mstr", [128, 9 * 640], BF16)
        mrow = ctx.sbuf("mrow", [128, 640], BF16)
        ub = [ctx.sbuf(f"ub{i}", [128, 512], BF16) for i in range(2)]
        o_fm = ctx.sbuf("o_fm", [128, 4 * 640], BF16)
        rden = ctx.sbuf("rden", [1, 640], F32)
        bcden = ctx.sbuf("bcden", [128, 512], F32)
        xc1 = ctx.sbuf("xc1", [128, 8 * 640], F32)
        ln2 = ctx.sbuf("ln2", [128, 8 * 512], BF16)
        w1b = [ctx.sbuf(f"w1b{i}", [128, 8 * 256], BF16) for i in range(2)]
        h4 = big[:].bitcast(BF16)[:, 0:11264]
        w2b = [qtm[:][:, 0:1024], qtm[:][:, 1024:2048]]
        ktmf = ktm[:].bitcast(F32)
        _oa = [kfm[:].bitcast(F32), vtm[:].bitcast(F32)]
        def out_acc8_sl(oc):
            return _oa[oc // 4][:, (oc % 4)*512:(oc % 4 + 1)*512]
        ystg = [ktmf[:, 0:512], ktmf[:, 512:1024]]
        xor_ = [ktmf[:, 1024:1536], ktmf[:, 1536:2048]]
        ctx.identities()
        B = [None] * 8
        B[0] = ctx.psum("bank0", [128, 512], F32)
        B[1] = ctx.psum("bank1", [128, 512], F32)
        B[2] = ctx.psum("bank2", [128, 512], F32)
        B[3] = ctx.psum("bank3", [128, 512], F32)
        B[4] = ctx.psum("bank4", [128, 512], F32)
        B[5] = ctx.psum("bank5", [128, 1024], BF16)
        B[6] = ctx.psum("bank6", [128, 512], F32)
        B[7] = ctx.psum("bank7", [128, 512], F32)
        K = [f"bank{i}" for i in range(8)]
        IC2 = [(0, 512), (512, 128)]

        P.emit("pool", lambda g: g.memset(ones1[:], 1.0), (), ("ones1",), tag="o1")
        P.emit("pool", lambda g: g.memset(onesr[:], 1.0), (), ("onesr",), tag="o2")
        P.emit("pool", lambda g: g.memset(cexp[:], -10.0), (), ("cexp",), tag="o3")
        P.emit("pool", lambda g: g.memset(ones64x[:], 1.0), (), ("ones64x",), tag="o4")


        def ldcast(dst_bf, src_d, key, nch=8, ncols=512, scale=None):
            wv = wtmp[:, 0:nch*ncols].rearrange("p (c d) -> p c d", c=nch)
            P.dma(lambda sp: sp.dma_start(
                out=wv, in_=src_d.rearrange("(c p) d -> p c d", p=128)),
                ("wtmp_f",), ("wtmp",), tag="ld_" + key)
            if scale is None:
                P.emit("dve", lambda v: v.tensor_copy(
                    dst_bf[:].rearrange("p (c d) -> p c d", c=nch), wv),
                    ("wtmp",), (key, "wtmp_f"), tag="cast_" + key)
            else:
                P.emit("dve", lambda v: v.tensor_scalar(
                    out=dst_bf[:].rearrange("p (c d) -> p c d", c=nch),
                    in0=wv, scalar1=scale, scalar2=None, op0=ALU.mult),
                    ("wtmp",), (key, "wtmp_f"), tag="cast_" + key)
        ldcast(wqb, wq_d, "wqb")
        ldcast(wkb, wkb_src := wk_d, "wkb")
        ldcast(wvb, wv_d, "wvb")
        ldcast(wob, wo_d, "wob", nch=4, ncols=1024)
        for nm, d_, dst, w, sc in (("qs", qs_d, qs_b, 64, 8.0), ("ks", ks_d, ks_b, 64, None),
                                   ("if", if_d, if_b, 32, None), ("lnb", lb_d, lnb_b, 32, None)):
            P.dma(lambda sp, d_=d_, w=w: sp.dma_start(out=srow[0:1, 0:w], in_=d_),
                  ("srow_f",), ("srow",), tag="ld_" + nm)
            P.emit("pe", lambda t, w=w: t.matmul(B[4][:, 0:w], onesr[0:1, :],
                                                 srow[0:1, 0:w], start=True, stop=True),
                   ("srow", "onesr"), (K[4],), tag="bc_" + nm)
            if sc is None:
                P.emit("act", lambda s, dst=dst, w=w: s.copy(dst[:], B[4][:, 0:w]),
                       (K[4],), (nm + "_b", "srow_f"), tag="cp_" + nm)
            else:
                P.emit("act", lambda s, dst=dst, w=w, sc=sc: s.activation(
                    out=dst[:], in_=B[4][:, 0:w], func=AF.Copy, scale=sc),
                    (K[4],), (nm + "_b", "srow_f"), tag="cp_" + nm)
        for d_, dst, key in ((pos_d, posv, "posv"), (vm_d, vmv, "vmv"),
                             (b1_d, b1v, "b1v"), (b2_d, b2v, "b2v"),
                             (lg_d, lgv, "lgv"), (lbias_d, lbv, "lbv")):
            P.dma(lambda sp, d_=d_, dst=dst: sp.dma_start(out=dst[:], in_=d_),
                  (), (key,), tag="ld_misc")

        for jt in range(9):
            P.emit("pool", lambda g, jt=jt: g.iota(
                rt1[:], pattern=[[1, 640]], base=512 - jt*128, channel_multiplier=-1,
                allow_small_or_imprecise_dtypes=True), (), ("rt1",), tag="ms_iota")
            P.emit("dve", lambda v: v.tensor_scalar(
                out=rt2[:], in0=rt1[:], scalar1=0.0, scalar2=None, op0=ALU.is_ge),
                ("rt1",), ("rt2",), tag="ms_ge")
            P.emit("dve", lambda v: v.tensor_scalar(
                out=rt1[:], in0=rt1[:], scalar1=512.0, scalar2=None, op0=ALU.is_le),
                ("rt1",), ("rt1",), tag="ms_le")
            P.emit("dve", lambda v, jt=jt: v.tensor_tensor(
                out=mstr[:, jt*640:(jt+1)*640], in0=rt1[:], in1=rt2[:], op=ALU.mult),
                ("rt1", "rt2"), ("mstr",), tag="ms_and")

        RR = 1.0 / (2 * math.pi)
        MAGIC = 12582912.0

        def range_reduce_sin(dst, src, keys):
            t = cs[:, 224:256]
            P.emit("dve", lambda v: v.tensor_scalar(
                out=t, in0=src, scalar1=RR, scalar2=MAGIC, op0=ALU.mult, op1=ALU.add),
                keys, ("cs_t",), tag="rr1")
            P.emit("dve", lambda v: v.tensor_scalar(
                out=t, in0=t, scalar1=MAGIC, scalar2=None, op0=ALU.subtract),
                ("cs_t",), ("cs_t",), tag="rr2")
            P.emit("dve", lambda v: v.scalar_tensor_tensor(
                out=t, in0=t, scalar=-2 * math.pi, in1=src, op0=ALU.mult, op1=ALU.add),
                ("cs_t",) + keys, ("cs_t",), tag="rr3")
            P.emit("dve", lambda v: v.tensor_scalar(
                out=t, in0=t, scalar1=math.pi * 0.999999, scalar2=-math.pi * 0.999999,
                op0=ALU.min, op1=ALU.max), ("cs_t",), ("cs_t",), tag="rr4")
            P.emit("act", lambda s: s.activation(out=dst, in_=t, func=AF.Sin),
                   ("cs_t",), ("cs",), tag="rr_sin")

        cdst = big[:].bitcast(BF16)[:, 8192:12288]
        for i in range(11):
            P.dma(lambda sp, i=i: sp.dma_start(
                out=wtmp.rearrange("p (c d) -> p c d", c=8),
                in_=w1_d[:, i*512:(i+1)*512].rearrange("(c p) d -> p c d", p=128)),
                ("wtmp_f",), ("wtmp",), tag="w1s_ld")
            P.emit("dve", lambda v: v.tensor_copy(cdst, wtmp),
                   ("wtmp",), ("cdst", "wtmp_f"), tag="w1s_cast")
            P.dma(lambda sp, i=i: sp.dma_start(
                out=w1bd[:, i*512:(i+1)*512].rearrange("(c p) d -> p c d", p=128),
                in_=cdst.rearrange("p (c d) -> p c d", c=8)),
                ("cdst",), ("w1bd", "cdst_f"), tag="w1s_st")
            P.emit("dve", lambda v: v.engine_nop(), ("cdst_f",), ("cdst",), tag="w1s_free")
        for i in range(11):
            P.dma(lambda sp, i=i: sp.dma_start(
                out=wtmp[:, 0:2048].rearrange("p (c d) -> p c d", c=2),
                in_=w2_d[i*256:(i+1)*256, :].rearrange("(c p) d -> p c d", p=128)),
                ("wtmp_f",), ("wtmp",), tag="w2s_ld")
            P.emit("dve", lambda v: v.tensor_copy(cdst[:, 0:2048], wtmp[:, 0:2048]),
                   ("wtmp",), ("cdst", "wtmp_f"), tag="w2s_cast")
            P.dma(lambda sp, i=i: sp.dma_start(
                out=w2bd[i*256:(i+1)*256, :].rearrange("(c p) d -> p c d", p=128),
                in_=cdst[:, 0:2048].rearrange("p (c d) -> p c d", c=2)),
                ("cdst",), ("w2bd", "cdst_f"), tag="w2s_st")
            P.emit("dve", lambda v: v.engine_nop(), ("cdst_f",), ("cdst",), tag="w2s_free")
        P.emit("pool", lambda g: g.engine_nop(), ("w1bd", "w2bd"),
               ("wtmp", "xr0_f", "xr1_f", "cdst"), tag="bridge0")
        for m in range(4):
            pc0 = m * PW
            P.emit("pool", lambda g: g.engine_nop(), (),
                   ("h4", "xr0_f", "xr1_f", "qtm", "ktm", "kfm", "vtm", "oacc8",
                    "ystg0", "ystg1", "xor0", "xor1", "w2bb0", "w2bb1"), tag="bridge1")
            P.emit("dve", lambda v: v.memset(
                vtm[:].rearrange("p (j h e) -> p j h e", h=8, e=65)[:, :, :, 64:65], 1.0),
                (), ("v_ones", "vtm"), tag="v_ones")
            for c3, (coff, cw) in enumerate(((0, 512), (512, 512), (1024, 128))):
                ntile = cw // 128
                for src_i, x_dram in ((0, xp_d), (1, xm_d)):
                    r = src_i
                    P.dma(lambda sp, x_dram=x_dram, pc0=pc0, coff=coff, cw=cw, r=r: sp.dma_start(
                        out=xr[r].rearrange("p (c t) -> p c t", c=8)[:, :, 0:cw],
                        in_=x_dram[:, pc0+coff:pc0+coff+cw].rearrange("(c p) t -> p c t", p=128)),
                        (f"xr{r}_f",), (f"xr{r}",), tag="p1_ld")
                    dchunks = range(8) if src_i == 0 else range(4, 8)
                    if first:
                        mu = srow[0:1, 0:512]; rs = srow[0:1, 512:1024]; tmp = srow[0:1, 1024:1536]
                        for dc in range(8):
                            P.emit("pe", lambda t, dc=dc, r=r, cw=cw: t.matmul(
                                B[0][0:1, 0:cw], ones1[:], xr[r][:, dc*512: dc*512+cw],
                                start=(dc == 0), stop=(dc == 7)),
                                (f"xr{r}", "ones1"), (K[0],), tag="p1_sum")
                        for dc in range(8):
                            P.emit("act", lambda s, dc=dc, r=r, cw=cw: s.activation(
                                out=sq[:, 0:cw], in_=xr[r][:, dc*512: dc*512+cw],
                                func=AF.Square), (f"xr{r}",), ("sq",), tag="p1_sq")
                            P.emit("pe", lambda t, dc=dc, cw=cw: t.matmul(
                                B[1][0:1, 0:cw], ones1[:], sq[:, 0:cw],
                                start=(dc == 0), stop=(dc == 7)),
                                ("sq", "ones1"), (K[1],), tag="p1_sumsq")
                        P.emit("dve", lambda v, cw=cw: v.tensor_scalar(
                            out=mu[:, 0:cw], in0=B[0][0:1, 0:cw], scalar1=1.0/1024,
                            scalar2=None, op0=ALU.mult), (K[0],), ("srow",), tag="p1_mu")
                        P.emit("dve", lambda v, cw=cw: v.tensor_scalar(
                            out=rs[:, 0:cw], in0=B[1][0:1, 0:cw], scalar1=1.0/1024,
                            scalar2=None, op0=ALU.mult), (K[1], "srow"), ("srow",), tag="p1_ms")
                        P.emit("dve", lambda v, cw=cw: v.tensor_tensor(
                            out=tmp[:, 0:cw], in0=mu[:, 0:cw], in1=mu[:, 0:cw],
                            op=ALU.mult), ("srow",), ("srow",), tag="p1_mu2")
                        P.emit("dve", lambda v, cw=cw: v.tensor_tensor(
                            out=rs[:, 0:cw], in0=rs[:, 0:cw], in1=tmp[:, 0:cw],
                            op=ALU.subtract), ("srow",), ("srow",), tag="p1_var")
                        P.emit("dve", lambda v, cw=cw: v.tensor_scalar(
                            out=rs[:, 0:cw], in0=rs[:, 0:cw], scalar1=1e-5,
                            scalar2=None, op0=ALU.add), ("srow",), ("srow",), tag="p1_eps")
                        P.emit("act", lambda s, cw=cw: s.activation(
                            out=rs[:, 0:cw], in_=rs[:, 0:cw], func=AF.Sqrt),
                            ("srow",), ("srow",), tag="p1_sqrt")
                        P.emit("dve", lambda v, cw=cw: v.reciprocal(
                            out=rs[:, 0:cw], in_=rs[:, 0:cw]), ("srow",), ("srow",), tag="p1_rec")
                        P.emit("pe", lambda t, cw=cw: t.matmul(
                            B[2][:, 0:cw], onesr[0:1, :], mu[:, 0:cw],
                            start=True, stop=True), ("srow", "onesr"), (K[2],), tag="p1_bmu")
                        P.emit("pe", lambda t, cw=cw: t.matmul(
                            B[3][:, 0:cw], onesr[0:1, :], rs[:, 0:cw],
                            start=True, stop=True), ("srow", "onesr"), (K[3],), tag="p1_brs")
                        for dc in dchunks:
                            dst = (xc0[:, dc*PW + coff: dc*PW + coff + cw] if src_i == 0
                                   else xmln[:, (dc-4)*512: (dc-4)*512 + cw])
                            dkey = "xc0" if src_i == 0 else "xmln"
                            P.emit("dve", lambda v, dc=dc, r=r, cw=cw: v.tensor_tensor(
                                out=sq[:, 0:cw], in0=xr[r][:, dc*512: dc*512+cw],
                                in1=B[2][:, 0:cw], op=ALU.subtract),
                                (f"xr{r}", K[2]), ("sq",), tag="p1_sub")
                            P.emit("dve", lambda v, dst=dst, cw=cw: v.tensor_tensor(
                                out=dst, in0=sq[:, 0:cw], in1=B[3][:, 0:cw], op=ALU.mult),
                                ("sq", K[3]), (dkey,), tag="p1_mul")
                    else:
                        for dc in dchunks:
                            dst = (xc0[:, dc*PW + coff: dc*PW + coff + cw] if src_i == 0
                                   else xmln[:, (dc-4)*512: (dc-4)*512 + cw])
                            dkey = "xc0" if src_i == 0 else "xmln"
                            P.emit("dve", lambda v, dc=dc, r=r, dst=dst, cw=cw: v.tensor_copy(
                                dst, xr[r][:, dc*512: dc*512+cw]),
                                (f"xr{r}",), (dkey,), tag="p1_cp")
                    P.emit("pool", lambda g: g.engine_nop(), (f"xr{r}",), (f"xr{r}_f",),
                           tag="p1_free")
                for ti in range(ntile):
                    gt = (coff // 128) + ti
                    streams = [("k", wkb, "wkb", ktm, "ktm"), ("v", wvb, "wvb", vtm, "vtm")]
                    if gt >= 4:
                        streams.append(("q", wqb, "wqb", qtm, "qtm"))
                    for snm, wsb, wkey, dtm, dkey in streams:
                        bk = 6 if snm != "v" else 7
                        for dc in range(8):
                            src = (xc0[:, dc*PW + coff + ti*128: dc*PW + coff + (ti+1)*128]
                                   if dc < 4 else
                                   xmln[:, (dc-4)*512 + ti*128: (dc-4)*512 + (ti+1)*128])
                            skey = "xc0" if dc < 4 else "xmln"
                            P.emit("pe", lambda t, src=src, dc=dc, wsb=wsb, bk=bk: t.matmul(
                                B[bk][:], src, wsb[:, dc*512:(dc+1)*512],
                                start=(dc == 0), stop=(dc == 7)),
                                (skey, wkey), (K[bk],), tag="p1_qkv")
                        if snm == "v":
                            vv = vtm[:].rearrange("p (j h e) -> p j h e", h=8, e=65)
                            P.emit("act", lambda s, gt=gt, bk=bk, vv=vv: s.copy(
                                vv[:, gt, :, 0:64],
                                B[bk][:].rearrange("p (h e) -> p h e", h=8)),
                                (K[bk],), ("vtm",), tag="p1_vevac")
                        else:
                            P.emit("act", lambda s, gt=gt, dtm=dtm, bk=bk: s.copy(
                                dtm[:, gt*512:(gt+1)*512], B[bk][:]),
                                (K[bk],), (dkey,), tag="p1_evac")
            for gt in range(9):
                col = m * 9 + gt
                f = cs[:, 0:32]; sn = cs[:, 32:64]; co = cs[:, 64:96]
                sc_ = cs[:, 96:128]; isc = cs[:, 128:160]; arg = cs[:, 160:192]
                f2 = cs[:, 192:224]
                P.emit("dve", lambda v, col=col: v.tensor_scalar(
                    out=f, in0=if_b[:], scalar1=posv[:, col:col+1], scalar2=None,
                    op0=ALU.mult), ("if_b", "posv"), ("cs",), tag="r_f")
                range_reduce_sin(sn, f, ("cs",))
                P.emit("dve", lambda v: v.tensor_scalar(
                    out=f2, in0=f, scalar1=math.pi/2, scalar2=None, op0=ALU.add),
                    ("cs",), ("cs",), tag="r_f2")
                range_reduce_sin(co, f2, ("cs",))
                P.emit("dve", lambda v, col=col: v.tensor_scalar(
                    out=arg[:, 0:1], in0=posv[:, col:col+1], scalar1=-2048.0,
                    scalar2=1.0/256.0, op0=ALU.add, op1=ALU.mult),
                    ("posv",), ("cs_pw",), tag="r_pw")
                P.emit("dve", lambda v: v.tensor_scalar(
                    out=arg, in0=lnb_b[:], scalar1=arg[:, 0:1], scalar2=None,
                    op0=ALU.mult), ("lnb_b", "cs_pw"), ("cs",), tag="r_arg")
                P.emit("act", lambda s: s.activation(out=sc_, in_=arg, func=AF.Exp),
                       ("cs",), ("cs",), tag="r_sc")
                P.emit("act", lambda s: s.activation(out=isc, in_=arg, func=AF.Exp,
                                                     scale=-1.0),
                       ("cs",), ("cs",), tag="r_isc")
                for half in range(2):
                    hs = slice(half*32, (half+1)*32)
                    P.emit("dve", lambda v, hs=hs: v.tensor_tensor(
                        out=rt1[:, 0:32], in0=co, in1=sc_, op=ALU.mult),
                        ("cs",), ("rt1",), tag="r_cq0")
                    P.emit("dve", lambda v, hs=hs: v.tensor_tensor(
                        out=csq[:, 0:64][:, hs], in0=rt1[:, 0:32], in1=qs_b[:, hs],
                        op=ALU.mult), ("rt1", "qs_b"), ("csq",), tag="r_cq1")
                    P.emit("dve", lambda v, hs=hs: v.tensor_tensor(
                        out=rt1[:, 0:32], in0=sn, in1=sc_, op=ALU.mult),
                        ("cs",), ("rt1",), tag="r_cq2")
                    hs2 = slice((half*32 + 32) % 64, (half*32 + 32) % 64 + 32)
                    P.emit("dve", lambda v, hs=hs, hs2=hs2: v.tensor_tensor(
                        out=csq[:, 64:128][:, hs], in0=rt1[:, 0:32], in1=qs_b[:, hs2],
                        op=ALU.mult), ("rt1", "qs_b"), ("csq",), tag="r_cq3")
                    P.emit("dve", lambda v, hs=hs: v.tensor_tensor(
                        out=rt1[:, 0:32], in0=co, in1=isc, op=ALU.mult),
                        ("cs",), ("rt1",), tag="r_ck0")
                    P.emit("dve", lambda v, hs=hs: v.tensor_tensor(
                        out=csq[:, 128:192][:, hs], in0=rt1[:, 0:32], in1=ks_b[:, hs],
                        op=ALU.mult), ("rt1", "ks_b"), ("csq",), tag="r_ck1")
                    P.emit("dve", lambda v, hs=hs: v.tensor_tensor(
                        out=rt1[:, 0:32], in0=sn, in1=isc, op=ALU.mult),
                        ("cs",), ("rt1",), tag="r_ck2")
                    P.emit("dve", lambda v, hs=hs, hs2=hs2: v.tensor_tensor(
                        out=csq[:, 192:256][:, hs], in0=rt1[:, 0:32], in1=ks_b[:, hs2],
                        op=ALU.mult), ("rt1", "ks_b"), ("csq",), tag="r_ck3")
                strs = [("k", ktm, "ktm", 128, 192)]
                if gt >= 4:
                    strs.append(("q", qtm, "qtm", 0, 64))
                for snm, dtm, dkey, c1off, c2off in strs:
                    tile = dtm[:, gt*512:(gt+1)*512]
                    t3 = tile.rearrange("p (h d) -> p h d", h=8)
                    P.emit("act", lambda s, tile=tile: s.activation(
                        out=rt1[:, 0:512], in_=tile, func=AF.Square),
                        (dkey,), ("rt1",), tag="n_sq")
                    P.emit("dve", lambda v: v.tensor_reduce(
                        out=rt2[:, 0:8], in_=rt1[:, 0:512].rearrange("p (h d) -> p h d", h=8),
                        axis=AX.X, op=ALU.add), ("rt1",), ("rt2",), tag="n_ssq")
                    P.emit("act", lambda s: s.activation(out=rt2[:, 0:8], in_=rt2[:, 0:8],
                                                         func=AF.Sqrt),
                           ("rt2",), ("rt2",), tag="n_sqrt")
                    P.emit("dve", lambda v: v.tensor_scalar(
                        out=rt2[:, 0:8], in0=rt2[:, 0:8], scalar1=1e-12, scalar2=None,
                        op0=ALU.max), ("rt2",), ("rt2",), tag="n_clip")
                    P.emit("dve", lambda v: v.reciprocal(out=rt2[:, 0:8], in_=rt2[:, 0:8]),
                           ("rt2",), ("rt2",), tag="n_rec")
                    rnb = rt2[:, 0:8].rearrange("p (h o) -> p h o", o=1).to_broadcast([128, 8, 64])
                    P.emit("dve", lambda v, t3=t3, rnb=rnb: v.tensor_tensor(
                        out=rt1[:, 0:512].rearrange("p (h d) -> p h d", h=8),
                        in0=t3, in1=rnb, op=ALU.mult),
                        (dkey, "rt2"), ("rt1",), tag="n_hat")
                    r3o = rt2[:, 0:512].rearrange("p (h d) -> p h d", h=8)
                    n3 = rt1[:, 0:512].rearrange("p (h d) -> p h d", h=8)
                    P.emit("dve", lambda v, r3o=r3o, n3=n3: v.tensor_scalar(
                        out=r3o[:, :, 0:32], in0=n3[:, :, 32:64], scalar1=-1.0,
                        scalar2=None, op0=ALU.mult), ("rt1",), ("rt2",), tag="n_rh1")
                    P.emit("dve", lambda v, r3o=r3o, n3=n3: v.tensor_copy(
                        r3o[:, :, 32:64], n3[:, :, 0:32]), ("rt1",), ("rt2",), tag="n_rh2")
                    cs1b = csq[:, c1off:c1off+64].rearrange("p (o d) -> p o d", o=1).to_broadcast([128, 8, 64])
                    cs2b = csq[:, c2off:c2off+64].rearrange("p (o d) -> p o d", o=1).to_broadcast([128, 8, 64])
                    P.emit("dve", lambda v, n3=n3, cs1b=cs1b: v.tensor_tensor(
                        out=n3, in0=n3, in1=cs1b, op=ALU.mult),
                        ("rt1", "csq"), ("rt1",), tag="n_t1")
                    P.emit("dve", lambda v, r3o=r3o, cs2b=cs2b: v.tensor_tensor(
                        out=r3o, in0=r3o, in1=cs2b, op=ALU.mult),
                        ("rt2", "csq"), ("rt2",), tag="n_t2")
                    P.emit("dve", lambda v, tile=tile: v.tensor_tensor(
                        out=tile, in0=rt1[:, 0:512], in1=rt2[:, 0:512], op=ALU.add),
                        ("rt1", "rt2"), (dkey,), tag="n_out")
            if m == 1:
                P.dma(lambda sp: sp.dma_start(out=kdbg_d, in_=ktm[:]),
                      ("ktm",), ("kdbg",), tag="dbg_k")
                P.dma(lambda sp: sp.dma_start(out=qdbg_d, in_=qtm[:]),
                      ("qtm",), ("qdbg",), tag="dbg_q")
            for gt in range(9):
                for fc in range(4):
                    emit_transpose(ctx, B[5][:, fc*128:(fc+1)*128],
                                   ktm[:, gt*512 + fc*128: gt*512 + (fc+1)*128],
                                   ("ktm",), (K[5],), BF16)
                P.emit("act", lambda s, gt=gt: s.copy(
                    kfm[:].rearrange("p (c j) -> p c j", c=4)[:, :, gt*128:(gt+1)*128],
                    B[5][:, 0:512].rearrange("p (c j) -> p c j", c=4)),
                    (K[5],), ("kfm",), tag="t_kevac")
            for qi in range(5):
                gt = 4 + qi
                for fc in range(4):
                    emit_transpose(ctx, B[5][:, fc*128:(fc+1)*128],
                                   qtm[:, gt*512 + fc*128: gt*512 + (fc+1)*128],
                                   ("qtm",), (K[5],), BF16)
                P.emit("act", lambda s, qi=qi: s.copy(
                    qfm[:].rearrange("p (c j) -> p c j", c=4)[:, :, qi*128:(qi+1)*128],
                    B[5][:, 0:512].rearrange("p (c j) -> p c j", c=4)),
                    (K[5],), ("qfm",), tag="t_qevac")
            for h in range(8):
                c = h // 2
                ro = (h % 2) * 64
                pav = [2, 3]
                for jt in range(9):
                    P.emit("dve", lambda v, jt=jt, m=m: v.tensor_scalar(
                        out=mrow[:], in0=mstr[:, jt*640:(jt+1)*640],
                        scalar1=vmv[:, m*9+jt: m*9+jt+1], scalar2=None, op0=ALU.mult),
                        ("mstr", "vmv"), ("mrow",), tag="a_mrow")
                    for i2, (io, iw) in enumerate(IC2):
                        bs = i2
                        P.emit("pe", lambda t, jt=jt, c=c, ro=ro, io=io, iw=iw, bs=bs: t.matmul(
                            B[bs][:, 0:iw], kfm[ro:ro+64, c*PW + jt*128: c*PW + (jt+1)*128],
                            qfm[ro:ro+64, c*640 + io: c*640 + io + iw],
                            start=True, stop=True),
                            ("kfm", "qfm"), (K[bs],), tag="a_sc")
                        P.emit("dve", lambda v, bs=bs, iw=iw: v.tensor_scalar(
                            out=rt1[:, 0:iw], in0=B[bs][:, 0:iw], scalar1=60.0,
                            scalar2=None, op0=ALU.min), (K[bs],), ("rt1",), tag="a_clamp")
                        P.emit("act", lambda s, bs=bs, iw=iw, i2=i2: s.activation(
                            out=ub[i2][:, 0:iw], in_=rt1[:, 0:iw], func=AF.Exp,
                            bias=cexp[:]), ("rt1", "cexp"), (f"ub{i2}",), tag="a_exp")
                        P.emit("dve", lambda v, i2=i2, io=io, iw=iw: v.tensor_tensor(
                            out=ub[i2][:, 0:iw], in0=ub[i2][:, 0:iw],
                            in1=mrow[:, io:io+iw], op=ALU.mult),
                            (f"ub{i2}", "mrow"), (f"ub{i2}",), tag="a_msk")
                        if m == 1 and h == 0 and jt == 6 and i2 == 0:
                            P.dma(lambda sp, i2=i2: sp.dma_start(
                                out=udbg_d, in_=ub[i2][:]), (f"ub{i2}",), ("udbg",), tag="dbg_u")
                            P.dma(lambda sp: sp.dma_start(
                                out=mdbg_d, in_=mrow[:]), ("mrow",), ("mdbg",), tag="dbg_m")
                        P.emit("pe", lambda t, jt=jt, h=h, i2=i2, iw=iw,
                               first_=(jt == 0), last_=(jt == 8): t.matmul(
                            B[pav[i2]][0:65, 0:iw],
                            vtm[:, jt*520 + h*65: jt*520 + (h+1)*65],
                            ub[i2][:, 0:iw], start=first_, stop=last_),
                            ("vtm", "v_ones", f"ub{i2}"), (K[pav[i2]],), tag="a_av")
                if m == 1 and h == 0:
                    P.emit("dve", lambda v: v.tensor_copy(rt2[:1, 0:512], B[pav[0]][64:65, 0:512]),
                           (K[pav[0]],), ("rt2",), tag="dbg_dcp")
                    P.dma(lambda sp: sp.dma_start(out=ddbg_d[0:1, 0:512], in_=rt2[0:1, 0:512]),
                          ("rt2",), ("ddbg",), tag="dbg_d")
                for i2, (io, iw) in enumerate(IC2):
                    P.emit("dve", lambda v, i2=i2, io=io, iw=iw: v.tensor_scalar(
                        out=rden[0:1, io:io+iw], in0=B[pav[i2]][64:65, 0:iw],
                        scalar1=1e-30, scalar2=None, op0=ALU.max),
                        (K[pav[i2]],), ("rden",), tag="a_dclamp")
                    P.emit("dve", lambda v, io=io, iw=iw: v.reciprocal(
                        out=rden[0:1, io:io+iw], in_=rden[0:1, io:io+iw]),
                        ("rden",), ("rden",), tag="a_rden")
                    P.emit("pe", lambda t, io=io, iw=iw: t.matmul(
                        B[4][0:64, 0:iw], ctx.sb["ones64x"][0:1, :],
                        rden[0:1, io:io+iw], start=True, stop=True),
                        ("rden", "ones64x"), (K[4],), tag="a_dbc")
                    P.emit("act", lambda s, iw=iw: s.copy(bcden[0:64, 0:iw],
                                                          B[4][0:64, 0:iw]),
                           (K[4],), ("bcden",), tag="a_dcp")
                    P.emit("dve", lambda v, i2=i2, io=io, iw=iw, c=c, ro=ro: v.tensor_tensor(
                        out=o_fm[ro:ro+64, c*640 + io: c*640 + io + iw],
                        in0=B[pav[i2]][0:64, 0:iw], in1=bcden[0:64, 0:iw], op=ALU.mult),
                        (K[pav[i2]], "bcden"), ("o_fm",), tag="a_onorm")
            if m == 1:
                P.dma(lambda sp: sp.dma_start(out=odbg_d, in_=o_fm[:]),
                      ("o_fm",), ("odbg",), tag="dbg_o")
                P.dma(lambda sp: sp.dma_start(out=vdbg_d, in_=vtm[:]),
                      ("vtm", "v_ones"), ("vdbg",), tag="dbg_v")
            for dc in range(8):
                for i2, (io, iw) in enumerate(IC2):
                    bo = i2
                    for c4 in range(4):
                        P.emit("pe", lambda t, dc=dc, c4=c4, io=io, iw=iw, bo=bo: t.matmul(
                            B[bo][:, 0:iw], wob[:, c4*1024 + dc*128: c4*1024 + (dc+1)*128],
                            o_fm[:, c4*640 + io: c4*640 + io + iw],
                            start=(c4 == 0), stop=(c4 == 3)),
                            ("wob", "o_fm"), (K[bo],), tag="w_mm")
                    P.emit("dve", lambda v, dc=dc, io=io, iw=iw, bo=bo: v.tensor_tensor(
                        out=xc1[:, dc*640 + io: dc*640 + io + iw], in0=B[bo][:, 0:iw],
                        in1=xc0[:, dc*PW + 512 + io: dc*PW + 512 + io + iw], op=ALU.add),
                        (K[bo], "xc0"), ("xc1",), tag="w_xc1")
            mu = srow[0:1, 0:512]; rs = srow[0:1, 512:1024]; tmp = srow[0:1, 1024:1536]
            for dc in range(8):
                off = 128 if dc < 4 else 127
                P.emit("pe", lambda t, dc=dc, off=off: t.matmul(
                    B[2][0:1, :], ones1[:], xc1[:, dc*640 + off: dc*640 + off + 512],
                    start=(dc == 0), stop=(dc == 7)),
                    ("xc1", "ones1"), (K[2],), tag="f_sum")
            for dc in range(8):
                off = 128 if dc < 4 else 127
                P.emit("act", lambda s, dc=dc, off=off: s.activation(
                    out=sq[:, 0:512], in_=xc1[:, dc*640 + off: dc*640 + off + 512],
                    func=AF.Square), ("xc1",), ("sq",), tag="f_sq")
                P.emit("pe", lambda t, dc=dc: t.matmul(
                    B[3][0:1, :], ones1[:], sq[:, 0:512],
                    start=(dc == 0), stop=(dc == 7)), ("sq", "ones1"), (K[3],), tag="f_sumsq")
            P.emit("dve", lambda v: v.tensor_scalar(
                out=mu, in0=B[2][0:1, :], scalar1=1.0/1024, scalar2=None, op0=ALU.mult),
                (K[2],), ("srow",), tag="f_mu")
            P.emit("dve", lambda v: v.tensor_scalar(
                out=rs, in0=B[3][0:1, :], scalar1=1.0/1024, scalar2=None, op0=ALU.mult),
                (K[3], "srow"), ("srow",), tag="f_ms")
            P.emit("dve", lambda v: v.tensor_tensor(out=tmp, in0=mu, in1=mu, op=ALU.mult),
                   ("srow",), ("srow",), tag="f_mu2")
            P.emit("dve", lambda v: v.tensor_tensor(out=rs, in0=rs, in1=tmp, op=ALU.subtract),
                   ("srow",), ("srow",), tag="f_var")
            P.emit("dve", lambda v: v.tensor_scalar(out=rs, in0=rs, scalar1=1e-5,
                                                    scalar2=None, op0=ALU.add),
                   ("srow",), ("srow",), tag="f_eps")
            P.emit("act", lambda s: s.activation(out=rs, in_=rs, func=AF.Sqrt),
                   ("srow",), ("srow",), tag="f_sqrt")
            P.emit("dve", lambda v: v.reciprocal(out=rs, in_=rs), ("srow",), ("srow",), tag="f_rec")
            P.emit("pe", lambda t: t.matmul(B[2][:], onesr[0:1, :], mu, start=True,
                                            stop=True), ("srow", "onesr"), (K[2],), tag="f_bmu")
            P.emit("pe", lambda t: t.matmul(B[3][:], onesr[0:1, :], rs, start=True,
                                            stop=True), ("srow", "onesr"), (K[3],), tag="f_brs")
            for dc in range(8):
                off = 128 if dc < 4 else 127
                P.emit("dve", lambda v, dc=dc, off=off: v.tensor_tensor(
                    out=sq[:, 0:512], in0=xc1[:, dc*640 + off: dc*640 + off + 512],
                    in1=B[2][:], op=ALU.subtract), ("xc1", K[2]), ("sq",), tag="f_sub")
                P.emit("dve", lambda v, dc=dc: v.tensor_tensor(
                    out=sq[:, 0:512], in0=sq[:, 0:512], in1=B[3][:], op=ALU.mult),
                    ("sq", K[3]), ("sq",), tag="f_mulr")
                P.emit("dve", lambda v, dc=dc: v.tensor_scalar(
                    out=ln2[:, dc*512:(dc+1)*512], in0=sq[:, 0:512],
                    scalar1=lgv[:, dc:dc+1], scalar2=lbv[:, dc:dc+1],
                    op0=ALU.mult, op1=ALU.add),
                    ("sq", "lgv", "lbv"), ("ln2",), tag="f_gb")
            for dcq in range(8):
                P.dma(lambda sp, dcq=dcq, m=m: sp.dma_start(
                    out=ydbg_d[dcq*128:(dcq+1)*128, m*640:(m+1)*640],
                    in_=xc1[:, dcq*640:(dcq+1)*640]), ("xc1",), ("ydbg",), tag="dbg_st")
            P.emit("pool", lambda g: g.engine_nop(), (),
                   ("xr0", "xr1", "h4"), tag="bridge2")
            if m == 1:
                P.dma(lambda sp: sp.dma_start(out=ldbg_d, in_=ln2[:]),
                      ("ln2",), ("ldbg",), tag="dbg_ln2")
            for fc in range(22):
                r = fc % 2
                P.dma(lambda sp, fc=fc, r=r: sp.dma_start(
                    out=w1b[r][:].rearrange("p (c d) -> p c d", c=8),
                    in_=w1bd[:, fc*256:(fc+1)*256].rearrange("(c p) d -> p c d", p=128)),
                    (f"w1b{r}_f", "w1bd"), (f"w1b{r}",), tag="f_ldw1")
                for half, bk in ((0, 6), (1, 7)):
                    for dc in range(8):
                        P.emit("pe", lambda t, fc=fc, dc=dc, half=half, bk=bk, r=r: t.matmul(
                            B[bk][:], w1b[r][:, dc*256 + half*128: dc*256 + half*128 + 128],
                            ln2[:, dc*512:(dc+1)*512], start=(dc == 0), stop=(dc == 7)),
                            (f"w1b{r}", "ln2"), (K[bk],), tag="f_w1mm")
                P.emit("dve", lambda v, r=r: v.engine_nop(), (f"w1b{r}",), (f"w1b{r}_f",),
                       tag="f_w1free")
                P.emit("dve", lambda v, fc=fc: v.tensor_scalar(
                    out=rt1[:, 0:512], in0=B[7][:], scalar1=b1v[:, fc*2+1:fc*2+2],
                    scalar2=None, op0=ALU.add), (K[7], "b1v"), ("rt1",), tag="f_gb1")
                P.emit("act", lambda s: s.activation(out=rt2[:, 0:512], in_=rt1[:, 0:512],
                                                     func=AF.Gelu),
                       ("rt1",), ("rt2",), tag="f_gelu")
                P.emit("dve", lambda v, fc=fc: v.tensor_scalar(
                    out=rt1[:, 0:512], in0=B[6][:], scalar1=b1v[:, fc*2:fc*2+1],
                    scalar2=None, op0=ALU.add), (K[6], "b1v"), ("rt1",), tag="f_ab1")
                P.emit("dve", lambda v, fc=fc: v.tensor_tensor(
                    out=h4[:, fc*512:(fc+1)*512], in0=rt1[:, 0:512], in1=rt2[:, 0:512],
                    op=ALU.mult), ("rt1", "rt2"), ("h4",), tag="f_h4")
            if m == 1:
                P.dma(lambda sp: sp.dma_start(out=hdbg_d, in_=h4),
                      ("h4",), ("hdbg",), tag="dbg_h4")
            P.emit("pool", lambda g: g.engine_nop(), (),
                   ("qtm", "ktm", "w2bb0_f", "w2bb1_f", "ystg0", "ystg1",
                    "xor0_f", "xor1_f"), tag="bridge3")
            P.emit("pool", lambda g: g.engine_nop(), (), ("kfm", "vtm", "v_ones", "oacc8"),
                   tag="bridge4")
            for fc in range(22):
                r2 = fc % 2
                P.dma(lambda sp, fc=fc, r2=r2: sp.dma_start(
                    out=w2b[r2][:], in_=w2bd[fc*128:(fc+1)*128, :]),
                    (f"w2bb{r2}_f", "w2bd"), (f"w2bb{r2}",), tag="f_ldw2")
                for oc in range(8):
                    bo = oc % 2
                    P.emit("pe", lambda t, oc=oc, fc=fc, bo=bo, r2=r2: t.matmul(
                        B[bo][:], w2b[r2][:, oc*128:(oc+1)*128],
                        h4[:, fc*512:(fc+1)*512], start=True, stop=True),
                        (f"w2bb{r2}", "h4"), (K[bo],), tag="f_w2mm")
                    if fc == 0:
                        P.emit("dve", lambda v, oc=oc, bo=bo: v.tensor_copy(
                            out_acc8_sl(oc), B[bo][:]),
                            (K[bo],), ("oacc8",), tag="f_ocp")
                    else:
                        P.emit("dve", lambda v, oc=oc, bo=bo: v.tensor_tensor(
                            out=out_acc8_sl(oc),
                            in0=out_acc8_sl(oc), in1=B[bo][:],
                            op=ALU.add), ("oacc8", K[bo]), ("oacc8",), tag="f_oadd")
                P.emit("dve", lambda v, r2=r2: v.engine_nop(), (f"w2bb{r2}",),
                       (f"w2bb{r2}_f",), tag="f_w2f")
            for oc in range(8):
                pr = oc % 2
                if last:
                    P.dma(lambda sp, oc=oc, m=m, pr=pr: sp.dma_start(
                        out=xor_[pr][:], in_=xo_d[oc*128:(oc+1)*128, m*512:(m+1)*512]),
                        (f"xor{pr}_f",), (f"xor{pr}",), tag="f_ldxo")
                P.emit("dve", lambda v, oc=oc, pr=pr: v.scalar_tensor_tensor(
                    out=ystg[pr][:], in0=out_acc8_sl(oc),
                    scalar=b2v[:, oc:oc+1],
                    in1=xc1[:, oc*640 + 128: oc*640 + 640], op0=ALU.add, op1=ALU.add),
                    ("oacc8", "b2v", "xc1"), (f"ystg{pr}",), tag="f_res1")
                if last:
                    P.emit("dve", lambda v, pr=pr: v.tensor_tensor(
                        out=ystg[pr][:], in0=ystg[pr][:], in1=xor_[pr][:], op=ALU.add),
                        (f"ystg{pr}", f"xor{pr}"), (f"ystg{pr}", f"xor{pr}_f"), tag="f_res2")
                P.dma(lambda sp, oc=oc, m=m, pr=pr: sp.dma_start(
                    out=y_d[oc*128:(oc+1)*128, m*512:(m+1)*512], in_=ystg[pr][:]),
                    (f"ystg{pr}",), ("y_out",), tag="f_sty")
            P.emit("dve", lambda v: v.engine_nop(), ("w2bb0", "w2bb1"),
                   ("w2bb0_f", "w2bb1_f"), tag="f_w2free")
        P.finalize()
    return nc



def build_l1attn_proj():
    from contextlib import ExitStack
    nc = bass.Bass("TRN2", target_bir_lowering=False, debug=False)
    N = 4096
    x_d = nc.dram_tensor("x", [1024, N], F32, kind="ExternalInput").ap()
    wvt_d = nc.dram_tensor("wvt", [1024, 1024], F32, kind="ExternalInput").ap()
    w1t_d = nc.dram_tensor("w1t", [1024, 1024], F32, kind="ExternalInput").ap()
    w2t_d = nc.dram_tensor("w2t", [1024, 1024], F32, kind="ExternalInput").ap()
    f1_d = nc.dram_tensor("f1", [64, 256], F32, kind="ExternalInput").ap()
    fw1_d = nc.dram_tensor("fw1", [1, 64], F32, kind="ExternalInput").ap()
    fb1_d = nc.dram_tensor("fb1", [1, 64], F32, kind="ExternalInput").ap()
    fw2_d = nc.dram_tensor("fw2t", [64, 1024], F32, kind="ExternalInput").ap()
    fb2_d = nc.dram_tensor("fb2", [1, 1024], F32, kind="ExternalInput").ap()
    cv_d = nc.dram_tensor("cv", [64, 2, 128, 1024], BF16, kind="ExternalOutput").ap()
    ch_d = nc.dram_tensor("ch", [64, 2, 128, 1024], BF16, kind="ExternalOutput").ap()
    x1g_d = nc.dram_tensor("x1g", [64, 64, 1024], BF16, kind="ExternalOutput").ap()
    x2g_d = nc.dram_tensor("x2g", [64, 64, 1024], BF16, kind="ExternalOutput").ap()

    es = ExitStack()
    P = Prog(nc)
    ctx = Ctx(nc, P, es)
    with es:
        xring = [ctx.sbuf(f"xring{i}", [128, 8 * 512], F32) for i in range(2)]
        ln_fm = ctx.sbuf("ln_fm", [128, 8 * N], BF16)
        statT = ctx.sbuf("statT", [1, 2 * N + 512], F32)
        sq = ctx.sbuf("sq", [128, 512], F32)
        ones1 = ctx.sbuf("ones1", [128, 1], F32)
        onesr = ctx.sbuf("onesr", [1, 128], F32)
        wres = xring[0][:]
        wvb = ctx.sbuf("wvb", [128, 8 * 1024], BF16)
        w1b = ctx.sbuf("w1b", [128, 8 * 1024], BF16)
        w2b = ctx.sbuf("w2b", [128, 8 * 1024], BF16)
        f1b = ctx.sbuf("f1b", [64, 256], BF16)
        fw2b = ctx.sbuf("fw2b", [64, 1024], BF16)
        v1t = ctx.sbuf("v1t", [64, 4096], BF16)
        tbc = xring[1][:][0:128, 0:4096]
        cols = ctx.sbuf("cols", [64, 4], F32)
        fb2b = ctx.sbuf("fb2b", [64, 1024], F32)
        vsb = [ctx.sbuf(f"vsb{i}", [64, 1024], BF16) for i in range(2)]
        hsb = [ctx.sbuf(f"hsb{i}", [64, 1024], BF16) for i in range(2)]
        stg = [ctx.sbuf(f"stg{i}", [128, 512], BF16) for i in range(4)]
        misc = statT[0:1, 0:1024]
        one11 = statT[0:1, 1200:1201]
        ctx.identities()
        ps = {k: ctx.psum(f"ps_{k}", [128, 512], F32)
              for k in ("sum", "sumsq", "bc_mu", "bc_rs")}
        pmm = [ctx.psum(f"pmm{i}", [128, 512], F32) for i in range(4)]

        P.emit("pool", lambda g: g.memset(ones1[:], 1.0), (), ("ones1",), tag="o1")
        P.emit("pool", lambda g: g.memset(onesr[:], 1.0), (), ("onesr",), tag="o2")
        emit_ln_fm_stream(ctx, x_d, ln_fm, N, statT, sq[:], xring, ps, "ln_fm", "ln")
        P.emit("pool", lambda g: g.engine_nop(), (),
               ("ln_mu", "ln_rs", "ln_xr0", "ln_xr1", "miscrow", "wres_f", "tbc",
                "one11"), tag="bridgeA")
        P.emit("pool", lambda g: g.memset(one11[:], 1.0), ("one11",), ("one11",), tag="o3")

        def ldw(dst, src, key):
            for hh in range(2):
                P.dma(lambda sp, src=src, hh=hh: sp.dma_start(
                    out=wres[:, 0:4096].rearrange("p (c d) -> p c d", c=8),
                    in_=src[:, hh*512:(hh+1)*512].rearrange("(c p) d -> p c d", p=128)),
                    (key, "wres_f"), ("wres",), tag="ld_" + key)
                P.emit("dve", lambda v, dst=dst, hh=hh: v.tensor_copy(
                    dst[:].rearrange("p (c d) -> p c d", c=8)[:, :, hh*512:(hh+1)*512],
                    wres[:, 0:4096].rearrange("p (c d) -> p c d", c=8)),
                    ("wres",), (key, "wres_f"), tag="cast_" + key)
        ldw(wvb, wvt_d, "wvb")
        ldw(w1b, w1t_d, "w1b")
        ldw(w2b, w2t_d, "w2b")
        P.dma(lambda sp: sp.dma_start(out=wres[0:64, 0:256], in_=f1_d),
              ("wres_f",), ("wres",), tag="ld_f1")
        P.emit("dve", lambda v: v.tensor_copy(f1b[:], wres[0:64, 0:256]),
               ("wres",), ("f1b", "wres_f"), tag="cast_f1")
        P.dma(lambda sp: sp.dma_start(out=wres[0:64, 0:1024], in_=fw2_d),
              ("wres_f",), ("wres",), tag="ld_fw2")
        P.emit("dve", lambda v: v.tensor_copy(fw2b[:], wres[0:64, 0:1024]),
               ("wres",), ("fw2b", "wres_f"), tag="cast_fw2")
        P.dma(lambda sp: sp.dma_start(out=misc[:], in_=fb2_d), (), ("miscrow",), tag="ld_fb2")
        for cg in range(2):
            P.emit("pe", lambda t, cg=cg: t.matmul(
                pmm[3][0:64, :], onesr[0:1, 0:64], misc[0:1, cg*512:(cg+1)*512],
                start=True, stop=True), ("miscrow", "onesr"), ("pmm3",), tag="bc_fb2")
            P.emit("act", lambda s, cg=cg: s.copy(fb2b[:, cg*512:(cg+1)*512],
                                                  pmm[3][0:64, :]),
                   ("pmm3",), ("fb2b",), tag="cp_fb2")
        for ci, d_ in ((0, fw1_d), (1, fb1_d)):
            P.dma(lambda sp, d_=d_: sp.dma_start(out=misc[0:1, 0:64], in_=d_),
                  ("miscrow",), ("miscrow",), tag="ld_f1c")
            P.emit("pe", lambda t: t.matmul(pmm[3][0:64, 0:1], misc[0:1, 0:64],
                                            one11[:], start=True, stop=True),
                   ("miscrow", "one11"), ("pmm3",), tag="t_f1c")
            P.emit("act", lambda s, ci=ci: s.copy(cols[:, ci:ci+1], pmm[3][0:64, 0:1]),
                   ("pmm3",), ("cols",), tag="cp_f1c")
        P.emit("pool", lambda g: g.iota(misc[0:1, 0:1024].rearrange("o (a b) -> o a b", a=64)
                                        if False else misc[0:1, 0:1024],
                                        pattern=[[1, 1024]], base=0, channel_multiplier=0,
                                        allow_small_or_imprecise_dtypes=True),
               ("miscrow",), ("miscrow",), tag="t_iota0")
        for q in range(4):
            P.emit("pool", lambda g, q=q: g.iota(
                tbc[0:1, q*1024:(q+1)*1024].rearrange("o (a b) -> o a b", a=16),
                pattern=[[1, 16], [64, 64]], base=q*16, channel_multiplier=0,
                allow_small_or_imprecise_dtypes=True),
                (), ("tbc",), tag="t_iota")
        P.emit("dve", lambda v: v.tensor_scalar(
            out=tbc[0:1, :], in0=tbc[0:1, :], scalar1=1.0/4095.0, scalar2=None,
            op0=ALU.mult), ("tbc",), ("tbc",), tag="t_scale")
        for cg in range(8):
            P.emit("pe", lambda t, cg=cg: t.matmul(
                pmm[3][0:64, :], onesr[0:1, 0:64], tbc[0:1, cg*512:(cg+1)*512],
                start=True, stop=True), ("tbc", "onesr"), ("pmm3",), tag="t_bc")
            P.emit("act", lambda s, cg=cg: s.copy(tbc[0:64, cg*512:(cg+1)*512],
                                                  pmm[3][0:64, :]),
                   ("pmm3",), ("tbc",), tag="t_bccp")
        P.emit("dve", lambda v: v.tensor_scalar(
            out=tbc[0:64, :], in0=tbc[0:64, :], scalar1=cols[:, 0:1], scalar2=cols[:, 1:2],
            op0=ALU.mult, op1=ALU.add), ("tbc", "cols"), ("tbc",), tag="v1_ts")
        P.emit("act", lambda s: s.activation(out=v1t[:], in_=tbc[0:64, :], func=AF.Silu),
               ("tbc",), ("v1t",), tag="v1_silu")

        for g in range(64):
            r = g % 2
            tok = ln_fm[:].rearrange("p (c t) -> p c t", c=8)
            for cg in range(2):
                pm = cg % 2
                for dc in range(8):
                    P.emit("pe", lambda t, g=g, dc=dc, cg=cg, pm=pm: t.matmul(
                        pmm[pm][0:64, :], tok[:, dc, g::64][:, 0:64],
                        wvb[:, dc*1024 + cg*512: dc*1024 + (cg+1)*512],
                        start=(dc == 0), stop=(dc == 7)),
                        ("ln_fm", "wvb"), (f"pmm{pm}",), tag="vp_mm")
                P.emit("dve", lambda v, cg=cg, pm=pm, r=r: v.tensor_copy(
                    vsb[r][:, cg*512:(cg+1)*512], pmm[pm][0:64, :]),
                    (f"pmm{pm}",), (f"vsb{r}",), tag="vp_evac")
            for cg in range(2):
                for pl in range(2):
                    bank = 2 + pl
                    P.emit("pe", lambda t, g=g, cg=cg, pl=pl, r=r, bank=bank: t.matmul(
                        pmm[bank][:], f1b[:, pl*128:(pl+1)*128],
                        vsb[r][:, cg*512:(cg+1)*512], start=True, stop=True),
                        ("f1b", f"vsb{r}"), (f"pmm{bank}",), tag="s1v_mm")
                    sg = (cg * 2 + pl) % 4
                    P.emit("dve", lambda v, bank=bank, sg=sg: v.tensor_copy(
                        stg[sg][:], pmm[bank][:]), (f"pmm{bank}",), (f"stg{sg}",),
                        tag="s1v_evac")
                    P.dma(lambda sp, g=g, cg=cg, pl=pl, sg=sg: sp.dma_start(
                        out=cv_d[g, pl, :, cg*512:(cg+1)*512], in_=stg[sg][:]),
                        (f"stg{sg}",), ("cv",), tag="s1v_st")
            for wsrc, dst, key in ((w1b, x1g_d, "x1g"), (w2b, x2g_d, "x2g")):
                for cg in range(2):
                    pm = cg % 2
                    for dc in range(8):
                        P.emit("pe", lambda t, g=g, dc=dc, cg=cg, pm=pm, wsrc=wsrc: t.matmul(
                            pmm[pm][0:64, :], tok[:, dc, g::64][:, 0:64],
                            wsrc[:, dc*1024 + cg*512: dc*1024 + (cg+1)*512],
                            start=(dc == 0), stop=(dc == 7)),
                            ("ln_fm", "w1b", "w2b"), (f"pmm{pm}",), tag="xg_mm")
                    sg = cg % 4
                    P.emit("dve", lambda v, cg=cg, pm=pm, sg=sg: v.tensor_copy(
                        stg[sg][0:64, :], pmm[pm][0:64, :]),
                        (f"pmm{pm}",), (f"stg{sg}",), tag="xg_evac")
                    P.dma(lambda sp, g=g, cg=cg, sg=sg, dst=dst: sp.dma_start(
                        out=dst[g, :, cg*512:(cg+1)*512], in_=stg[sg][0:64, :]),
                        (f"stg{sg}",), (key,), tag="xg_st")
            for cg in range(2):
                pm = cg % 2
                P.emit("pe", lambda t, g=g, cg=cg, pm=pm: t.matmul(
                    pmm[pm][0:64, :], v1t[:, g*64:(g+1)*64],
                    fw2b[:, cg*512:(cg+1)*512], start=True, stop=True),
                    ("v1t", "fw2b"), (f"pmm{pm}",), tag="h_mm")
                P.emit("dve", lambda v, cg=cg, pm=pm, r=r: v.tensor_tensor(
                    out=hsb[r][:, cg*512:(cg+1)*512], in0=pmm[pm][0:64, :],
                    in1=fb2b[:, cg*512:(cg+1)*512], op=ALU.add),
                    (f"pmm{pm}", "fb2b"), (f"hsb{r}",), tag="h_evac")
            for cg in range(2):
                for pl in range(2):
                    bank = 2 + pl
                    P.emit("pe", lambda t, g=g, cg=cg, pl=pl, r=r, bank=bank: t.matmul(
                        pmm[bank][:], f1b[:, pl*128:(pl+1)*128],
                        hsb[r][:, cg*512:(cg+1)*512], start=True, stop=True),
                        ("f1b", f"hsb{r}"), (f"pmm{bank}",), tag="s1h_mm")
                    sg = (cg * 2 + pl) % 4
                    P.emit("dve", lambda v, bank=bank, sg=sg: v.tensor_copy(
                        stg[sg][:], pmm[bank][:]), (f"pmm{bank}",), (f"stg{sg}",),
                        tag="s1h_evac")
                    P.dma(lambda sp, g=g, cg=cg, pl=pl, sg=sg: sp.dma_start(
                        out=ch_d[g, pl, :, cg*512:(cg+1)*512], in_=stg[sg][:]),
                        (f"stg{sg}",), ("ch",), tag="s1h_st")
        P.finalize()
    return nc


def build_l1attn_mid():
    from contextlib import ExitStack
    nc = bass.Bass("TRN2", target_bir_lowering=False, debug=False)
    cvp_d = nc.dram_tensor("cvp", [128, 2, 64, 1024], BF16, kind="ExternalInput").ap()
    chp_d = nc.dram_tensor("chp", [128, 2, 64, 1024], BF16, kind="ExternalInput").ap()
    a2_d = nc.dram_tensor("a2", [128, 2, 64, 128], F32, kind="ExternalInput").ap()
    e2_d = nc.dram_tensor("e2", [128, 128], F32, kind="ExternalInput").ap()
    yv_d = nc.dram_tensor("yv", [128, 128, 1024], BF16, kind="ExternalOutput").ap()

    es = ExitStack()
    P = Prog(nc)
    ctx = Ctx(nc, P, es)
    with es:
        a2b = ctx.sbuf("a2b", [64, 128 * 2 * 128], BF16)
        e2r = ctx.sbuf("e2r", [64, 128], BF16)
        e2i = ctx.sbuf("e2i", [64, 128], BF16)
        wres = ctx.sbuf("wres", [128, 8 * 512], F32)
        cvb = [ctx.sbuf(f"cvb{i}", [64, 2 * 512], BF16) for i in range(2)]
        chb = [ctx.sbuf(f"chb{i}", [64, 2 * 512], BF16) for i in range(2)]
        hsp = ctx.sbuf("hsp", [64, 1024], F32)
        vsp = ctx.sbuf("vsp", [64, 1024], F32)
        zsp = ctx.sbuf("zsp", [64, 1024], BF16)
        t1 = ctx.sbuf("t1", [64, 1024], F32)
        stg = [ctx.sbuf(f"stg{i}", [128, 512], BF16) for i in range(2)]
        ctx.identities()
        pmm = [ctx.psum(f"pmm{i}", [128, 512], F32) for i in range(4)]

        for q in range(8):
            P.dma(lambda sp, q=q: sp.dma_start(
                out=wres[0:64, 0:4096].rearrange("g (m two c) -> g m two c", m=16, two=2),
                in_=a2_d[q*16:(q+1)*16].rearrange("m two g c -> g m two c")),
                ("wres_f",), ("wres",), tag="ld_a2")
            P.emit("dve", lambda v, q=q: v.tensor_copy(
                a2b[:, q*4096:(q+1)*4096], wres[0:64, 0:4096]),
                ("wres",), ("a2b", "wres_f"), tag="cast_a2")
        P.dma(lambda sp: sp.dma_start(out=wres[0:64, 0:256].rearrange(
            "p (two c) -> p two c", two=2), in_=e2_d.rearrange(
            "(two p) c -> p two c", p=64)), ("wres_f",), ("wres",), tag="ld_e2")
        P.emit("dve", lambda v: v.tensor_copy(e2r[:], wres[0:64, 0:128]),
               ("wres",), ("e2b",), tag="cast_e2r")
        P.emit("dve", lambda v: v.tensor_copy(e2i[:], wres[0:64, 128:256]),
               ("wres",), ("e2b", "wres_f"), tag="cast_e2i")

        for m in range(128):
          for cg0 in range(2):
            r = (m * 2 + cg0) % 2
            if True:
                for src_d, dst, key in ((cvp_d, cvb, "cvb"), (chp_d, chb, "chb")):
                    cg = cg0
                    P.dma(lambda sp, m=m, cg=cg, r=r, src_d=src_d, dst=dst, key=key: sp.dma_start(
                        out=dst[r][:].rearrange("g (two c) -> g two c", c=512),
                        in_=src_d[m, :, :, cg*512:(cg+1)*512]
                        .rearrange("two g c -> g two c")),
                        (f"{key}{r}_f",), (f"{key}{r}",), tag="ld_c")
            for cg in [cg0]:
                for which, cb, keyc, bank in (("v", cvb, "cvb", 0), ("h", chb, "chb", 1)):
                    for t2 in range(2):
                        P.emit("pe", lambda t, m=m, cg=cg, r=r, t2=t2, cb=cb,
                               keyc=keyc, bank=bank: t.matmul(
                            pmm[bank][:],
                            a2b[:, m*256 + t2*128: m*256 + (t2+1)*128],
                            cb[r][:].rearrange("g (two c) -> g two c", c=512)[:, t2],
                            start=(t2 == 0), stop=(t2 == 1)),
                            ("a2b", f"{keyc}{r}"), (f"pmm{bank}",), tag="s2_mm")
                P.emit("act", lambda s: s.copy(hsp[0:64, 0:512], pmm[1][0:64, :]),
                       ("pmm1",), ("hsp",), tag="sp_hre")
                P.emit("act", lambda s: s.copy(hsp[0:64, 512:1024], pmm[1][64:128, :]),
                       ("pmm1",), ("hsp",), tag="sp_him")
                P.emit("dve", lambda v: v.tensor_copy(vsp[0:64, 0:512], pmm[0][0:64, :]),
                       ("pmm0",), ("vsp",), tag="sp_vre")
                P.emit("dve", lambda v: v.tensor_copy(vsp[0:64, 512:1024], pmm[0][64:128, :]),
                       ("pmm0",), ("vsp", "pmm0_f"), tag="sp_vim")
                P.emit("dve", lambda v: v.tensor_tensor(
                    out=t1[0:64, 0:512], in0=vsp[0:64, 0:512], in1=hsp[0:64, 0:512],
                    op=ALU.mult), ("vsp", "hsp"), ("t1",), tag="sp_rr")
                P.emit("dve", lambda v: v.tensor_tensor(
                    out=t1[0:64, 512:1024], in0=vsp[0:64, 512:1024],
                    in1=hsp[0:64, 512:1024], op=ALU.mult),
                    ("vsp", "hsp"), ("t1",), tag="sp_ii")
                P.emit("dve", lambda v: v.tensor_tensor(
                    out=zsp[0:64, 0:512], in0=t1[0:64, 0:512], in1=t1[0:64, 512:1024],
                    op=ALU.subtract), ("t1",), ("zsp",), tag="sp_re")
                P.emit("dve", lambda v: v.tensor_tensor(
                    out=t1[0:64, 0:512], in0=vsp[0:64, 0:512], in1=hsp[0:64, 512:1024],
                    op=ALU.mult), ("vsp", "hsp", "t1"), ("t1",), tag="sp_ir")
                P.emit("dve", lambda v: v.tensor_tensor(
                    out=t1[0:64, 512:1024], in0=vsp[0:64, 512:1024], in1=hsp[0:64, 0:512],
                    op=ALU.mult), ("vsp", "hsp", "t1"), ("t1",), tag="sp_ri")
                P.emit("dve", lambda v: v.tensor_tensor(
                    out=zsp[0:64, 512:1024], in0=t1[0:64, 0:512], in1=t1[0:64, 512:1024],
                    op=ALU.add), ("t1",), ("zsp", "hsp_f", "vsp_f"), tag="sp_im")
                P.emit("pe", lambda t: t.matmul(
                    pmm[2][:], e2r[:], zsp[0:64, 0:512], start=True, stop=False),
                    ("e2b", "zsp"), ("pmm2",), tag="ii_mm1")
                P.emit("pe", lambda t: t.matmul(
                    pmm[2][:], e2i[:], zsp[0:64, 512:1024], start=False, stop=True),
                    ("e2b", "zsp"), ("pmm2",), tag="ii_mm2")
                sg = cg % 2
                P.emit("dve", lambda v, sg=sg: v.tensor_copy(stg[sg][:], pmm[2][:]),
                       ("pmm2",), (f"stg{sg}",), tag="ii_evac")
                P.dma(lambda sp, m=m, cg=cg, sg=sg: sp.dma_start(
                    out=yv_d[m, :, cg*512:(cg+1)*512], in_=stg[sg][:]),
                    (f"stg{sg}",), ("yv",), tag="ii_st")
            P.emit("dve", lambda v, r=r: v.engine_nop(),
                   (f"cvb{r}", f"chb{r}"), (f"cvb{r}_f", f"chb{r}_f"), tag="c_free")
        P.finalize()
    return nc


def build_l1attn_out(with_stage1):
    from contextlib import ExitStack
    nc = bass.Bass("TRN2", target_bir_lowering=False, debug=False)
    yv_d = nc.dram_tensor("yvp", [128, 128, 1024], BF16, kind="ExternalInput").ap()
    f3_d = nc.dram_tensor("f3", [64, 2, 128, 64], F32, kind="ExternalInput").ap()
    xg_d = nc.dram_tensor("xg", [64, 64, 1024], BF16, kind="ExternalInput").ap()
    if with_stage1:
        f1_d = nc.dram_tensor("f1", [64, 256], F32, kind="ExternalInput").ap()
        fw1_d = nc.dram_tensor("fw1", [1, 64], F32, kind="ExternalInput").ap()
        fb1_d = nc.dram_tensor("fb1", [1, 64], F32, kind="ExternalInput").ap()
        fw2_d = nc.dram_tensor("fw2t", [64, 1024], F32, kind="ExternalInput").ap()
        fb2_d = nc.dram_tensor("fb2", [1, 1024], F32, kind="ExternalInput").ap()
        cv_d = nc.dram_tensor("cv", [64, 2, 128, 1024], BF16, kind="ExternalOutput").ap()
        ch_d = nc.dram_tensor("ch", [64, 2, 128, 1024], BF16, kind="ExternalOutput").ap()
    else:
        wo_d = nc.dram_tensor("wout_t", [1024, 1024], F32, kind="ExternalInput").ap()
        xp_d = nc.dram_tensor("xperm", [1024, 4096], F32, kind="ExternalInput").ap()
        y_d = nc.dram_tensor("y", [1024, 4096], F32, kind="ExternalOutput").ap()

    es = ExitStack()
    P = Prog(nc)
    ctx = Ctx(nc, P, es)
    with es:
        wres = ctx.sbuf("wres", [128, 4096], F32)
        f3b = ctx.sbuf("f3b", [128, 64 * 2 * 64], BF16)
        yr = [ctx.sbuf(f"yr{i}", [128, 1024], BF16) for i in range(2)]
        xgr = [ctx.sbuf(f"xgr{i}", [64, 512], BF16) for i in range(2)]
        osb = [ctx.sbuf(f"osb{i}", [64, 512], BF16) for i in range(2)]
        ctx.identities()
        pmm = [ctx.psum(f"pmm{i}", [128, 512], F32) for i in range(4)]
        ps_t = ctx.psum("ps_t", [128, 1024], BF16)
        if with_stage1:
            f1b = ctx.sbuf("f1b", [64, 256], BF16)
            fw2b = ctx.sbuf("fw2b", [64, 1024], BF16)
            fb2b = ctx.sbuf("fb2b", [64, 1024], F32)
            v1t = ctx.sbuf("v1t", [64, 4096], BF16)
            tbc = ctx.sbuf("tbc", [128, 4096], F32)
            cols = ctx.sbuf("cols", [64, 4], F32)
            onesr = ctx.sbuf("onesr", [1, 128], F32)
            one11 = ctx.sbuf("one11", [1, 1], F32)
            misc = ctx.sbuf("miscrow", [1, 1024], F32)
            stg = [ctx.sbuf(f"stg{i}", [128, 512], BF16) for i in range(4)]
        else:
            wob = ctx.sbuf("wob", [128, 8 * 1024], BF16)
            oT = ctx.sbuf("oT", [128, 8 * 64], BF16)
            xr = [ctx.sbuf(f"xr{i}", [128, 64], F32) for i in range(2)]
            ystg = [ctx.sbuf(f"ystg{i}", [128, 64], F32) for i in range(2)]

        for q in range(4):
            P.dma(lambda sp, q=q: sp.dma_start(
                out=wres[:, 0:2048].rearrange("m (g ri j) -> m g ri j", g=16, ri=2),
                in_=f3_d[q*16:(q+1)*16].rearrange("g ri m j -> m g ri j")),
                ("wres_f",), ("wres",), tag="ld_f3")
            P.emit("dve", lambda v, q=q: v.tensor_copy(
                f3b[:, q*2048:(q+1)*2048], wres[:, 0:2048]),
                ("wres",), ("f3b", "wres_f"), tag="cast_f3")
        if with_stage1:
            P.emit("pool", lambda g: g.memset(onesr[:], 1.0), (), ("onesr",), tag="o2")
            P.emit("pool", lambda g: g.memset(one11[:], 1.0), (), ("one11",), tag="o3")
            P.dma(lambda sp: sp.dma_start(out=wres[0:64, 0:256], in_=f1_d),
                  ("wres_f",), ("wres",), tag="ld_f1")
            P.emit("dve", lambda v: v.tensor_copy(f1b[:], wres[0:64, 0:256]),
                   ("wres",), ("f1b", "wres_f"), tag="cast_f1")
            P.dma(lambda sp: sp.dma_start(out=wres[0:64, 0:1024], in_=fw2_d),
                  ("wres_f",), ("wres",), tag="ld_fw2")
            P.emit("dve", lambda v: v.tensor_copy(fw2b[:], wres[0:64, 0:1024]),
                   ("wres",), ("fw2b", "wres_f"), tag="cast_fw2")
            P.dma(lambda sp: sp.dma_start(out=misc[:], in_=fb2_d), (), ("miscrow",), tag="ld_fb2")
            for cg in range(2):
                P.emit("pe", lambda t, cg=cg: t.matmul(
                    pmm[3][0:64, :], onesr[0:1, 0:64], misc[0:1, cg*512:(cg+1)*512],
                    start=True, stop=True), ("miscrow", "onesr"), ("pmm3",), tag="bc_fb2")
                P.emit("act", lambda s, cg=cg: s.copy(fb2b[:, cg*512:(cg+1)*512],
                                                      pmm[3][0:64, :]),
                       ("pmm3",), ("fb2b",), tag="cp_fb2")
            for ci, d_ in ((0, fw1_d), (1, fb1_d)):
                P.dma(lambda sp, d_=d_: sp.dma_start(out=misc[0:1, 0:64], in_=d_),
                      ("miscrow",), ("miscrow",), tag="ld_f1c")
                P.emit("pe", lambda t: t.matmul(pmm[3][0:64, 0:1], misc[0:1, 0:64],
                                                one11[:], start=True, stop=True),
                       ("miscrow", "one11"), ("pmm3",), tag="t_f1c")
                P.emit("act", lambda s, ci=ci: s.copy(cols[:, ci:ci+1], pmm[3][0:64, 0:1]),
                       ("pmm3",), ("cols",), tag="cp_f1c")
            for q in range(4):
                P.emit("pool", lambda g, q=q: g.iota(
                    tbc[0:1, q*1024:(q+1)*1024].rearrange("o (a b) -> o a b", a=16),
                    pattern=[[1, 16], [64, 64]], base=q*16, channel_multiplier=0,
                    allow_small_or_imprecise_dtypes=True),
                    (), ("tbc",), tag="t_iota")
            P.emit("dve", lambda v: v.tensor_scalar(
                out=tbc[0:1, :], in0=tbc[0:1, :], scalar1=1.0/4095.0, scalar2=None,
                op0=ALU.mult), ("tbc",), ("tbc",), tag="t_scale")
            for cg in range(8):
                P.emit("pe", lambda t, cg=cg: t.matmul(
                    pmm[3][0:64, :], onesr[0:1, 0:64], tbc[0:1, cg*512:(cg+1)*512],
                    start=True, stop=True), ("tbc", "onesr"), ("pmm3",), tag="t_bc")
                P.emit("act", lambda s, cg=cg: s.copy(tbc[0:64, cg*512:(cg+1)*512],
                                                      pmm[3][0:64, :]),
                       ("pmm3",), ("tbc",), tag="t_bccp")
            P.emit("dve", lambda v: v.tensor_scalar(
                out=tbc[0:64, :], in0=tbc[0:64, :], scalar1=cols[:, 0:1],
                scalar2=cols[:, 1:2], op0=ALU.mult, op1=ALU.add),
                ("tbc", "cols"), ("tbc",), tag="v1_ts")
            P.emit("act", lambda s: s.activation(out=v1t[:], in_=tbc[0:64, :],
                                                 func=AF.Silu),
                   ("tbc",), ("v1t",), tag="v1_silu")
        else:
            for hh in range(2):
                P.dma(lambda sp, hh=hh: sp.dma_start(
                    out=wres[:, 0:4096].rearrange("p (c d) -> p c d", c=8),
                    in_=wo_d[:, hh*512:(hh+1)*512].rearrange("(c p) d -> p c d", p=128)),
                    ("wob", "wres_f"), ("wres",), tag="ld_wo")
                P.emit("dve", lambda v, hh=hh: v.tensor_copy(
                    wob[:].rearrange("p (c d) -> p c d", c=8)[:, :, hh*512:(hh+1)*512],
                    wres[:, 0:4096].rearrange("p (c d) -> p c d", c=8)),
                    ("wres",), ("wob", "wres_f"), tag="cast_wo")

        for g in range(64):
            for cg in range(2):
                r = cg
                P.dma(lambda sp, g=g, cg=cg, r=r: sp.dma_start(
                    out=yr[r][:].rearrange("m (ri c) -> m ri c", ri=2),
                    in_=yv_d[g:g+65:64, :, cg*512:(cg+1)*512]
                    .rearrange("ri m c -> m ri c")),
                    (f"yr{r}_f",), (f"yr{r}",), tag="ld_y")
                for ri in range(2):
                    P.emit("pe", lambda t, g=g, ri=ri, r=r: t.matmul(
                        pmm[0][0:64, :],
                        f3b[:, g*128 + ri*64: g*128 + (ri+1)*64],
                        yr[r][:].rearrange("m (ri c) -> m ri c", ri=2)[:, ri],
                        start=(ri == 0), stop=(ri == 1)),
                        ("f3b", f"yr{r}"), ("pmm0",), tag="oi_mm")
                P.dma(lambda sp, g=g, cg=cg, r=r: sp.dma_start(
                    out=xgr[r][:], in_=xg_d[g, :, cg*512:(cg+1)*512]),
                    (f"xgr{r}_f",), (f"xgr{r}",), tag="ld_xg")
                P.emit("dve", lambda v, r=r: v.tensor_tensor(
                    out=osb[r][:], in0=pmm[0][0:64, :], in1=xgr[r][:], op=ALU.mult),
                    ("pmm0", f"xgr{r}"), (f"osb{r}", f"xgr{r}_f", f"yr{r}_f"), tag="oi_mul")
                if with_stage1:
                    for pl in range(2):
                        bank = 1 + pl
                        P.emit("pe", lambda t, pl=pl, r=r, bank=bank: t.matmul(
                            pmm[bank][:], f1b[:, pl*128:(pl+1)*128], osb[r][:],
                            start=True, stop=True),
                            ("f1b", f"osb{r}"), (f"pmm{bank}",), tag="s1v_mm")
                        sg = (cg * 2 + pl) % 4
                        P.emit("dve", lambda v, bank=bank, sg=sg: v.tensor_copy(
                            stg[sg][:], pmm[bank][:]), (f"pmm{bank}",), (f"stg{sg}",),
                            tag="s1v_evac")
                        P.dma(lambda sp, g=g, cg=cg, pl=pl, sg=sg: sp.dma_start(
                            out=cv_d[g, pl, :, cg*512:(cg+1)*512], in_=stg[sg][:]),
                            (f"stg{sg}",), ("cv",), tag="s1v_st")
                else:
                    for c4 in range(4):
                        emit_transpose(ctx, ps_t[:, c4*128:c4*128+64].bitcast(BF16)
                                       if False else ps_t[:, c4*64:(c4+1)*64],
                                       osb[r][:, c4*128:(c4+1)*128],
                                       (f"osb{r}",), ("ps_t",), BF16)
                    P.emit("dve", lambda v, cg=cg: v.tensor_copy(
                        oT[:, cg*256:(cg+1)*256], ps_t[:, 0:256]),
                        ("ps_t",), ("oT",), tag="t_oevac")
            if with_stage1:
                for cg in range(2):
                    pm = cg % 2
                    P.emit("pe", lambda t, g=g, cg=cg, pm=pm: t.matmul(
                        pmm[pm][0:64, :], v1t[:, g*64:(g+1)*64],
                        fw2b[:, cg*512:(cg+1)*512], start=True, stop=True),
                        ("v1t", "fw2b"), (f"pmm{pm}",), tag="h_mm")
                    P.emit("dve", lambda v, cg=cg, pm=pm: v.tensor_tensor(
                        out=osb[0][:] if cg == 0 else osb[1][:], in0=pmm[pm][0:64, :],
                        in1=fb2b[:, cg*512:(cg+1)*512], op=ALU.add),
                        (f"pmm{pm}", "fb2b"), (f"osb{cg}",), tag="h_evac")
                for cg in range(2):
                    for pl in range(2):
                        bank = 1 + pl
                        P.emit("pe", lambda t, cg=cg, pl=pl, bank=bank: t.matmul(
                            pmm[bank][:], f1b[:, pl*128:(pl+1)*128], osb[cg][:],
                            start=True, stop=True),
                            ("f1b", f"osb{cg}"), (f"pmm{bank}",), tag="s1h_mm")
                        sg = (cg * 2 + pl) % 4
                        P.emit("dve", lambda v, bank=bank, sg=sg: v.tensor_copy(
                            stg[sg][:], pmm[bank][:]), (f"pmm{bank}",), (f"stg{sg}",),
                            tag="s1h_evac")
                        P.dma(lambda sp, g=g, cg=cg, pl=pl, sg=sg: sp.dma_start(
                            out=ch_d[g, pl, :, cg*512:(cg+1)*512], in_=stg[sg][:]),
                            (f"stg{sg}",), ("ch",), tag="s1h_st")
            else:
                for dc in range(8):
                    po = dc % 2
                    for cc in range(8):
                        P.emit("pe", lambda t, g=g, dc=dc, cc=cc, po=po: t.matmul(
                            pmm[po][:, 0:64], wob[:, cc*1024 + dc*128: cc*1024 + (dc+1)*128],
                            oT[:, cc*64:(cc+1)*64], start=(cc == 0), stop=(cc == 7)),
                            ("wob", "oT"), (f"pmm{po}",), tag="wo_mm")
                    pr = dc % 2
                    P.dma(lambda sp, g=g, dc=dc, pr=pr: sp.dma_start(
                        out=xr[pr][:], in_=xp_d[dc*128:(dc+1)*128, g*64:(g+1)*64]),
                        (f"ystg{pr}",), (f"xr{pr}",), tag="ld_xr")
                    P.emit("dve", lambda v, po=po, pr=pr: v.tensor_tensor(
                        out=ystg[pr][:], in0=pmm[po][:, 0:64], in1=xr[pr][:], op=ALU.add),
                        (f"pmm{po}", f"xr{pr}"), (f"ystg{pr}",), tag="resid")
                    P.dma(lambda sp, g=g, dc=dc, pr=pr: sp.dma_start(
                        out=y_d[dc*128:(dc+1)*128, g*64:(g+1)*64], in_=ystg[pr][:]),
                        (f"ystg{pr}",), ("y_out",), tag="st_y")
                P.emit("dve", lambda v: v.engine_nop(), ("oT",), ("oT_f" ,), tag="oT_free")
        P.finalize()
    return nc



def build_l0_proj():
    from contextlib import ExitStack
    nc = bass.Bass("TRN2", target_bir_lowering=False, debug=False)
    N = 4096
    J = 4224
    x_d = nc.dram_tensor("x", [1024, N], F32, kind="ExternalInput").ap()
    wq_d = nc.dram_tensor("wq", [1024, 1024], F32, kind="ExternalInput").ap()
    wkv_d = nc.dram_tensor("wkv", [1024, 2048], F32, kind="ExternalInput").ap()
    mk_d = nc.dram_tensor("memk", [1024, 16], F32, kind="ExternalInput").ap()
    mv_d = nc.dram_tensor("memv", [16, 1024], F32, kind="ExternalInput").ap()
    q_d = nc.dram_tensor("q", [1024, N], BF16, kind="ExternalOutput").ap()
    k_d = nc.dram_tensor("k", [1024, J], BF16, kind="ExternalOutput").ap()
    v_d = nc.dram_tensor("v", [J, 1024], BF16, kind="ExternalOutput").ap()

    es = ExitStack()
    P = Prog(nc)
    ctx = Ctx(nc, P, es)
    with es:
        xring = [ctx.sbuf(f"xring{i}", [128, 8 * 512], F32) for i in range(2)]
        ln_fm = ctx.sbuf("ln_fm", [128, 8 * N], BF16)
        statT = ctx.sbuf("statT", [1, 2 * N + 512], F32)
        sq = ctx.sbuf("sq", [128, 512], F32)
        ones1 = ctx.sbuf("ones1", [128, 1], F32)
        onesr = ctx.sbuf("onesr", [1, 128], F32)
        wc = ctx.sbuf("wc", [128, 8 * 512], F32)
        wcb = [ctx.sbuf(f"wcb{i}", [128, 8 * 512], BF16) for i in range(2)]
        stg = [ctx.sbuf(f"stg{i}", [128, 512], BF16) for i in range(2)]
        mstg = ctx.sbuf("mstg", [128, 128], BF16)
        ctx.identities()
        ps = {k2: ctx.psum(f"ps_{k2}", [128, 512], F32)
              for k2 in ("sum", "sumsq", "bc_mu", "bc_rs")}
        pmm = [ctx.psum(f"pmm{i}", [128, 512], F32) for i in range(2)]

        P.emit("pool", lambda g: g.memset(ones1[:], 1.0), (), ("ones1",), tag="o1")
        P.emit("pool", lambda g: g.memset(onesr[:], 1.0), (), ("onesr",), tag="o2")
        emit_ln_fm_stream(ctx, x_d, ln_fm, N, statT, sq[:], xring, ps, "ln_fm", "ln")

        P.dma(lambda sp: sp.dma_start(
            out=wc[:, 0:128].rearrange("p (c e) -> p c e", c=8),
            in_=mk_d.rearrange("(c p) e -> p c e", p=128)),
            ("wc_f",), ("wc",), tag="ld_mk")
        P.emit("dve", lambda v: v.tensor_copy(
            mstg[:], wc[:, 0:128]), ("wc",), ("mstg", "wc_f"), tag="cast_mk")
        P.dma(lambda sp: sp.dma_start(
            out=k_d[:, 0:16].rearrange("(c p) e -> p c e", p=128),
            in_=mstg[:].rearrange("p (c e) -> p c e", c=8)), ("mstg",), ("k_out",),
            tag="st_mk")
        P.dma(lambda sp: sp.dma_start(out=wc[0:16, 0:1024], in_=mv_d),
              ("wc_f", "k_out"), ("wc",), tag="ld_mv")
        P.emit("dve", lambda v: v.tensor_copy(stg[0][0:16, :].rearrange(
            "p (a b) -> p a b", a=2)[:, 0, :] if False else stg[0][0:16, 0:512],
            wc[0:16, 0:512]), ("wc",), ("stg0",), tag="cast_mv0")
        P.emit("dve", lambda v: v.tensor_copy(stg[1][0:16, 0:512], wc[0:16, 512:1024]),
               ("wc",), ("stg1", "wc_f"), tag="cast_mv1")
        P.dma(lambda sp: sp.dma_start(out=v_d[0:16, 0:512], in_=stg[0][0:16, 0:512]),
              ("stg0",), ("v_out",), tag="st_mv0")
        P.dma(lambda sp: sp.dma_start(out=v_d[0:16, 512:1024], in_=stg[1][0:16, 0:512]),
              ("stg1",), ("v_out",), tag="st_mv1")
        P.emit("dve", lambda v: v.memset(stg[0][:], 0.0), ("stg0",), ("stg0",), tag="z0")
        for hf in range(2):
            P.dma(lambda sp, hf=hf: sp.dma_start(
                out=k_d[hf*512:(hf+1)*512, 4112:4224].rearrange(
                    "(c p) e -> p c e", p=128),
                in_=stg[0][:, 0:448].rearrange("p (c e) -> p c e", c=4)),
                ("stg0",), ("k_out",), tag="st_kpad")
        for q2 in range(2):
            P.dma(lambda sp, q2=q2: sp.dma_start(
                out=v_d[4112:4168, q2*512:(q2+1)*512] if q2 == 0 else
                    v_d[4168:4224, q2*512 - 512:(q2+1)*512 - 512],
                in_=stg[0][0:56, 0:512]), ("stg0",), ("v_out",), tag="st_vpad")
        P.dma(lambda sp: sp.dma_start(
            out=v_d[4112:4224, 512:1024], in_=stg[0][0:112, 0:512]),
            ("stg0",), ("v_out",), tag="st_vpad2")

        NCH = N // 512
        for cg in range(6):
            cb = cg % 2
            src = wq_d if cg < 2 else wkv_d
            scol = cg * 512 if cg < 2 else (cg - 2) * 512
            P.dma(lambda sp, src=src, scol=scol: sp.dma_start(
                out=wc[:].rearrange("p (c d) -> p c d", c=8),
                in_=src[:, scol:scol+512].rearrange("(c p) d -> p c d", p=128)),
                ("wc_f",), ("wc",), tag="ld_w")
            P.emit("dve", lambda v, cb=cb: v.tensor_copy(wcb[cb][:], wc[:]),
                   ("wc",), (f"wcb{cb}", "wc_f"), tag="cast_w")
            if cg < 4:
                scale = 0.125 if cg < 2 else 1.0
                dst = q_d if cg < 2 else k_d
                dkey = "q_out" if cg < 2 else "k_out"
                joff = 0 if cg < 2 else 16
                for oc in range(4):
                    gchunk = (cg % 2) * 4 + oc
                    for tc in range(NCH):
                        pm = (oc * NCH + tc) % 2
                        sg2 = (oc * NCH + tc) % 2
                        for dc in range(8):
                            P.emit("pe", lambda t, oc=oc, tc=tc, dc=dc, pm=pm, cb=cb: t.matmul(
                                pmm[pm][:], wcb[cb][:, dc*512 + oc*128: dc*512 + (oc+1)*128],
                                ln_fm[:, dc*N + tc*512: dc*N + (tc+1)*512],
                                start=(dc == 0), stop=(dc == 7)),
                                (f"wcb{cb}", "ln_fm"), (f"pmm{pm}",), tag="qk_mm")
                        P.emit("dve", lambda v, pm=pm, sg2=sg2, scale=scale: v.tensor_scalar(
                            out=stg[sg2][:], in0=pmm[pm][:], scalar1=scale, scalar2=None,
                            op0=ALU.mult),
                            (f"pmm{pm}",), (f"stg{sg2}",), tag="qk_evac")
                        P.dma(lambda sp, sg2=sg2, gchunk=gchunk, tc=tc, dst=dst,
                              joff=joff: sp.dma_start(
                            out=dst[gchunk*128:(gchunk+1)*128,
                                    joff + tc*512: joff + (tc+1)*512],
                            in_=stg[sg2][:]), (f"stg{sg2}",), (dkey,), tag="qk_st")
            else:
                for jt in range(32):
                    pm = jt % 2
                    sg2 = jt % 2
                    for dc in range(8):
                        P.emit("pe", lambda t, jt=jt, dc=dc, pm=pm, cb=cb: t.matmul(
                            pmm[pm][:], ln_fm[:, dc*N + jt*128: dc*N + (jt+1)*128],
                            wcb[cb][:, dc*512:(dc+1)*512],
                            start=(dc == 0), stop=(dc == 7)),
                            (f"wcb{cb}", "ln_fm"), (f"pmm{pm}",), tag="v_mm")
                    P.emit("dve", lambda v, pm=pm, sg2=sg2: v.tensor_copy(
                        stg[sg2][:], pmm[pm][:]),
                        (f"pmm{pm}",), (f"stg{sg2}",), tag="v_evac")
                    P.dma(lambda sp, sg2=sg2, jt=jt, cg=cg: sp.dma_start(
                        out=v_d[16 + jt*128: 16 + (jt+1)*128, (cg-4)*512:(cg-3)*512],
                        in_=stg[sg2][:]), (f"stg{sg2}",), ("v_out",), tag="v_st")
        P.finalize()
    return nc



def build_l0_attn():
    from contextlib import ExitStack
    nc = bass.Bass("TRN2", target_bir_lowering=False, debug=False)
    J4 = 4224
    q_d = nc.dram_tensor("q", [1024, 2048], BF16, kind="ExternalInput").ap()
    k_d = nc.dram_tensor("k", [1024, J4], BF16, kind="ExternalInput").ap()
    v_d = nc.dram_tensor("v", [J4, 1024], BF16, kind="ExternalInput").ap()
    kron_d = nc.dram_tensor("kron", [128, 128], F32, kind="ExternalInput").ap()
    c1_d = nc.dram_tensor("c1tab", [128, 256], F32, kind="ExternalInput").ap()
    ir_d = nc.dram_tensor("irow", [128, 1280], F32, kind="ExternalInput").ap()
    wo_d = nc.dram_tensor("wout", [1024, 1024], F32, kind="ExternalInput").ap()
    bo_d = nc.dram_tensor("bout", [128, 8], F32, kind="ExternalInput").ap()
    xo_d = nc.dram_tensor("xo", [1024, 2048], F32, kind="ExternalInput").ap()
    y_d = nc.dram_tensor("y", [1024, 2048], F32, kind="ExternalOutput").ap()
    ssb_d = nc.dram_tensor("ssb", [64, 128, J4], BF16, kind="Internal").ap()
    wob_d = nc.dram_tensor("wobd", [1024, 1024], BF16, kind="Internal").ap()

    es = ExitStack()
    P = Prog(nc)
    ctx = Ctx(nc, P, es)
    with es:
        utb = ctx.sbuf("utb", [128, 33 * 1024], BF16)
        ssb = [ctx.sbuf(f"ssb{i}", [128, 8192], BF16) for i in range(2)]
        sring = [ctx.sbuf(f"sring{i}", [128, J4], BF16) for i in range(2)]
        qsb = ctx.sbuf("qsb", [128, J4], BF16)
        ktile = [ctx.sbuf(f"ktile{i}", [128, 1024], BF16) for i in range(4)]
        vt = [ctx.sbuf(f"vt{i}", [128, 1024], BF16) for i in range(4)]
        wt = [ctx.sbuf(f"wt{i}", [128, 1024], BF16) for i in range(2)]
        xot = [ctx.sbuf(f"xot{i}", [128, 512], F32) for i in range(2)]
        ofm = ctx.sbuf("ofm", [128, 8 * 512], BF16)
        ystg = [ctx.sbuf(f"ystg{i}", [128, 512], F32) for i in range(2)]
        tsb4 = [ctx.sbuf(f"tsb4{i}", [128, 512], BF16) for i in range(2)]
        mstg = [ctx.sbuf(f"mstg{i}", [128, 1024], BF16) for i in range(2)]
        irow = ctx.sbuf("irowsb", [128, 1280], F32)
        c1sb = ctx.sbuf("c1sb", [128, 256], F32)
        mskb = ctx.sbuf("mskb", [128, 1280], BF16)
        bosb = ctx.sbuf("bosb", [128, 8], F32)
        kronb = ctx.sbuf("kronb", [128, 128], BF16)
        kscr = ctx.sbuf("kscr", [128, 128], F32)
        cb40 = ctx.sbuf("cb40", [128, 1], F32)
        lo_c = ctx.sbuf("lo_c", [128, 1], F32)
        cnt_c = ctx.sbuf("cnt_c", [128, 1], F32)
        ge_c = ctx.sbuf("ge_c", [128, 1], F32)
        cf_c = ctx.sbuf("cf_c", [128, 1], F32)
        cu_c = ctx.sbuf("cu_c", [128, 1], U16)
        tc1 = ctx.sbuf("tc1", [128, 1], F32)
        den_c = ctx.sbuf("den_c", [128, 1], F32)
        junk2 = ctx.sbuf("junk2", [128, J4], BF16)
        mskb2 = ctx.sbuf("mskb2", [128, 1280], BF16)
        lo_c2 = ctx.sbuf("lo_c2", [128, 1], F32)
        cnt_c2 = ctx.sbuf("cnt_c2", [128, 1], F32)
        ge_c2 = ctx.sbuf("ge_c2", [128, 1], F32)
        cu_c2 = ctx.sbuf("cu_c2", [128, 1], U16)
        tc12 = ctx.sbuf("tc12", [128, 1], F32)
        den_c2 = ctx.sbuf("den_c2", [128, 1], F32)
        vcol = ctx.sbuf("vcol", [128, 1], F32)
        negb = ctx.sbuf("negb", [128, 1], F32)
        ctx.identities()
        pmm = [ctx.psum(f"pmm{i}", [128, 512], F32) for i in range(2)]
        ptr = [ctx.psum(f"ptr{i}", [128, 1024], BF16) for i in range(2)]
        pmx = [ctx.psum(f"pmx{i}", [128, 512], F32) for i in range(2)]
        pav = [ctx.psum(f"pav{i}", [128, 512], F32) for i in range(2)]

        tbv = cu_c[:].bitcast(BF16)
        tbv2 = cu_c2[:].bitcast(BF16)

        P.emit("pool", lambda g: g.memset(cb40[:], -40.0), (), ("cb40",), tag="c40")
        P.dma(lambda sp: sp.dma_start(out=irow[:], in_=ir_d), (), ("irow",), tag="ld_ir")
        P.dma(lambda sp: sp.dma_start(out=c1sb[:], in_=c1_d), (), ("c1sb",), tag="ld_c1")
        P.dma(lambda sp: sp.dma_start(out=bosb[:], in_=bo_d), (), ("bosb",), tag="ld_bo")
        P.dma(lambda sp: sp.dma_start(out=kscr[:], in_=kron_d), ("kscr_f",), ("kscr",),
              tag="ld_kr")
        P.emit("dve", lambda v: v.tensor_copy(kronb[:], kscr[:]), ("kscr",),
               ("kronb", "kscr_f"), tag="cast_kr")
        for oc in range(8):
            sc = sring[0].bitcast(F32)[:, 0:1024]
            st2 = sring[1][:, 0:1024]
            P.dma(lambda sp, oc=oc, sc=sc: sp.dma_start(
                out=sc.rearrange("p (c e) -> p c e", c=8),
                in_=wo_d[:, oc*128:(oc+1)*128].rearrange("(c p) e -> p c e", p=128)),
                ("sring0_f",), ("sring0",), tag="ld_wo")
            P.emit("dve", lambda v, sc=sc, st2=st2: v.tensor_copy(st2, sc),
                   ("sring0",), ("sring1", "sring0_f"), tag="cast_wo")
            P.dma(lambda sp, oc=oc, st2=st2: sp.dma_start(
                out=wob_d[:, oc*128:(oc+1)*128].rearrange("(c p) e -> p c e", p=128),
                in_=st2.rearrange("p (c e) -> p c e", c=8)),
                ("sring1",), ("sring1_f",), tag="st_wo")

        utv = utb[:].rearrange("p (j k g q) -> p j k g q", j=33, k=8, g=8)
        for m in range(4):
            J = 8 * m + 9
            W = J * 128
            P.dma(lambda sp, m=m: sp.dma_start(
                out=qsb[:, 0:4096].rearrange("p (c e) -> p c e", c=8),
                in_=q_d[:, m*512:(m+1)*512].rearrange("(c p) e -> p c e", p=128)),
                ("qsb",), ("qsb",), tag="ld_q")
            for jt in range(J):
                kr = jt % 4
                sb = ssb[jt % 2]
                sbv = sb[:].rearrange("p (g h q) -> p g h q", g=64, h=16)
                P.dma(lambda sp, jt=jt, kr=kr: sp.dma_start(
                    out=ktile[kr][:].rearrange("p (c e) -> p c e", c=8),
                    in_=k_d[:, jt*128:(jt+1)*128].rearrange("(c p) e -> p c e", p=128)),
                    (f"ktile{kr}",), (f"ktile{kr}",), tag="ld_kt")
                for h in range(16):
                    pm = jt % 2
                    bp = (h % 2) * 64
                    ch = h // 2
                    P.emit("pe", lambda t, kr=kr, bp=bp, ch=ch, pm=pm: t.matmul(
                        pmm[pm][:], ktile[kr][bp:bp+64, ch*128:(ch+1)*128],
                        qsb[bp:bp+64, ch*512:(ch+1)*512], start=True, stop=True),
                        (f"ktile{kr}", "qsb"), (f"pmm{pm}",), tag="dots")
                    P.emit("act", lambda s, h=h, pm=pm, sbv=sbv: s.copy(
                        sbv[:, :, h, :],
                        pmm[pm][:].rearrange("p (g q) -> p g q", g=64)),
                        (f"pmm{pm}",), (f"ssb{pm}" if False else f"ssb{jt % 2}",),
                        tag="dots_ev")
                for g in range(64):
                    gq = (g // 4) % 2
                    P.emit("pe", lambda t, g=g, gq=gq, sb=sb: t.transpose(
                        ptr[gq][:, (g % 4)*128:(g % 4 + 1)*128],
                        sb[:, g*128:(g+1)*128], ctx.id_bf[:]),
                        (f"ssb{jt % 2}", "id_bf"), (f"ptr{gq}",), tag="t1")
                    if g % 4 == 3:
                        P.emit("act", lambda s, gq=gq: s.copy(
                            tsb4[gq][:], ptr[gq][:, 0:512]),
                            (f"ptr{gq}",), (f"tsb4{gq}",), tag="t1_ev")
                        for g2 in range(g - 3, g + 1):
                            P.emit("pe", lambda t, g2=g2, gq=gq: t.matmul(
                                pmx[gq][:, (g2 % 4)*128:(g2 % 4 + 1)*128],
                                kronb[:], tsb4[gq][:, (g2 % 4)*128:(g2 % 4 + 1)*128],
                                start=True, stop=True),
                                ("kronb", f"tsb4{gq}"), (f"pmx{gq}",), tag="mix")
                        oct2 = (g // 8) % 2
                        P.emit("act", lambda s, gq=gq, g=g, oct2=oct2: s.copy(
                            mstg[oct2][:, (g % 8 // 4)*512:(g % 8 // 4 + 1)*512],
                            pmx[gq][:]),
                            (f"pmx{gq}",), (f"mstg{oct2}",), tag="mix_ev")
                    if g % 8 == 7:
                        oct2 = (g // 8) % 2
                        P.dma(lambda sp, g=g, jt=jt, oct2=oct2: sp.dma_start(
                            out=ssb_d[g-7:g+1, :, jt*128:(jt+1)*128].rearrange(
                                "g p j -> p g j"),
                            in_=mstg[oct2][:].rearrange("p (g j) -> p g j", g=8)),
                            (f"mstg{oct2}",), (f"mstg{oct2}",), tag="spill")
            def chain_ops(g, ci):
                ops = []
                sr = (g // 2) % 2 if False else g % 2
                ub = ssb[ci][:, 0:W]
                u2 = qsb[:, 0:W] if ci == 0 else junk2[:, 0:W]
                lo, cnt, ge, cu, tb, tcx, den, msk = (
                    (lo_c, cnt_c, ge_c, cu_c, tbv, tc1, den_c, mskb) if ci == 0 else
                    (lo_c2, cnt_c2, ge_c2, cu_c2, tbv2, tc12, den_c2, mskb2))
                bk = f"bis{ci}"
                uk = "ssb0" if ci == 0 else "ssb1"
                u2k = "qsb" if ci == 0 else "junk2"
                mk = "mskb" if ci == 0 else "mskb2"
                srk = f"sring{ci}"
                ops.append(("dma", lambda sp, g=g, srk_i=ci, W=W: sp.dma_start(
                    out=sring[srk_i][:, 0:W], in_=ssb_d[g, :, 0:W]),
                    (srk,), (srk,), "ld_srow"))
                ops.append(("act", lambda s, ci=ci, ub=ub, W=W: s.activation(
                    out=ub, in_=sring[ci][:, 0:W], func=AF.Exp, bias=cb40[:]),
                    (srk, "cb40"), (uk,), "exp"))
                jtmin = max(0, (1024*m + 8*g - 111) // 128)
                W0 = 128 * jtmin
                Wt = W - W0
                col = m * 64 + g
                ops.append(("dve", lambda v, col=col, W0=W0, tcx=tcx: v.tensor_scalar(
                    out=tcx[:], in0=c1sb[:, col:col+1], scalar1=float(-W0),
                    scalar2=None, op0=ALU.add), ("c1sb",), (bk,), "tc1"))
                ops.append(("dve", lambda v, Wt=Wt, msk=msk, tcx=tcx: v.tensor_scalar(
                    out=msk[:, 0:Wt], in0=irow[:, 0:Wt], scalar1=tcx[:],
                    scalar2=None, op0=ALU.is_le), ("irow", bk), (mk,), "msk"))
                ops.append(("dve", lambda v, ub=ub, W0=W0, W=W, Wt=Wt, msk=msk:
                    v.tensor_tensor(out=ub[:, W0:W], in0=ub[:, W0:W],
                                    in1=msk[:, 0:Wt], op=ALU.mult),
                    (uk, mk), (uk,), "msk_ap"))
                ops.append(("dve", lambda v, lo=lo: v.memset(lo[:], 0.0),
                            (bk,), (bk,), "lo0"))
                for bitp in range(13, -1, -1):
                    stp = float(1 << bitp)
                    ops.append(("dve", lambda v, stp=stp, lo=lo, cu=cu: v.tensor_scalar(
                        out=cu[:], in0=lo[:], scalar1=stp, scalar2=None,
                        op0=ALU.add), (bk,), (bk,), "cand"))
                    if ci == 0:
                        ops.append(("dve", lambda v, ub=ub, u2=u2, W=W, tb=tb, cnt=cnt:
                            v.scalar_tensor_tensor(
                                out=u2, in0=ub, scalar=1.0,
                                in1=tb.to_broadcast([128, W]), op0=ALU.mult,
                                op1=ALU.is_ge, accum_out=cnt[:]),
                            (uk, bk), (u2k, bk), "count"))
                        ops.append(("dve", lambda v, stp=stp, ge=ge, cnt=cnt:
                            v.tensor_scalar(
                                out=ge[:], in0=cnt[:], scalar1=64.0, scalar2=stp,
                                op0=ALU.is_ge, op1=ALU.mult), (bk,), (bk,), "geup"))
                    else:
                        ops.append(("act", lambda s, tb=tb: s.copy(vcol[:], tb),
                                    (bk,), (bk,), "cval"))
                        ops.append(("act", lambda s: s.activation(
                            out=negb[:], in_=vcol[:], func=AF.Copy,
                            scale=-(1.0 - 2.0**-10)), (bk,), (bk,), "cmid"))
                        ops.append(("act", lambda s, ub=ub, u2=u2, cnt=cnt:
                            s.activation(out=u2, in_=ub, func=AF.Sign,
                                         bias=negb[:], accum_out=cnt[:]),
                            (uk, bk), (u2k, bk), "countA"))
                        ops.append(("dve", lambda v, stp=stp, ge=ge, cnt=cnt, W=W:
                            v.tensor_scalar(
                                out=ge[:], in0=cnt[:], scalar1=float(128 - W),
                                scalar2=stp, op0=ALU.is_ge, op1=ALU.mult),
                            (bk,), (bk,), "geupA"))
                    ops.append(("dve", lambda v, lo=lo, ge=ge: v.tensor_tensor(
                        out=lo[:], in0=lo[:], in1=ge[:], op=ALU.add),
                        (bk,), (bk,), "loup"))
                ops.append(("dve", lambda v, cu=cu, lo=lo: v.tensor_copy(cu[:], lo[:]),
                            (bk,), (bk,), "thru"))
                ops.append(("dve", lambda v, ub=ub, u2=u2, W=W, tb=tb:
                    v.scalar_tensor_tensor(
                        out=u2, in0=ub, scalar=1.0, in1=tb.to_broadcast([128, W]),
                        op0=ALU.mult, op1=ALU.is_ge, accum_out=None),
                    (uk, bk), (u2k,), "sel"))
                ops.append(("dve", lambda v, ub=ub, u2=u2, den=den:
                    v.scalar_tensor_tensor(
                        out=u2, in0=u2, scalar=1.0, in1=ub, op0=ALU.mult,
                        op1=ALU.mult, accum_out=den[:]),
                    (uk, u2k), (u2k, bk), "uprime"))
                ops.append(("dve", lambda v, den=den: v.tensor_scalar(
                    out=den[:], in0=den[:], scalar1=1e-30, scalar2=None,
                    op0=ALU.max), (bk,), (bk,), "dclamp"))
                ops.append(("dve", lambda v, den=den: v.reciprocal(
                    out=den[:], in_=den[:]), (bk,), (bk,), "drecip"))
                ops.append(("dve", lambda v, u2=u2, den=den: v.tensor_scalar(
                    out=u2, in0=u2, scalar1=den[:], scalar2=None,
                    op0=ALU.mult), (u2k, bk), (u2k,), "unorm"))
                gg = g % 8
                for jt in range(J):
                    tq = (jt // 8) % 2
                    slot = (jt % 4) + 4 * ci
                    ops.append(("pe", lambda t, jt=jt, tq=tq, slot=slot, u2=u2:
                        t.transpose(
                            ptr[tq][:, slot*128:(slot+1)*128],
                            u2[:, jt*128:(jt+1)*128], ctx.id_bf[:]),
                        (u2k, "id_bf"), (f"ptr{tq}",), "t3"))
                    ops.append(("act", lambda s, jt=jt, tq=tq, slot=slot, gg=gg: s.copy(
                        utv[:, jt, :, gg, :],
                        ptr[tq][:, slot*128:(slot+1)*128].rearrange(
                            "p (k q) -> p k q", k=8)),
                        (f"ptr{tq}",), ("utb",), "t3_ev"))
                return ops

            for gb in range(8):
                for gg2 in range(0, 8, 2):
                    opsA = chain_ops(gb*8 + gg2, 0)
                    opsB = chain_ops(gb*8 + gg2 + 1, 1)
                    la, lb = len(opsA), len(opsB)
                    ia = ib = 0
                    while ia < la or ib < lb:
                        if ia < la:
                            e, fn, r, w, tg = opsA[ia]; ia += 1
                            if e == "dma":
                                P.dma(fn, r, w, tag=tg)
                            else:
                                P.emit(e, fn, r, w, tag=tg)
                        if ib < lb:
                            e, fn, r, w, tg = opsB[ib]; ib += 1
                            if e == "dma":
                                P.dma(fn, r, w, tag=tg)
                            else:
                                P.emit(e, fn, r, w, tag=tg)
                avbank = [pmm[0][:], pmm[1][:], ptr[0][:].bitcast(F32),
                          ptr[1][:].bitcast(F32), pmx[0][:], pmx[1][:],
                          pav[0][:], pav[1][:]]
                avkey = ["pmm0", "pmm1", "ptr0", "ptr1", "pmx0", "pmx1",
                         "pav0", "pav1"]
                for jt in range(J):
                    vr = jt % 4
                    P.dma(lambda sp, jt=jt, vr=vr: sp.dma_start(
                        out=vt[vr][:], in_=v_d[jt*128:(jt+1)*128, :]),
                        (f"vt{vr}",), (f"vt{vr}",), tag="ld_vt")
                    for kp in range(8):
                        P.emit("pe", lambda t, jt=jt, kp=kp, vr=vr, J=J: t.matmul(
                            avbank[kp][:, 0:128],
                            vt[vr][:, kp*128:(kp+1)*128],
                            utb[:, jt*1024 + kp*128: jt*1024 + (kp+1)*128],
                            start=(jt == 0), stop=(jt == J - 1)),
                            (f"vt{vr}", "utb"), (avkey[kp],), tag="av")
                for k in range(16):
                    kp = k // 2
                    ro = (k % 2) * 64
                    dstv = ofm[:, (k//2)*512 + gb*64:(k//2)*512 + (gb+1)*64]\
                        .rearrange("p (g e) -> p g e", e=8)
                    srcv = avbank[kp][:, 0:128]\
                        .rearrange("p (g q) -> p g q", q=16)
                    P.emit("act", lambda s, dstv=dstv, srcv=srcv, ro=ro, k=k: s.copy(
                        dstv[ro:ro+64, :, :],
                        srcv[ro:ro+64, :, (k % 2)*8:(k % 2)*8+8]),
                        (avkey[kp],), ("ofm",), tag="av_ev")
            for oc in range(8):
                po = oc % 2
                P.dma(lambda sp, oc=oc, po=po: sp.dma_start(
                    out=wt[po][:].rearrange("p (c e) -> p c e", c=8),
                    in_=wob_d[:, oc*128:(oc+1)*128].rearrange("(c p) e -> p c e", p=128)),
                    (f"wt{po}",), (f"wt{po}",), tag="ld_wt")
                P.dma(lambda sp, oc=oc, po=po, m=m: sp.dma_start(
                    out=xot[po][:], in_=xo_d[oc*128:(oc+1)*128, m*512:(m+1)*512]),
                    (f"xot{po}",), (f"xot{po}",), tag="ld_xot")
                for c in range(8):
                    P.emit("pe", lambda t, c=c, po=po: t.matmul(
                        pmm[po][:], wt[po][:, c*128:(c+1)*128],
                        ofm[:, c*512:(c+1)*512], start=(c == 0), stop=(c == 7)),
                        (f"wt{po}", "ofm"), (f"pmm{po}",), tag="oproj")
                P.emit("dve", lambda v, po=po, oc=oc: v.tensor_scalar(
                    out=ystg[po][:], in0=pmm[po][:], scalar1=bosb[:, oc:oc+1],
                    scalar2=None, op0=ALU.add), (f"pmm{po}", "bosb"),
                    (f"ystg{po}",), tag="ybias")
                P.emit("dve", lambda v, po=po: v.tensor_tensor(
                    out=ystg[po][:], in0=ystg[po][:], in1=xot[po][:], op=ALU.add),
                    (f"ystg{po}", f"xot{po}"), (f"ystg{po}",), tag="yres")
                P.dma(lambda sp, oc=oc, po=po, m=m: sp.dma_start(
                    out=y_d[oc*128:(oc+1)*128, m*512:(m+1)*512], in_=ystg[po][:]),
                    (f"ystg{po}",), (f"ystg{po}",), tag="st_y")
        P.finalize()
    return nc

import ml_dtypes as _mld

_PR = {}


def _progs():
    if not _PR:
        _PR["l0p"] = build_l0_proj()
        _PR["l0a"] = build_l0_attn()
        _PR["l0w"] = build_l0ffn_w()
        _PR["l0f"] = build_l0ffn_main()
        _PR["l1p"] = build_l1attn_proj()
        _PR["l1m"] = build_l1attn_mid()
        _PR["l1o1"] = build_l1attn_out(True)
        _PR["l1o2"] = build_l1attn_out(False)
        _PR["l1fA"] = build_l1ffn_layer(True, False)
        _PR["l1fB"] = build_l1ffn_layer(False, True)
        _PR["l2p"] = build_l2_proj()
        _PR["l2a"] = build_l2_attn()
        _PR["l3a"] = build_l3_attn()
        _PR["l2f"] = build_l2_ffn()
        _PR["l3f"] = build_l3_ffn()
    return _PR


def _own_cols(s):
    idx = []
    for m in range(4):
        idx.extend(range(1024*m + 512*s, 1024*m + 512*s + 512))
    return np.array(idx)


def _run8(nc, in_maps):
    return run_bass_kernel_spmd(nc, in_maps, core_ids=list(range(8))).results


def kernel(**z):
    PR = _progs()
    f32 = np.float32
    x0 = np.asarray(z["L0_attn_x"], f32)
    N = 4096

    mem_k = np.asarray(z["L0_attn_mem_k"], f32)
    mem_v = np.asarray(z["L0_attn_mem_v"], f32)
    memk = np.ascontiguousarray(mem_k.transpose(0, 2, 1).reshape(1024, 16))
    memv = np.ascontiguousarray(mem_v.transpose(1, 0, 2).reshape(16, 1024))
    in_maps = []
    for c in range(8):
        b = c // 2
        in_maps.append({
            "x": np.ascontiguousarray(x0[b].T).astype(f32),
            "wq": np.ascontiguousarray(np.asarray(z["L0_attn_Wq"], f32)),
            "wkv": np.ascontiguousarray(np.asarray(z["L0_attn_Wkv"], f32)),
            "memk": memk, "memv": memv,
        })
    resA = _run8(PR["l0p"], in_maps)

    prep = np.asarray(z["L0_attn_pre_proj"], f32)
    kron = np.zeros((128, 128), f32)
    for h in range(16):
        for k2 in range(16):
            for i in range(8):
                kron[h*8+i, k2*8+i] = prep[h, k2]
    irow = np.tile(np.arange(1280, dtype=f32), (128, 1))
    p8 = (np.arange(128) % 8).astype(f32)
    bout = np.asarray(z["L0_attn_bout"], f32)
    in_maps = []
    for c in range(8):
        b, s = c // 2, c % 2
        oc = _own_cols(s)
        c1 = np.zeros((128, 256), f32)
        for m in range(4):
            for g in range(64):
                c1[:, m*64+g] = 1024*m + 512*s + 8*g + 16 + p8
        in_maps.append({
            "q": np.ascontiguousarray(resA[c]["q"][:, oc]),
            "k": np.ascontiguousarray(resA[c]["k"]),
            "v": np.ascontiguousarray(resA[c]["v"]),
            "kron": kron, "c1tab": c1, "irow": irow,
            "wout": np.ascontiguousarray(np.asarray(z["L0_attn_Wout"], f32)),
            "bout": np.ascontiguousarray(bout.reshape(8, 128).T),
            "xo": np.ascontiguousarray(x0[b].T[:, oc]).astype(f32),
        })
    resB = _run8(PR["l0a"], in_maps)
    x1 = np.zeros_like(x0)
    for c in range(8):
        b, s = c // 2, c % 2
        x1[b, _own_cols(s)] = resB[c]["y"].T

    A1 = np.asarray(z["L0_ffn_fc1_A"], f32); B1 = np.asarray(z["L0_ffn_fc1_B"], f32)
    A2 = np.asarray(z["L0_ffn_fc2_A"], f32); B2 = np.asarray(z["L0_ffn_fc2_B"], f32)
    lg1 = np.asarray(z["L0_ffn_fc1_logits"], f32)
    lg2 = np.asarray(z["L0_ffn_fc2_logits"], f32)
    A1h = np.ascontiguousarray(A1.transpose(0, 2, 1).reshape(16384, 1024))
    B1h = B1.reshape(16384, 4096)
    A2h = A2.transpose(0, 2, 1).reshape(16384, 4096)
    B2h = np.ascontiguousarray(B2.reshape(16384, 1024))
    in_maps = []
    for c in range(8):
        in_maps.append({
            "a1": A1h,
            "b1": np.ascontiguousarray(B1h[:, c*512:(c+1)*512]),
            "a2": np.ascontiguousarray(A2h[:, c*512:(c+1)*512]),
            "b2": B2h,
            "lg1": lg1.reshape(1, 256), "lg2": lg2.reshape(1, 256),
        })
    resW = _run8(PR["l0w"], in_maps)
    w1b = np.concatenate([resW[c]["w1s"] for c in range(8)], axis=1)
    w2b = np.concatenate([resW[c]["w2s"] for c in range(8)], axis=0)
    in_maps = []
    for c in range(8):
        b, s = c // 2, c % 2
        oc = _own_cols(s)
        in_maps.append({
            "x": np.ascontiguousarray(x1[b].T[:, oc]).astype(f32),
            "w1": np.ascontiguousarray(w1b), "w2": np.ascontiguousarray(w2b),
        })
    resF = _run8(PR["l0f"], in_maps)
    x2 = np.zeros_like(x0)
    for c in range(8):
        b, s = c // 2, c % 2
        x2[b, _own_cols(s)] = resF[c]["y"].T

    W_in = np.asarray(z["L1_attn_W_in"], f32)
    W_out = np.asarray(z["L1_attn_W_out"], f32)
    fw1 = np.asarray(z["L1_attn_f_w1"], f32); fb1 = np.asarray(z["L1_attn_f_b1"], f32)
    fw2 = np.asarray(z["L1_attn_f_w2"], f32); fb2 = np.asarray(z["L1_attn_f_b2"], f32)
    j_ = np.arange(64); m_ = np.arange(128); n_ = np.arange(64); g_ = np.arange(64)
    F1 = np.zeros((64, 256), f32)
    ang = 2*np.pi*np.outer(j_, m_)/128.0
    F1[:, 0:128] = np.cos(ang); F1[:, 128:256] = -np.sin(ang)
    A2c = np.zeros((128, 2, 64, 128), f32)
    for mm in range(128):
        th = 2*np.pi*np.outer(g_, (mm/8192.0 + n_/64.0))
        A2c[mm, 0, :, 0:64] = np.cos(th);  A2c[mm, 0, :, 64:128] = -np.sin(th)
        A2c[mm, 1, :, 0:64] = np.sin(th);  A2c[mm, 1, :, 64:128] = np.cos(th)
    E2 = np.zeros((128, 128), f32)
    ph = 2*np.pi*np.outer(n_, g_)/64.0
    E2[0:64, 0:64] = np.cos(ph); E2[0:64, 64:128] = np.sin(ph)
    E2[64:128, 0:64] = -np.sin(ph); E2[64:128, 64:128] = np.cos(ph)
    F3 = np.zeros((64, 2, 128, 64), f32)
    for gg in range(64):
        ps = 2*np.pi*np.outer(m_, (gg/8192.0 + j_/128.0))
        F3[gg, 0] = np.cos(ps)/8192.0
        F3[gg, 1] = -np.sin(ps)/8192.0
    wvt = np.ascontiguousarray(W_in[0:1024].T)
    w1t = np.ascontiguousarray(W_in[1024:2048].T)
    w2t = np.ascontiguousarray(W_in[2048:3072].T)
    woutT = np.ascontiguousarray(W_out.T)
    uu = np.zeros(4096, np.int64)
    for g in range(64):
        for jj in range(64):
            uu[g*64 + jj] = g + 64*jj
    in1 = []
    for c in range(8):
        b = c // 2
        fm = np.ascontiguousarray(x2[b].T).astype(f32)
        in1.append({"x": fm, "wvt": wvt, "w1t": w1t, "w2t": w2t, "f1": F1,
                    "fw1": fw1[0].reshape(1, 64), "fb1": fb1[0].reshape(1, 64),
                    "fw2t": np.ascontiguousarray(fw2[0].T),
                    "fb2": fb2[0].reshape(1, 1024)})
    res1 = _run8(PR["l1p"], in1)

    def permC(cv):
        return np.ascontiguousarray(cv.transpose(2, 1, 0, 3))
    in2 = [{"cvp": permC(res1[c]["cv"]), "chp": permC(res1[c]["ch"]),
            "a2": A2c, "e2": E2} for c in range(8)]
    res2 = _run8(PR["l1m"], in2)
    in3 = [{"yvp": np.ascontiguousarray(res2[c]["yv"].transpose(1, 0, 2)),
            "f3": F3, "xg": res1[c]["x1g"], "f1": F1,
            "fw1": fw1[1].reshape(1, 64), "fb1": fb1[1].reshape(1, 64),
            "fw2t": np.ascontiguousarray(fw2[1].T), "fb2": fb2[1].reshape(1, 1024)}
           for c in range(8)]
    res3 = _run8(PR["l1o1"], in3)
    in4 = [{"cvp": permC(res3[c]["cv"]), "chp": permC(res3[c]["ch"]),
            "a2": A2c, "e2": E2} for c in range(8)]
    res4 = _run8(PR["l1m"], in4)
    in5 = []
    for c in range(8):
        b = c // 2
        fm = np.ascontiguousarray(x2[b].T[:, uu]).astype(f32)
        in5.append({"yvp": np.ascontiguousarray(res4[c]["yv"].transpose(1, 0, 2)),
                    "f3": F3, "xg": res1[c]["x2g"], "wout_t": woutT, "xperm": fm})
    res5 = _run8(PR["l1o2"], in5)
    x3 = np.zeros_like(x0)
    for c in range(8):
        b = c // 2
        if c % 2 == 0:
            x3[b][uu] = res5[c]["y"].T

    wqkv1 = np.asarray(z["L1_ffn_wqkv"], f32); qs1 = np.asarray(z["L1_ffn_q_scale"], f32)
    ks1 = np.asarray(z["L1_ffn_k_scale"], f32); wo1 = np.asarray(z["L1_ffn_wo"], f32)
    lng1 = np.asarray(z["L1_ffn_ln_g"], f32); lnb1 = np.asarray(z["L1_ffn_ln_b"], f32)
    w11 = np.asarray(z["L1_ffn_w1"], f32); b11 = np.asarray(z["L1_ffn_b1"], f32)
    w21 = np.asarray(z["L1_ffn_w2"], f32); b21 = np.asarray(z["L1_ffn_b2"], f32)
    half_idx = np.arange(0, 64, 2, dtype=np.float64)
    invf = (1.0 / 10000.0 ** (half_idx / 64)).astype(f32).reshape(1, 32)
    basef = ((half_idx + 0.4 * 64) / (1.4 * 64)).astype(f32)
    lnbase = np.log(basef.astype(np.float64)).astype(f32).reshape(1, 32)

    def own_windows(s):
        return [2*m + s for m in range(4)]

    def pair_cols(w):
        return np.arange(512*w - 640, 512*w + 512)

    def mk_pairs(full_fm, s):
        out = np.zeros((1024, 4608), f32)
        for m, w in enumerate(own_windows(s)):
            ids = pair_cols(w)
            ok = (ids >= 0) & (ids < N)
            out[:, m*1152 + np.where(ok)[0]] = full_fm[:, ids[ok]]
        return out

    def layer_inputs(l, cur_full, s, last, x_orig):
        fm = np.ascontiguousarray(cur_full.T).astype(f32)
        sh = np.zeros_like(cur_full); sh[1:] = cur_full[:-1]
        shfm = np.ascontiguousarray(sh.T).astype(f32)
        pos = np.zeros((128, 36), f32)
        vm = np.zeros((128, 36), f32)
        for m, w in enumerate(own_windows(s)):
            ids = pair_cols(w)
            for tt in range(9):
                pos[:, m*9+tt] = ids[tt*128:(tt+1)*128]
                vm[:, m*9+tt] = ((ids[tt*128:(tt+1)*128] >= 0) &
                                 (ids[tt*128:(tt+1)*128] < N)).astype(f32)
        w1l = w11[l]; b1l = b11[l]
        w1p = np.zeros((1024, 5632), f32)
        b1p = np.zeros((128, 44), f32)
        for fc in range(22):
            a0, a1 = fc*128, min((fc+1)*128, 2730)
            if a0 < 2730:
                w1p[:, fc*256: fc*256 + (a1-a0)] = w1l[:, a0:a1]
                w1p[:, fc*256+128: fc*256+128 + (a1-a0)] = w1l[:, 2730+a0: 2730+a1]
                b1p[0:(a1-a0), fc*2] = b1l[a0:a1]
                b1p[0:(a1-a0), fc*2+1] = b1l[2730+a0: 2730+a1]
        w2p = np.zeros((2816, 1024), f32)
        w2p[:2730] = w21[l]
        own = np.concatenate([pair_cols(w)[640:] for w in own_windows(s)])
        return {
            "xp": mk_pairs(fm, s), "xm1": mk_pairs(shfm, s),
            "pos": pos, "vmask": vm,
            "wq": np.ascontiguousarray(wqkv1[l][:, :512]),
            "wk": np.ascontiguousarray(wqkv1[l][:, 512:1024]),
            "wv": np.ascontiguousarray(wqkv1[l][:, 1024:]),
            "wo": np.ascontiguousarray(wo1[l]),
            "qsrow": qs1[l].reshape(1, 64), "ksrow": ks1[l].reshape(1, 64),
            "invf": invf, "lnb": lnbase,
            "w1p": w1p, "b1p": b1p, "w2p": w2p,
            "b2": np.ascontiguousarray(b21[l].reshape(8, 128).T),
            "lg": np.ascontiguousarray(lng1[l].reshape(8, 128).T),
            "lbeta": np.ascontiguousarray(lnb1[l].reshape(8, 128).T),
            "xo": (np.ascontiguousarray(x_orig.T[:, own]).astype(f32)
                   if last else np.zeros((1024, 2048), f32)),
        }
    in_maps = []
    for c in range(8):
        b, s = c // 2, c % 2
        in_maps.append(layer_inputs(0, x3[b], s, False, x3[b]))
    resA1 = _run8(PR["l1fA"], in_maps)
    xc1 = np.zeros((4, 4096, 1024), f32)
    for c in range(8):
        b, s = c // 2, c % 2
        y = resA1[c]["y"]
        for m, w in enumerate(own_windows(s)):
            xc1[b, 512*w: 512*w+512] = y[:, m*512:(m+1)*512].T
    in_maps = []
    for c in range(8):
        b, s = c // 2, c % 2
        in_maps.append(layer_inputs(1, xc1[b], s, True, x3[b]))
    resB1 = _run8(PR["l1fB"], in_maps)
    x4 = np.zeros_like(x0)
    for c in range(8):
        b, s = c // 2, c % 2
        y = resB1[c]["y"]
        for m, w in enumerate(own_windows(s)):
            x4[b, 512*w: 512*w+512] = y[:, m*512:(m+1)*512].T

    Wqkv2 = np.asarray(z["L2_attn_Wqkv"], f32); bqkv2 = np.asarray(z["L2_attn_bqkv"], f32)
    Wout2 = np.asarray(z["L2_attn_Wout"], f32); bout2 = np.asarray(z["L2_attn_bout"], f32)
    in_maps = []
    for c in range(8):
        b = c // 2
        in_maps.append({
            "x": np.ascontiguousarray(x4[b].T).astype(f32),
            "wqkv": np.ascontiguousarray(Wqkv2),
            "bqkv": np.ascontiguousarray(bqkv2.reshape(24, 128).T),
            "vbrow": np.ascontiguousarray(bqkv2[2048:].reshape(1, 1024)),
        })
    resP = _run8(PR["l2p"], in_maps)
    in_maps = []
    for c in range(8):
        b, s = c // 2, c % 2
        oc = _own_cols(s)
        in_maps.append({
            "q": np.ascontiguousarray(resP[c]["q"][:, oc]),
            "k": np.ascontiguousarray(resP[c]["k"]),
            "v": np.ascontiguousarray(resP[c]["v"]),
            "wout": np.ascontiguousarray(Wout2),
            "bout": np.ascontiguousarray(bout2.reshape(8, 128).T),
            "xo": np.ascontiguousarray(x4[b].T[:, oc]).astype(f32),
            "astrip": np.full((128, 1), 3968.0 - 512.0*s, f32),
        })
    resQ = _run8(PR["l2a"], in_maps)
    x5 = np.zeros_like(x0)
    for c in range(8):
        b, s = c // 2, c % 2
        x5[b, _own_cols(s)] = resQ[c]["y"].T

    gw = np.asarray(z["L2_ffn_gate_w"], f32); be = np.asarray(z["L2_ffn_bias_e"], f32)
    up = np.asarray(z["L2_ffn_up_w"], f32); dn = np.asarray(z["L2_ffn_down_w"], f32)
    su = np.asarray(z["L2_ffn_shared_up"], f32); sd = np.asarray(z["L2_ffn_shared_down"], f32)
    upt = np.concatenate([up.transpose(2, 0, 1).reshape(1024, 16*512),
                          np.ascontiguousarray(su.T)], axis=1)
    dnt = np.concatenate([dn.transpose(0, 2, 1).reshape(16*512, 1024),
                          np.ascontiguousarray(sd.T)], axis=0)
    in_maps = []
    for c in range(8):
        b, s = c // 2, c % 2
        oc = _own_cols(s)
        in_maps.append({
            "x": np.ascontiguousarray(x5[b].T[:, oc]).astype(f32),
            "gwt": np.ascontiguousarray(gw.T),
            "bias_e": be.reshape(1, 16),
            "upt": np.ascontiguousarray(upt),
            "dnt": np.ascontiguousarray(dnt),
        })
    resM = _run8(PR["l2f"], in_maps)
    x6 = np.zeros_like(x0)
    for c in range(8):
        b, s = c // 2, c % 2
        x6[b, _own_cols(s)] = resM[c]["y"].T

    wq3 = np.asarray(z["L3_attn_w_q"], f32); wk3 = np.asarray(z["L3_attn_w_k"], f32)
    wv3 = np.asarray(z["L3_attn_w_v"], f32); wo3 = np.asarray(z["L3_attn_w_o"], f32)
    wqh = np.ascontiguousarray(wq3.transpose(1, 0, 2).reshape(1024, 1024))
    wkh = np.ascontiguousarray(wk3.transpose(1, 0, 2).reshape(1024, 1024))
    wvh = np.ascontiguousarray(wv3.transpose(1, 0, 2).reshape(1024, 1024))
    wqkv3 = np.concatenate([wqh, wkh, wvh], axis=1)
    in_maps = []
    for c in range(8):
        b = c // 2
        in_maps.append({
            "x": np.ascontiguousarray(x6[b].T).astype(f32),
            "wqkv": np.ascontiguousarray(wqkv3),
            "bqkv": np.zeros((128, 24), f32),
            "vbrow": np.zeros((1, 1024), f32),
        })
    resP3 = _run8(PR["l2p"], in_maps)
    in_maps = []
    for c in range(8):
        b, s = c // 2, c % 2
        oc = _own_cols(s)
        in_maps.append({
            "q": np.ascontiguousarray(resP3[c]["q"][:, oc]),
            "k": np.ascontiguousarray(resP3[c]["k"]),
            "v": np.ascontiguousarray(resP3[c]["v"]),
            "wout": np.ascontiguousarray(wo3),
            "xo": np.ascontiguousarray(x6[b].T[:, oc]).astype(f32),
            "astrip": np.full((128, 1), 3968.0 - 512.0*s, f32),
        })
    resQ3 = _run8(PR["l3a"], in_maps)
    x7 = np.zeros_like(x0)
    for c in range(8):
        b, s = c // 2, c % 2
        x7[b, _own_cols(s)] = resQ3[c]["y"].T

    w3 = np.asarray(z["L3_ffn_weight"], f32); bias3 = np.asarray(z["L3_ffn_bias"], f32)
    in_maps = []
    for c in range(8):
        b, hh = c // 2, c % 2
        in_maps.append({
            "x": np.ascontiguousarray(x7[b].T[:, hh*2048:(hh+1)*2048]).astype(f32),
            "w": np.ascontiguousarray(w3),
            "bias": np.ascontiguousarray(bias3.reshape(8, 128).T),
        })
    resE = _run8(PR["l3f"], in_maps)
    x8 = np.zeros_like(x0)
    for c in range(8):
        b, hh = c // 2, c % 2
        x8[b, hh*2048:(hh+1)*2048] = resE[c]["y"].T
    return x8
